# version 06
# speedup vs baseline: 1.0225x; 1.0006x over previous
from contextlib import ExitStack

import numpy as np
import concourse.bass as bass
import concourse.mybir as mybir
from concourse.bass_utils import run_bass_kernel_spmd

F32 = mybir.dt.float32
BF16 = mybir.dt.bfloat16
AF = mybir.ActivationFunctionType
ALU = mybir.AluOpType
AX = mybir.AxisListType.X

H = 2048
S = 8192
N_CORES = 8
SCALE_A = float(1.0 / np.sqrt(128.0))
EPS = 1e-5
D_INNER = 4096
D_STATE = 128
HEADDIM = 64
NHEADS_M = 64
D_CONV = 4
CHUNK = 256
CONV_DIM = D_INNER + 2 * D_STATE
D_IN_PROJ = 2 * D_INNER + 2 * D_STATE + NHEADS_M


class Ch:

    def __init__(self, ctx, nc, name):
        self.sem = ctx.enter_context(nc.semaphore(name))
        self.n = 0

    def inc(self, inst, k=1):
        inst.then_inc(self.sem, k)
        self.n += k
        return inst


def emit_ln_tile(blk, C, xt_ap, xn_ap, sqscr, stats, eps_ap, dxch):
    dxc = C[dxch]
    dxn = dxc.n
    nseg = xt_ap.shape[-1] // 512

    @blk.vector
    def _(v):
        v.wait_ge(dxc.sem, dxn)
        for i in range(nseg):
            v.bn_stats(stats[:, 6 * i:6 * (i + 1)],
                       xt_ap[:, 512 * i:512 * (i + 1)])
        v.drain(fusable=False)
        C["nv"].inc(v.bn_aggr(stats[:, 24:26], stats[:, 0:6 * nseg]))

    nvn = C["nv"].n

    @blk.scalar
    def _(s):
        s.wait_ge(C["nv"].sem, nvn)
        s.activation(stats[:, 27:28], stats[:, 25:26], AF.Ln, bias=eps_ap)
        s.drain(fusable=False)
        C["rs"].inc(s.activation(stats[:, 26:27], stats[:, 27:28], AF.Exp,
                                 scale=-0.5))

    rsn = C["rs"].n

    @blk.vector
    def _(v):
        v.wait_ge(C["rs"].sem, rsn)
        C["xn"].inc(v.tensor_scalar(xn_ap, xt_ap, stats[:, 24:25],
                                    stats[:, 26:27], ALU.subtract, ALU.mult))


def build_l1(seq=S):
    nc = bass.Bass("TRN2", target_bir_lowering=False)
    NT = seq // 128

    xin = nc.dram_tensor("xin", [seq, H], F32, kind="ExternalInput").ap()
    wqkvT = nc.dram_tensor("wqkvT", [H, 768], F32, kind="ExternalInput").ap()
    maskc = nc.dram_tensor("maskc", [4, 128, 512], F32, kind="ExternalInput").ap()
    aout = nc.dram_tensor("attn_out", [seq, 256], F32, kind="ExternalOutput").ap()

    with ExitStack() as ctx:
        sb = lambda n, s, d: ctx.enter_context(nc.sbuf_tensor(n, s, d))

        cst = sb("cst", [128, 2], F32)
        ident = sb("ident", [128, 128], BF16)
        qT = sb("qT", [128, 2, seq], BF16)
        kT = sb("kT", [128, 2, seq], BF16)
        vsb = sb("vsb", [128, NT, 2, 128], BF16)
        mask = sb("mask", [128, 4, 512], F32)

        names = ["dx0", "dx1", "dwg", "dwm", "dwe", "sq", "nv", "rs", "xn", "tr", "xnT", "qkv",
                 "qke", "ve", "vtr", "vfin", "qkm", "sc", "scv", "mx",
                 "exp", "rcp", "ptr", "pte", "pvg", "pv", "ao", "st"]
        C = {n: Ch(ctx, nc, "c_" + n) for n in names}

        blk = ctx.enter_context(nc.Block())

        @blk.gpsimd
        def _(g):
            C["dwe"].inc(g.memset(ident[:, :], 0.0))
            C["dwe"].inc(g.memset(cst[:, 0:1], EPS))
            g.wait_ge(C["dwe"].sem, C["dwe"].n)
            C["dwe"].inc(g.affine_select(
                out=ident[:, :], in_=ident[:, :],
                compare_op=ALU.not_equal, fill=1.0, base=0,
                pattern=[[-1, 128]], channel_multiplier=1))

        @blk.sync
        def _(sy):
            C["dwm"].inc(sy.dma_start(out=mask[:, :, :],
                                      in_=maskc.rearrange("m p f -> p m f")), 16)

        def wait_preamble(e):
            e.wait_ge(C["dwg"].sem, C["dwg"].n)
            e.wait_ge(C["dwm"].sem, C["dwm"].n)
            e.wait_ge(C["dwe"].sem, C["dwe"].n)

        @blk.scalar
        def _(s):
            wait_preamble(s)

        @blk.vector
        def _(v):
            wait_preamble(v)

        with ExitStack() as actx:
            sba = lambda n, s, d: actx.enter_context(nc.sbuf_tensor(n, s, d))
            psa = lambda n, s, d: actx.enter_context(nc.psum_tensor(n, s, d))
            wq = sba("wq", [128, 16, 768], BF16)
            xt = sba("xt", [128, 2, H], F32)

            @blk.gpsimd
            def _(g):
                C["dwg"].inc(g.dma_start(
                    out=wq[:, :, :],
                    in_=wqkvT.rearrange("(c p) f -> p c f", p=128)), 16)

            sqscr = sba("sqscr", [128, H], BF16)
            xn = sba("xn", [128, 2, H], BF16)
            xnT = sba("xnT", [128, 2, H], BF16)
            stats = sba("stats", [128, 32], F32)
            vt = sba("vt", [128, 256], BF16)
            trp = psa("trp", [128, 2048], BF16)
            qkp = psa("qkp", [128, 512], F32)
            vp = psa("vp", [128, 256], F32)
            vtp = psa("vtp", [128, 256], BF16)

            for tt in range(NT):
                @blk.sync
                def _(sy, tt=tt):
                    if tt >= 2:
                        sy.wait_ge(C["xn"].sem, tt - 1)
                    C["dx%d" % (tt % 2)].inc(sy.dma_start(
                        out=xt[:, tt % 2, :],
                        in_=xin[tt * 128:(tt + 1) * 128, :]), 16)

                if tt >= 2:
                    @blk.vector
                    def _(v, tt=tt):
                        v.wait_ge(C["tr"].sem, tt - 1)

                emit_ln_tile(blk, C, xt[:, tt % 2, :], xn[:, tt % 2, :],
                             sqscr, stats, cst[:, 0:1], "dx%d" % (tt % 2))

                @blk.tensor
                def _(t, tt=tt):
                    t.wait_ge(C["xn"].sem, C["xn"].n)
                    if tt == 0:
                        wait_preamble(t)
                    if tt >= 2:
                        t.wait_ge(C["xnT"].sem, tt - 1)
                    for chk in range(16):
                        inst = t.transpose(
                            trp[:, chk * 128:(chk + 1) * 128],
                            xn[:, tt % 2, chk * 128:(chk + 1) * 128],
                            ident[:, :])
                    C["tr"].inc(inst)

                @blk.vector
                def _(v, tt=tt):
                    v.wait_ge(C["tr"].sem, C["tr"].n)
                    v.tensor_copy(xnT[:, tt % 2, 0:1024], trp[:, 0:1024])
                    C["xnT"].inc(v.tensor_copy(xnT[:, tt % 2, 1024:2048],
                                               trp[:, 1024:2048]))

                @blk.tensor
                def _(t, tt=tt):
                    t.wait_ge(C["xnT"].sem, C["xnT"].n)
                    if tt >= 1:
                        t.wait_ge(C["qke"].sem, 2 * tt)
                        t.wait_ge(C["ve"].sem, tt)
                    for ft in range(6):
                        dst = (qkp[:, ft * 128:(ft + 1) * 128] if ft < 4
                               else vp[:, (ft - 4) * 128:(ft - 3) * 128])
                        for chk in range(16):
                            inst = t.matmul(
                                dst, wq[:, chk, ft * 128:(ft + 1) * 128],
                                xnT[:, tt % 2, chk * 128:(chk + 1) * 128],
                                start=(chk == 0), stop=(chk == 15))
                    C["qkv"].inc(inst)

                @blk.scalar
                def _(s, tt=tt):
                    s.wait_ge(C["qkv"].sem, C["qkv"].n)
                    for hh in range(2):
                        s.mul(qT[:, hh, tt * 128:(tt + 1) * 128],
                              qkp[:, hh * 128:(hh + 1) * 128], SCALE_A)
                        C["qke"].inc(s.copy(
                            kT[:, hh, tt * 128:(tt + 1) * 128],
                            qkp[:, (2 + hh) * 128:(3 + hh) * 128]))

                @blk.vector
                def _(v, tt=tt):
                    v.wait_ge(C["qkv"].sem, C["qkv"].n)
                    if tt >= 1:
                        v.wait_ge(C["vtr"].sem, tt)
                    C["ve"].inc(v.tensor_copy(vt[:, :], vp[:, 0:256]))

                @blk.tensor
                def _(t, tt=tt):
                    t.wait_ge(C["ve"].sem, C["ve"].n)
                    if tt >= 1:
                        t.wait_ge(C["vfin"].sem, 2 * tt)
                    t.transpose(vtp[:, 0:128], vt[:, 0:128], ident[:, :])
                    C["vtr"].inc(t.transpose(vtp[:, 128:256], vt[:, 128:256],
                                             ident[:, :]))

                @blk.vector
                def _(v, tt=tt):
                    v.wait_ge(C["vtr"].sem, C["vtr"].n)
                    v.tensor_copy(vsb[:, tt, 0, :], vtp[:, 0:128])
                    C["vfin"].inc(v.tensor_copy(vsb[:, tt, 1, :],
                                                vtp[:, 128:256]), 2)

        kdone = C["qke"].n
        vdone = C["vfin"].n

        with ExitStack() as bctx:
            sbb = lambda n, s, d: bctx.enter_context(nc.sbuf_tensor(n, s, d))
            psb = lambda n, s, d: bctx.enter_context(nc.psum_tensor(n, s, d))
            scores = sbb("scores", [128, 2, seq], F32)
            pbuf = sbb("pbuf", [128, 2, seq], BF16)
            negM = sbb("negM", [128, 2], F32)
            rsum = sbb("rsum", [128, 2], F32)
            recip = sbb("recip", [128, 2], F32)
            pt = sbb("pt", [128, 2, 2, 512], BF16)
            ao = sbb("ao", [128, 256], F32)
            scp = psb("scp", [128, 2, 2, 512], F32)
            ptp = psb("ptp", [128, 2, 1024], BF16)
            op = psb("op", [128, 2, 512], F32)

            NROWS = 2 * NT
            exp_after = [0] * NROWS
            ptr_after = [0] * NROWS
            rcp_after = [0] * NROWS
            sc_after = [0] * NROWS
            scv_after = [0] * NROWS
            pvg_after = [0] * NROWS
            rowinfo = {}

            def emit_qk(ri):
                qt, hh = ri // 2, ri % 2
                P = ri % 2
                nu = qt // 4 + 1
                base_sc = C["sc"].n
                base_mm = C["qkm"].n
                g_exp = exp_after[ri - 2] if ri >= 2 else 0
                g_sc = sc_after[ri - 2] if ri >= 2 else 0
                g_scv = scv_after[ri - 2] if ri >= 2 else 0
                g_ptr = ptr_after[ri - 2] if ri >= 2 else 0
                g_rcp = rcp_after[ri - 2] if ri >= 2 else 0

                @blk.tensor
                def _(t):
                    if ri == 0:
                        t.wait_ge(C["qke"].sem, kdone)
                    t.wait_ge(C["sc"].sem, g_sc)
                    t.wait_ge(C["scv"].sem, g_scv)
                    for u in range(nu):
                        if u >= 2:
                            t.wait_ge(C["sc"].sem, base_sc + u - 1)
                        C["qkm"].inc(t.matmul(
                            scp[:, P, u % 2, :],
                            qT[:, hh, qt * 128:(qt + 1) * 128],
                            kT[:, hh, u * 512:(u + 1) * 512],
                            start=True, stop=True))

                @blk.scalar
                def _(s):
                    s.wait_ge(C["exp"].sem, g_exp)
                    for u in range(nu - 1):
                        s.wait_ge(C["qkm"].sem, base_mm + u + 1)
                        C["sc"].inc(s.copy(
                            scores[:, P, u * 512:(u + 1) * 512],
                            scp[:, P, u % 2, :]))

                @blk.vector
                def _(v):
                    v.wait_ge(C["exp"].sem, g_exp)
                    v.wait_ge(C["qkm"].sem, base_mm + nu)
                    u = nu - 1
                    C["scv"].inc(v.tensor_tensor(
                        scores[:, P, u * 512:(u + 1) * 512],
                        scp[:, P, u % 2, :], mask[:, qt % 4, :], ALU.add))
                    v.wait_ge(C["sc"].sem, base_sc + nu - 1)
                    v.drain(fusable=False)
                    C["mx"].inc(v.reduce_max(negM[:, P:P + 1],
                                             scores[:, P, 0:nu * 512],
                                             axis=AX, negate=True))

                mx_n = C["mx"].n

                @blk.scalar
                def _(s):
                    s.wait_ge(C["mx"].sem, mx_n)
                    s.wait_ge(C["ptr"].sem, g_ptr)
                    s.wait_ge(C["rcp"].sem, g_rcp)
                    C["exp"].inc(s.activation(pbuf[:, P, 0:nu * 512],
                                              scores[:, P, 0:nu * 512],
                                              AF.Exp, bias=negM[:, P:P + 1],
                                              accum_out=rsum[:, P:P + 1]))
                exp_after[ri] = C["exp"].n

                @blk.vector
                def _(v):
                    v.wait_ge(C["exp"].sem, C["exp"].n)
                    v.reciprocal(recip[:, P:P + 1], rsum[:, P:P + 1])
                    C["rcp"].inc(v.drain(fusable=False))
                rcp_after[ri] = C["rcp"].n
                sc_after[ri] = C["sc"].n
                scv_after[ri] = C["scv"].n
                rowinfo[ri] = dict(exp_now=C["exp"].n)

            def emit_trpv(ri):
                qt, hh = ri // 2, ri % 2
                P = ri % 2
                ktn = qt + 1
                ngrp = (ktn + 3) // 4
                exp_now = rowinfo[ri]["exp_now"]
                base_pte = C["pte"].n
                base_pvg = C["pvg"].n
                base_ptr2 = C["ptr"].n
                ao_before = C["ao"].n
                g_pvg = pvg_after[ri - 2] if ri >= 2 else 0

                @blk.tensor
                def _(t):
                    t.wait_ge(C["exp"].sem, exp_now)
                    if ri == 0:
                        t.wait_ge(C["vfin"].sem, vdone)
                    t.wait_ge(C["ao"].sem, max(0, ao_before - 1))
                    t.wait_ge(C["pte"].sem, base_pte)
                    for g in range(ngrp):
                        k0, k1 = g * 4, min(g * 4 + 4, ktn)
                        if g >= 1:
                            t.wait_ge(C["pte"].sem, base_pte + g)
                        for kt in range(k0, k1):
                            inst = t.transpose(
                                ptp[:, P, (kt - k0) * 128:(kt - k0 + 1) * 128],
                                pbuf[:, P, kt * 128:(kt + 1) * 128],
                                ident[:, :])
                        C["ptr"].inc(inst)
                        t.wait_ge(C["pte"].sem, base_pte + g + 1)
                        for kt in range(k0, k1):
                            inst = t.matmul(
                                op[:, P, 0:128],
                                pt[:, P, g % 2,
                                   (kt - k0) * 128:(kt - k0 + 1) * 128],
                                vsb[:, kt, hh, :],
                                start=(kt == 0), stop=(kt == ktn - 1))
                        C["pvg"].inc(inst)
                ptr_after[ri] = C["ptr"].n

                @blk.vector
                def _(v):
                    v.wait_ge(C["pvg"].sem, g_pvg)
                    for g in range(ngrp):
                        k0, k1 = g * 4, min(g * 4 + 4, ktn)
                        v.wait_ge(C["ptr"].sem, base_ptr2 + g + 1)
                        if g >= 2:
                            v.wait_ge(C["pvg"].sem, base_pvg + g - 1)
                        C["pte"].inc(v.tensor_copy(
                            pt[:, P, g % 2, 0:(k1 - k0) * 128],
                            ptp[:, P, 0:(k1 - k0) * 128]))
                pvg_after[ri] = C["pvg"].n

                @blk.vector
                def _(v):
                    v.wait_ge(C["pvg"].sem, base_pvg + ngrp)
                    if qt >= 1 and hh == 0:
                        v.wait_ge(C["st"].sem, 16 * qt)
                    C["ao"].inc(v.tensor_scalar(
                        ao[:, hh * 128:(hh + 1) * 128], op[:, P, 0:128],
                        recip[:, P:P + 1], None, ALU.mult))

            for ri in range(NROWS + 1):
                if ri < NROWS:
                    emit_qk(ri)
                if ri >= 1:
                    emit_trpv(ri - 1)
                    if (ri - 1) % 2 == 1:
                        qt = (ri - 1) // 2

                        @blk.sync
                        def _(sy, qt=qt):
                            sy.wait_ge(C["ao"].sem, C["ao"].n)
                            C["st"].inc(sy.dma_start(
                                out=aout[qt * 128:(qt + 1) * 128, :],
                                in_=ao[:, :]), 16)

            @blk.sync
            def _(sy):
                sy.wait_ge(C["st"].sem, C["st"].n)

    return nc


def _causal_mask_const():
    m = np.arange(4)[:, None, None] * 128
    i = np.arange(128)[None, :, None]
    j = np.arange(512)[None, None, :]
    return np.where(j <= m + i, 0.0, -1e30).astype(np.float32)


def run_l1(x, w_qkv, seq=S):
    nc = build_l1(seq)
    wT = np.ascontiguousarray(w_qkv.T)
    maskc = _causal_mask_const()
    in_maps = []
    for c in range(N_CORES):
        cols = []
        for part in range(3):
            base = part * H + c * 256
            cols.append(wT[:, base:base + 256])
        wslice = np.ascontiguousarray(np.concatenate(cols, axis=1))
        in_maps.append({"xin": x, "wqkvT": wslice, "maskc": maskc})
    res = run_bass_kernel_spmd(nc, in_maps, core_ids=list(range(N_CORES)))
    return np.concatenate([res.results[c]["attn_out"] for c in range(N_CORES)],
                          axis=1)


N_TILES = 8


def phase_barrier(blk, C, engines=("tensor", "vector", "scalar", "gpsimd", "sync")):
    snap = [(ch.sem, ch.n) for ch in C.values() if ch.n > 0]

    def emit(e):
        for sem, n in snap:
            e.wait_ge(sem, n)

    for name in engines:
        getattr(blk, name)(emit)


def emit_featmajor(blk, C, nc, xn_ap, xnT_ap, trp, ident, key):
    xn_n = C["xn"].n
    pre = C[key + "e"].n

    @blk.tensor
    def _(t):
        t.wait_ge(C["xn"].sem, xn_n)
        if pre >= 1:
            t.wait_ge(C[key + "e"].sem, pre)
        for chk in range(16):
            inst = t.transpose(trp[:, chk * 128:(chk + 1) * 128],
                               xn_ap[:, chk * 128:(chk + 1) * 128], ident)
        C[key + "t"].inc(inst)

    trn = C[key + "t"].n

    @blk.vector
    def _(v):
        v.wait_ge(C[key + "t"].sem, trn)
        v.tensor_copy(xnT_ap[:, 0:1024], trp[:, 0:1024])
        C[key + "e"].inc(v.tensor_copy(xnT_ap[:, 1024:2048], trp[:, 1024:2048]))


def emit_router(blk, C, nc, xnT, wn, rt_sb, rpsum, probs, top8, stats, n_tiles):
    for tt in range(n_tiles):
        pre_ev = C["rev"].n

        @blk.tensor
        def _(t, tt=tt, pre_ev=pre_ev):
            if pre_ev >= 1:
                t.wait_ge(C["rev"].sem, pre_ev)
            ne = rt_sb.shape[2]
            for chk in range(16):
                inst = t.matmul(rpsum[:, 0:ne],
                                xnT[:, tt, chk * 128:(chk + 1) * 128],
                                rt_sb[:, chk, :],
                                start=(chk == 0), stop=(chk == 15))
            C["rmm"].inc(inst)

        rmm_n = C["rmm"].n

        @blk.vector
        def _(v, rmm_n=rmm_n):
            v.wait_ge(C["rmm"].sem, rmm_n)
            v.reduce_max(stats[:, 0:1], rpsum[:, 0:probs.shape[1]], axis=AX, negate=True)
            C["rmx"].inc(v.drain(fusable=False))

        rmx_n = C["rmx"].n

        @blk.scalar
        def _(s, rmx_n=rmx_n):
            s.wait_ge(C["rmx"].sem, rmx_n)
            C["rex"].inc(s.activation(probs[:, :], rpsum[:, 0:probs.shape[1]], AF.Exp,
                                      bias=stats[:, 0:1],
                                      accum_out=stats[:, 1:2]))

        rex_n = C["rex"].n

        @blk.vector
        def _(v, tt=tt, rex_n=rex_n):
            v.wait_ge(C["rex"].sem, rex_n)
            v.max(out=top8[:, :], in_=probs[:, :])
            v.drain(fusable=False)
            v.scalar_tensor_tensor(wn[:, tt, :], probs[:, :], top8[:, 3:4],
                                   probs[:, :], ALU.is_ge, ALU.mult)
            v.drain(fusable=False)
            v.reduce_sum(stats[:, 2:3], wn[:, tt, :], axis=AX)
            v.drain(fusable=False)
            v.reciprocal(stats[:, 3:4], stats[:, 2:3])
            v.drain(fusable=False)
            C["rev"].inc(v.tensor_scalar(wn[:, tt, :], wn[:, tt, :],
                                         stats[:, 3:4], None, ALU.mult))


def emit_ffn(blk, C, nc, x_sb, xnT, wn, slots, gact, hT, htmp, utmp,
             gup, htp, dpsum, ident, gw_ap, uw_ap, dw_ap, n_tiles, n_exp,
             ldch, tag):
    phase_done = {}
    nring = len(slots)

    for k in range(3 * n_exp):
        e, j = divmod(k, 3)
        src = [gw_ap, uw_ap, dw_ap][j]

        @blk.gpsimd
        def _(g, k=k, e=e, src=src):
            if k >= nring:
                ch, cnt = phase_done[k - nring]
                g.wait_ge(C[ch].sem, cnt)
            ap = src[e].rearrange("(c p) f -> p c f", p=128)
            nch = ap.shape[1]
            C[ldch[k % nring]].inc(g.dma_start(
                out=slots[k % nring][:, :].rearrange("p (c f) -> p c f", c=nch),
                in_=ap), 16)

        ld_needed = 16 * (k // nring + 1)

        if j == 0:
            base_ge = C[tag + "ge"].n
            hh_cur = C[tag + "hh"].n

            ue_cur = C[tag + "ue"].n

            @blk.tensor
            def _(t, k=k, ld_needed=ld_needed, base_ge=base_ge, ue_cur=ue_cur):
                t.wait_ge(C[ldch[k % nring]].sem, ld_needed)
                t.wait_ge(C[tag + "ue"].sem, ue_cur)
                sl = slots[k % nring]
                for tt in range(n_tiles):
                    if tt >= 1:
                        t.wait_ge(C[tag + "ge"].sem, base_ge + tt)
                    for (f0, f1) in ((0, 512), (512, 768)):
                        for chk in range(16):
                            inst = t.matmul(
                                gup[:, f0:f1],
                                xnT[:, tt, chk * 128:(chk + 1) * 128],
                                sl[:, chk * 768 + f0:chk * 768 + f1],
                                start=(chk == 0), stop=(chk == 15))
                    C[tag + "gm"].inc(inst)

            base_gm = C[tag + "gm"].n - n_tiles
            for tt in range(n_tiles):
                @blk.scalar
                def _(s, tt=tt, base_gm=base_gm, hh_cur=hh_cur):
                    s.wait_ge(C[tag + "gm"].sem, base_gm + tt + 1)
                    s.wait_ge(C[tag + "hh"].sem, hh_cur)
                    C[tag + "ge"].inc(s.activation(
                        gact[:, tt, :], gup[:, 0:768], AF.Silu))
            phase_done[k] = (tag + "ge", C[tag + "ge"].n)

        elif j == 1:
            base_um = C[tag + "um"].n
            base_ue = C[tag + "ue"].n
            base_hh = C[tag + "hh"].n
            base_ht = C[tag + "ht"].n
            base_he = C[tag + "he"].n
            dm_cur = C[tag + "dm"].n

            ge_cur = C[tag + "ge"].n

            @blk.tensor
            def _(t, k=k, ld_needed=ld_needed, base_ue=base_ue,
                  base_hh=base_hh, base_he=base_he, ge_cur=ge_cur):
                t.wait_ge(C[ldch[k % nring]].sem, ld_needed)
                t.wait_ge(C[tag + "ge"].sem, ge_cur)
                sl = slots[k % nring]
                for tt in range(n_tiles):
                    if tt >= 1:
                        t.wait_ge(C[tag + "ue"].sem, base_ue + tt)
                    for (f0, f1) in ((0, 512), (512, 768)):
                        for chk in range(16):
                            inst = t.matmul(
                                gup[:, f0:f1],
                                xnT[:, tt, chk * 128:(chk + 1) * 128],
                                sl[:, chk * 768 + f0:chk * 768 + f1],
                                start=(chk == 0), stop=(chk == 15))
                    C[tag + "um"].inc(inst)
                    t.wait_ge(C[tag + "hh"].sem, base_hh + tt + 1)
                    if tt >= 2:
                        t.wait_ge(C[tag + "he"].sem, base_he + tt - 1)
                    for ic in range(6):
                        inst = t.transpose(
                            htp[:, ic * 128:(ic + 1) * 128],
                            htmp[:, tt % 2, ic * 128:(ic + 1) * 128],
                            ident)
                    C[tag + "ht"].inc(inst)

            for tt in range(n_tiles):
                @blk.vector
                def _(v, e=e, tt=tt, base_um=base_um, base_ht=base_ht):
                    v.wait_ge(C[tag + "um"].sem, base_um + tt + 1)
                    v.tensor_scalar(utmp[:, :], gup[:, 0:768],
                                    wn[:, tt, e:e + 1], None, ALU.mult)
                    C[tag + "ue"].inc(v.drain(fusable=False))
                    if tt >= 2:
                        v.wait_ge(C[tag + "ht"].sem, base_ht + tt - 1)
                    C[tag + "hh"].inc(v.tensor_mul(
                        htmp[:, tt % 2, :], gact[:, tt, :], utmp[:, :]))

                @blk.vector
                def _(v, tt=tt, base_ht=base_ht, dm_cur=dm_cur):
                    v.wait_ge(C[tag + "ht"].sem, base_ht + tt + 1)
                    v.wait_ge(C[tag + "dm"].sem, dm_cur)
                    C[tag + "he"].inc(v.tensor_copy(hT[:, tt, :],
                                                    htp[:, 0:768]))
            phase_done[k] = (tag + "he", C[tag + "he"].n)

        else:
            base_dm = C[tag + "dm"].n
            base_de = C[tag + "de"].n
            he_cur = C[tag + "he"].n

            @blk.tensor
            def _(t, k=k, ld_needed=ld_needed, base_de=base_de, he_cur=he_cur):
                t.wait_ge(C[ldch[k % nring]].sem, ld_needed)
                t.wait_ge(C[tag + "he"].sem, he_cur)
                sl = slots[k % nring]
                for tt in range(n_tiles):
                    if tt >= 1:
                        t.wait_ge(C[tag + "de"].sem, base_de + tt)
                    for ic in range(6):
                        for nb in range(4):
                            inst = t.matmul(
                                dpsum[:, nb * 512:(nb + 1) * 512],
                                hT[:, tt, ic * 128:(ic + 1) * 128],
                                sl[:, ic * 2048 + nb * 512:
                                   ic * 2048 + (nb + 1) * 512],
                                start=(ic == 0), stop=(ic == 5))
                    C[tag + "dm"].inc(inst)

            base_dm2 = C[tag + "dm"].n - n_tiles
            for tt in range(n_tiles):
                @blk.vector
                def _(v, tt=tt, base_dm2=base_dm2):
                    v.wait_ge(C[tag + "dm"].sem, base_dm2 + tt + 1)
                    C[tag + "de"].inc(v.tensor_add(
                        x_sb[:, tt, :], x_sb[:, tt, :], dpsum[:, :]))
            phase_done[k] = (tag + "de", C[tag + "de"].n)


def build_l2a(n_tiles=8, n_exp=32):
    nc = bass.Bass("TRN2", target_bir_lowering=False)
    ntok = n_tiles * 128

    xin = nc.dram_tensor("xin", [ntok, H], F32, kind="ExternalInput").ap()
    aoT = nc.dram_tensor("aoT", [H, ntok], F32, kind="ExternalInput").ap()
    woT = nc.dram_tensor("woT", [H, H], F32, kind="ExternalInput").ap()
    routerT = nc.dram_tensor("routerT", [H, n_exp], F32, kind="ExternalInput").ap()
    gwT = nc.dram_tensor("gwT", [n_exp, H, 768], F32, kind="ExternalInput").ap()
    uwT = nc.dram_tensor("uwT", [n_exp, H, 768], F32, kind="ExternalInput").ap()
    dwT = nc.dram_tensor("dwT", [n_exp, 768, H], F32, kind="ExternalInput").ap()
    win = nc.dram_tensor("win", [H, 8576], F32, kind="ExternalInput").ap()
    dtb_in = nc.dram_tensor("dtb", [64, 1], F32, kind="ExternalInput").ap()
    alog_in = nc.dram_tensor("alog", [64, 1], F32, kind="ExternalInput").ap()

    x2_out = nc.dram_tensor("x2", [ntok, H], F32, kind="ExternalOutput").ap()
    xbct_out = nc.dram_tensor("xbct", [CONV_DIM, ntok], F32, kind="ExternalOutput").ap()
    zt_out = nc.dram_tensor("zt", [D_INNER, ntok], BF16, kind="ExternalOutput").ap()
    dtt_out = nc.dram_tensor("dtt", [64, ntok], F32, kind="ExternalOutput").ap()
    dat_out = nc.dram_tensor("dat", [64, ntok], F32, kind="ExternalOutput").ap()

    gw_ap = [gwT[e] for e in range(n_exp)]
    uw_ap = [uwT[e] for e in range(n_exp)]
    dw_ap = [dwT[e] for e in range(n_exp)]

    with ExitStack() as ctx:
        sb = lambda n, s, d: ctx.enter_context(nc.sbuf_tensor(n, s, d))
        x_sb = sb("x_sb", [128, n_tiles, H], F32)
        xn = sb("xn", [128, H], BF16)
        xnT = sb("xnT", [128, n_tiles, H], BF16)
        wn = sb("wn", [128, n_tiles, n_exp], F32)
        cst = sb("cst", [128, 2], F32)
        ident = sb("ident", [128, 128], BF16)
        stats = sb("stats", [128, 32], F32)
        sqscr = sb("sqscr", [128, H], BF16)
        rt_sb = sb("rt_sb", [128, 16, n_exp], BF16)
        dtb = sb("dtbs", [128, 1], F32)
        nega = sb("nega", [128, 1], F32)

        names = ["dxa0", "dxa1", "dao0", "dao1", "dwo", "drt", "dsm", "dse",
                 "sq", "nv", "rs", "xn", "fat", "fae", "wom", "woe",
                 "rmm", "rmx", "rex", "rev",
                 "fgm", "fge", "fum", "fue", "fhh", "fht", "fhe", "fdm", "fde",
                 "fl0", "fl1", "fl2", "dwi0", "dwi1", "wim", "wie",
                 "dtc", "dtv", "st", "stz0", "stz1", "stb0", "stb1", "stx"]
        C = {n: Ch(ctx, nc, "c_" + n) for n in names}

        blk = ctx.enter_context(nc.Block())

        @blk.gpsimd
        def _(g):
            C["dse"].inc(g.memset(ident[:, :], 0.0))
            C["dse"].inc(g.memset(cst[:, 0:1], EPS))
            C["dse"].inc(g.memset(cst[:, 1:2], 1.0))
            C["dse"].inc(g.memset(dtb[:, :], 0.0))
            C["dse"].inc(g.memset(nega[:, :], 0.0))
            g.wait_ge(C["dse"].sem, C["dse"].n)
            C["dse"].inc(g.affine_select(
                out=ident[:, :], in_=ident[:, :],
                compare_op=ALU.not_equal, fill=1.0, base=0,
                pattern=[[-1, 128]], channel_multiplier=1))
            C["drt"].inc(g.dma_start(
                out=rt_sb[:, :, :],
                in_=routerT.rearrange("(c p) f -> p c f", p=128)), 16)

        @blk.sync
        def _(sy):
            sy.wait_ge(C["dse"].sem, 5)
            C["dsm"].inc(sy.dma_start(out=dtb[0:64, :], in_=dtb_in), 16)
            C["dsm"].inc(sy.dma_start(out=nega[0:64, :], in_=alog_in), 16)

        @blk.scalar
        def _(s):
            s.wait_ge(C["dse"].sem, C["dse"].n)
            s.wait_ge(C["dsm"].sem, C["dsm"].n)
            C["dse"].inc(s.activation(nega[:, :], nega[:, :], AF.Exp))

        pre_done = dict(dse=C["dse"].n, dsm=C["dsm"].n, drt=C["drt"].n)

        def wait_pre(e):
            for k, v in pre_done.items():
                e.wait_ge(C[k].sem, v)

        @blk.vector
        def _(v):
            wait_pre(v)

        @blk.tensor
        def _(t):
            wait_pre(t)

        @blk.scalar
        def _(s):
            wait_pre(s)

        with ExitStack() as p1:
            sb1 = lambda n, s, d: p1.enter_context(nc.sbuf_tensor(n, s, d))
            ps1 = lambda n, s, d: p1.enter_context(nc.psum_tensor(n, s, d))
            wo_sb = sb1("wo_sb", [128, 16, H], BF16)
            ao_sb = sb1("ao_sb", [128, 2, H], BF16)
            xt = sb1("xt", [128, 2, H], F32)
            wop = ps1("wop", [128, H], F32)
            trp = ps1("trp", [128, 2048], BF16)

            @blk.gpsimd
            def _(g):
                C["dwo"].inc(g.dma_start(
                    out=wo_sb[:, :, :],
                    in_=woT.rearrange("(c p) f -> p c f", p=128)), 16)

            for tt in range(n_tiles):
                @blk.sync
                def _(sy, tt=tt):
                    if tt >= 2:
                        sy.wait_ge(C["woe"].sem, tt - 1)
                    C["dxa%d" % (tt % 2)].inc(sy.dma_start(
                        out=xt[:, tt % 2, :],
                        in_=xin[tt * 128:(tt + 1) * 128, :]), 16)

                @blk.gpsimd
                def _(g, tt=tt):
                    if tt >= 2:
                        sl = C["wom"].n
                        g.wait_ge(C["wom"].sem, sl - 1)
                    C["dao%d" % (tt % 2)].inc(g.dma_start(
                        out=ao_sb[:, tt % 2, :].rearrange("p (c t) -> p c t", c=16),
                        in_=aoT[:, tt * 128:(tt + 1) * 128].rearrange(
                            "(c p) t -> p c t", p=128)), 16)

                dao_n = C["dao%d" % (tt % 2)].n
                woe_n = C["woe"].n

                @blk.tensor
                def _(t, tt=tt, dao_n=dao_n, woe_n=woe_n):
                    t.wait_ge(C["dao%d" % (tt % 2)].sem, dao_n)
                    if tt == 0:
                        t.wait_ge(C["dwo"].sem, 16)
                    if tt >= 1:
                        t.wait_ge(C["woe"].sem, woe_n)
                    for chk in range(16):
                        for nb in range(4):
                            inst = t.matmul(
                                wop[:, nb * 512:(nb + 1) * 512],
                                ao_sb[:, tt % 2, chk * 128:(chk + 1) * 128],
                                wo_sb[:, chk, nb * 512:(nb + 1) * 512],
                                start=(chk == 0), stop=(chk == 15))
                    C["wom"].inc(inst)

                dx_n = C["dxa%d" % (tt % 2)].n
                wom_n = C["wom"].n

                @blk.vector
                def _(v, tt=tt, dx_n=dx_n, wom_n=wom_n):
                    v.wait_ge(C["dxa%d" % (tt % 2)].sem, dx_n)
                    v.wait_ge(C["wom"].sem, wom_n)
                    C["woe"].inc(v.tensor_add(x_sb[:, tt, :], xt[:, tt % 2, :],
                                              wop[:, :]))

                if tt >= 1:
                    @blk.vector
                    def _(v, tt=tt):
                        v.wait_ge(C["fat"].sem, tt)

                emit_ln_tile(blk, C, x_sb[:, tt, :], xn[:, :], sqscr, stats,
                             cst[:, 0:1], "woe")
                emit_featmajor(blk, C, nc, xn, xnT[:, tt, :], trp, ident[:, :],
                               "fa")

        phase_barrier(blk, C)

        with ExitStack() as p2:
            rpsum = p2.enter_context(nc.psum_tensor("rpsum", [128, max(n_exp, 32)], F32))
            probs = p2.enter_context(nc.sbuf_tensor("probs", [128, n_exp], F32))
            top8 = p2.enter_context(nc.sbuf_tensor("top8", [128, 8], F32))
            emit_router(blk, C, nc, xnT, wn, rt_sb, rpsum, probs, top8, stats,
                        n_tiles)

        phase_barrier(blk, C)

        with ExitStack() as p3:
            sb3 = lambda n, s, d: p3.enter_context(nc.sbuf_tensor(n, s, d))
            ps3 = lambda n, s, d: p3.enter_context(nc.psum_tensor(n, s, d))
            slots = [sb3("slot%d" % i, [128, 12288], BF16) for i in range(3)]
            gact = sb3("gact", [128, n_tiles, 768], BF16)
            htmp = sb3("htmp", [128, 2, 768], BF16)
            utmp = sb3("utmp", [128, 768], BF16)
            gup = ps3("gup", [128, 1024], F32)
            htp = ps3("htp", [128, 1024], BF16)
            dpsum = ps3("dpsum", [128, 2048], F32)
            emit_ffn(blk, C, nc, x_sb, xnT, wn, slots, gact, gact, htmp, utmp,
                     gup, htp, dpsum, ident[:, :], gw_ap, uw_ap, dw_ap,
                     n_tiles, n_exp, ["fl0", "fl1", "fl2"], "f")

        phase_barrier(blk, C)

        with ExitStack() as p4:
            trp = p4.enter_context(nc.psum_tensor("trp2", [128, 2048], BF16))
            for tt in range(n_tiles):
                if tt >= 1:
                    @blk.vector
                    def _(v, tt=tt):
                        v.wait_ge(C["fat"].sem, n_tiles + tt)

                emit_ln_tile(blk, C, x_sb[:, tt, :], xn[:, :], sqscr, stats,
                             cst[:, 0:1], "fde")
                emit_featmajor(blk, C, nc, xn, xnT[:, tt, :], trp, ident[:, :],
                               "fa")

                @blk.sync
                def _(sy, tt=tt):
                    sy.wait_ge(C["fde"].sem, C["fde"].n)
                    C["stx"].inc(sy.dma_start(
                        out=x2_out[tt * 128:(tt + 1) * 128, :],
                        in_=x_sb[:, tt, :]), 16)

        phase_barrier(blk, C)

        with ExitStack() as p5:
            sb5 = lambda n, s, d: p5.enter_context(nc.sbuf_tensor(n, s, d))
            ps5 = lambda n, s, d: p5.enter_context(nc.psum_tensor(n, s, d))
            wi_sb = sb5("wi_sb", [128, 2, 16, 128], BF16)
            zst = sb5("zst", [128, 2, 1024], BF16)
            xst = sb5("xst", [128, 2, 1024], F32)
            dtt_sb = sb5("dtt_sb", [128, 1024], F32)
            dat_sb = sb5("dat_sb", [128, 1024], F32)
            dsc = sb5("dsc", [128, 2, 1024], F32)
            wip = [ps5("wip%d" % i, [128, 1024], F32) for i in range(2)]

            n_dt = 67
            tile_groups = [(i, min(i + 4, n_tiles)) for i in range(0, n_tiles, 4)]
            fae_all = C["fae"].n
            for d in range(n_dt):
                @blk.gpsimd
                def _(g, d=d):
                    if d >= 2:
                        g.wait_ge(C["wim"].sem, d - 1)
                    C["dwi%d" % (d % 2)].inc(g.dma_start(
                        out=wi_sb[:, d % 2, :, :],
                        in_=win[:, d * 128:(d + 1) * 128].rearrange(
                            "(c p) f -> p c f", p=128)), 16)

                dwi_n = C["dwi%d" % (d % 2)].n
                wie_n = C["wie"].n

                @blk.tensor
                def _(t, d=d, dwi_n=dwi_n, wie_n=wie_n):
                    t.wait_ge(C["dwi%d" % (d % 2)].sem, dwi_n)
                    if d == 0:
                        t.wait_ge(C["fae"].sem, fae_all)
                    if d >= 2:
                        t.wait_ge(C["wie"].sem, wie_n - 1)
                    for gi, (t0, t1) in enumerate(tile_groups):
                        for chk in range(16):
                            inst = t.matmul(
                                wip[d % 2][:, t0 * 128:t1 * 128],
                                wi_sb[:, d % 2, chk, :],
                                xnT[:, t0:t1, chk * 128:(chk + 1) * 128],
                                start=(chk == 0), stop=(chk == 15))
                    C["wim"].inc(inst)

                wim_n = C["wim"].n

                if d < 32:
                    @blk.vector
                    def _(v, d=d, wim_n=wim_n):
                        v.wait_ge(C["wim"].sem, wim_n)
                        if d >= 2:
                            v.wait_ge(C["stz%d" % (d % 2)].sem, 16 * (d // 2))
                        C["wie"].inc(v.tensor_copy(zst[:, d % 2, 0:ntok],
                                                   wip[d % 2][:, 0:ntok]))

                    @blk.sync
                    def _(sy, d=d):
                        sy.wait_ge(C["wie"].sem, C["wie"].n)
                        C["stz%d" % (d % 2)].inc(sy.dma_start(
                            out=zt_out[d * 128:(d + 1) * 128, :],
                            in_=zst[:, d % 2, 0:ntok]), 16)
                elif d < 66:
                    @blk.vector
                    def _(v, d=d, wim_n=wim_n):
                        v.wait_ge(C["wim"].sem, wim_n)
                        if d >= 34:
                            v.wait_ge(C["stb%d" % (d % 2)].sem,
                                      16 * ((d - 32) // 2))
                        C["wie"].inc(v.tensor_copy(xst[:, d % 2, 0:ntok],
                                                   wip[d % 2][:, 0:ntok]))

                    @blk.sync
                    def _(sy, d=d):
                        sy.wait_ge(C["wie"].sem, C["wie"].n)
                        C["stb%d" % (d % 2)].inc(sy.dma_start(
                            out=xbct_out[(d - 32) * 128:(d - 31) * 128, :],
                            in_=xst[:, d % 2, 0:ntok]), 16)
                else:
                    @blk.vector
                    def _(v, d=d, wim_n=wim_n):
                        v.wait_ge(C["wim"].sem, wim_n)
                        C["dtv"].inc(v.tensor_scalar(dsc[:, 0, 0:ntok],
                                                     wip[d % 2][:, 0:ntok],
                                                     dtb[:, :], None, ALU.add))

                    @blk.scalar
                    def _(s):
                        s.wait_ge(C["dtv"].sem, C["dtv"].n)
                        s.activation(dsc[:, 1, 0:ntok], dsc[:, 0, 0:ntok],
                                     AF.Abs)
                        s.drain(fusable=False)
                        s.activation(dsc[:, 1, 0:ntok], dsc[:, 1, 0:ntok],
                                     AF.Exp, scale=-1.0)
                        s.drain(fusable=False)
                        C["dtc"].inc(s.activation(dsc[:, 1, 0:ntok],
                                                  dsc[:, 1, 0:ntok],
                                                  AF.Ln, bias=cst[:, 1:2]))

                    @blk.vector
                    def _(v):
                        v.wait_ge(C["dtc"].sem, C["dtc"].n)
                        v.tensor_scalar(dsc[:, 0, 0:ntok], dsc[:, 0, 0:ntok],
                                        0.0, None, ALU.max)
                        v.drain(fusable=False)
                        v.tensor_add(dtt_sb[:, 0:ntok], dsc[:, 0, 0:ntok],
                                     dsc[:, 1, 0:ntok])
                        v.drain(fusable=False)
                        v.tensor_scalar(dat_sb[:, 0:ntok], dtt_sb[:, 0:ntok],
                                        nega[:, :], -1.0, ALU.mult, ALU.mult)
                        C["dtv"].inc(v.drain(fusable=False))

                    @blk.sync
                    def _(sy):
                        sy.wait_ge(C["dtv"].sem, C["dtv"].n)
                        C["st"].inc(sy.dma_start(out=dtt_out,
                                                 in_=dtt_sb[0:64, 0:ntok]), 16)
                        C["st"].inc(sy.dma_start(out=dat_out,
                                                 in_=dat_sb[0:64, 0:ntok]), 16)

        @blk.sync
        def _(sy):
            for chn in ("st", "stz0", "stz1", "stb0", "stb1", "stx"):
                sy.wait_ge(C[chn].sem, C[chn].n)

    return nc


def build_l2b(n_tiles=8):
    nc = bass.Bass("TRN2", target_bir_lowering=False)
    ntok = n_tiles * 128
    nch = ntok // 256

    xbce = nc.dram_tensor("xbce", [CONV_DIM, ntok + 3], F32, kind="ExternalInput").ap()
    dtt_in = nc.dram_tensor("dtt", [64, ntok], F32, kind="ExternalInput").ap()
    dat_in = nc.dram_tensor("dat", [64, ntok], F32, kind="ExternalInput").ap()
    convw = nc.dram_tensor("convw", [CONV_DIM, 4], F32, kind="ExternalInput").ap()
    convb = nc.dram_tensor("convb", [CONV_DIM, 1], F32, kind="ExternalInput").ap()
    triu = nc.dram_tensor("triu", [256, 256], F32, kind="ExternalInput").ap()

    xbao = nc.dram_tensor("xbca", [CONV_DIM, ntok], BF16, kind="ExternalOutput").ap()
    acso = nc.dram_tensor("acs", [64, ntok], F32, kind="ExternalOutput").ap()
    csto = nc.dram_tensor("cstates", [nch, 128, 64, 64], F32, kind="ExternalOutput").ap()
    ssto = nc.dram_tensor("sstate", [128, 64, 64], F32, kind="ExternalOutput").ap()
    ctoto = nc.dram_tensor("ctot", [64, nch], F32, kind="ExternalOutput").ap()
    stoto = nc.dram_tensor("stot", [64, 1], F32, kind="ExternalOutput").ap()

    NPT = CONV_DIM // 128

    with ExitStack() as ctx:
        sb = lambda n, s, d: ctx.enter_context(nc.sbuf_tensor(n, s, d))
        xact = sb("xact", [128, NPT, ntok], BF16)
        cw = sb("cw", [128, NPT, 4], F32)
        cb = sb("cb", [128, NPT, 1], F32)
        tri = sb("tri", [128, 2, 256], F32)
        dts = sb("dts", [128, ntok], F32)
        das = sb("das", [128, ntok], F32)
        ident = sb("identb", [128, 128], BF16)
        identf = sb("identf", [128, 128], F32)
        acs_sb = sb("acs_sb", [128, nch, 256], F32)
        tot_sb = sb("tot_sb", [128, 2 * nch], F32)
        fseg = sb("fseg", [128, nch, 256], F32)
        fchk = sb("fchk", [128, nch, 256], F32)
        fsegt = sb("fsegt", [128, nch, 2, 128], F32)
        fchkt = sb("fchkt", [128, nch, 2, 128], F32)
        datok = sb("datok", [128, 2, 128], F32)
        btok = sb("btok", [128, nch, 2, 128], BF16)
        xtok = sb("xtok", [128, nch, 2, 256], BF16)
        xds = sb("xds", [128, 256], BF16)
        sstage = sb("sstage", [128, 4, 64], F32)

        names = ["dxc0", "dxc1", "dcst", "dcsg", "dse", "cnv", "slu", "datr", "date",
                 "acm", "ace", "ttv", "fex", "fmu", "ftr", "fte", "btr", "bte",
                 "xtr", "xte", "xdv", "smm", "sev", "stS", "stC", "sta",
                 "stx0", "stx1"]
        C = {n: Ch(ctx, nc, "c_" + n) for n in names}

        blk = ctx.enter_context(nc.Block())

        @blk.gpsimd
        def _(g):
            C["dse"].inc(g.memset(ident[:, :], 0.0))
            C["dse"].inc(g.memset(identf[:, :], 0.0))
            C["dse"].inc(g.memset(dts[:, :], 0.0))
            C["dse"].inc(g.memset(das[:, :], 0.0))
            g.wait_ge(C["dse"].sem, C["dse"].n)
            C["dse"].inc(g.affine_select(
                out=ident[:, :], in_=ident[:, :],
                compare_op=ALU.not_equal, fill=1.0, base=0,
                pattern=[[-1, 128]], channel_multiplier=1))
            C["dse"].inc(g.affine_select(
                out=identf[:, :], in_=identf[:, :],
                compare_op=ALU.not_equal, fill=1.0, base=0,
                pattern=[[-1, 128]], channel_multiplier=1))

        @blk.sync
        def _(sy):
            C["dcst"].inc(sy.dma_start(
                out=cw[:, :, :], in_=convw.rearrange("(a p) f -> p a f", p=128)), 16)
            C["dcst"].inc(sy.dma_start(
                out=cb[:, :, :], in_=convb.rearrange("(a p) f -> p a f", p=128),
                allow_slow_non_contiguous=True), 16)
            C["dcst"].inc(sy.dma_start(
                out=tri[:, :, :], in_=triu.rearrange("(a p) l -> p a l", p=128)), 16)

        @blk.gpsimd
        def _(g):
            g.wait_ge(C["dse"].sem, 4)
            C["dcsg"].inc(g.dma_start(out=dts[0:64, :], in_=dtt_in), 16)
            C["dcsg"].inc(g.dma_start(out=das[0:64, :], in_=dat_in), 16)

        pre = dict(dse=C["dse"].n, dcst=C["dcst"].n, dcsg=C["dcsg"].n)

        for nm in ("tensor", "vector", "scalar"):
            def emit(e, pre=pre):
                for k, v in pre.items():
                    e.wait_ge(C[k].sem, v)
            getattr(blk, nm)(emit)

        with ExitStack() as p1:
            sb1 = lambda n, s, d: p1.enter_context(nc.sbuf_tensor(n, s, d))
            xr = sb1("xr", [128, 2, ntok + 3], F32)
            cacc = sb1("cacc", [128, 2, ntok], F32)

            for pt in range(NPT):
                @blk.sync
                def _(sy, pt=pt):
                    if pt >= 2:
                        sy.wait_ge(C["cnv"].sem, 3 * (pt - 1))
                    C["dxc%d" % (pt % 2)].inc(sy.dma_start(
                        out=xr[:, pt % 2, :],
                        in_=xbce[pt * 128:(pt + 1) * 128, :]), 16)

                dx_n = C["dxc%d" % (pt % 2)].n
                slu_n = C["slu"].n

                @blk.vector
                def _(v, pt=pt, dx_n=dx_n, slu_n=slu_n):
                    v.wait_ge(C["dxc%d" % (pt % 2)].sem, dx_n)
                    if pt >= 2:
                        v.wait_ge(C["slu"].sem, slu_n - 1)
                    b = pt % 2
                    v.tensor_scalar(cacc[:, b, :], xr[:, b, 0:ntok],
                                    cw[:, pt, 0:1], cb[:, pt, 0:1],
                                    ALU.mult, ALU.add)
                    C["cnv"].inc(v.drain(fusable=False))
                    v.scalar_tensor_tensor(cacc[:, b, :], xr[:, b, 1:1 + ntok],
                                           cw[:, pt, 1:2], cacc[:, b, :],
                                           ALU.mult, ALU.add)
                    C["cnv"].inc(v.drain(fusable=False))
                    v.scalar_tensor_tensor(cacc[:, b, :], xr[:, b, 2:2 + ntok],
                                           cw[:, pt, 2:3], cacc[:, b, :],
                                           ALU.mult, ALU.add)
                    v.drain(fusable=False)
                    v.scalar_tensor_tensor(cacc[:, b, :], xr[:, b, 3:3 + ntok],
                                           cw[:, pt, 3:4], cacc[:, b, :],
                                           ALU.mult, ALU.add)
                    C["cnv"].inc(v.drain(fusable=False))

                cnv_n = C["cnv"].n

                @blk.scalar
                def _(s, pt=pt, cnv_n=cnv_n):
                    s.wait_ge(C["cnv"].sem, cnv_n)
                    if pt >= 2:
                        s.wait_ge(C["stx%d" % (pt % 2)].sem, 16 * (pt // 2))
                    C["slu"].inc(s.activation(xact[:, pt, :], cacc[:, pt % 2, :],
                                              AF.Silu))

                @blk.sync
                def _(sy, pt=pt):
                    sy.wait_ge(C["slu"].sem, C["slu"].n)
                    C["stx%d" % (pt % 2)].inc(sy.dma_start(
                        out=xbao[pt * 128:(pt + 1) * 128, :],
                        in_=xact[:, pt, :]), 16)

        phase_barrier(blk, C, engines=("tensor", "vector", "scalar", "sync"))

        with ExitStack() as p2:
            ps2 = lambda n, s, d: p2.enter_context(nc.psum_tensor(n, s, d))
            trp = ps2("trpb", [128, 512], F32)
            acp = ps2("acp", [128, 256], F32)
            trb = ps2("trbb", [128, 1024], BF16)

            for c in range(nch):
                datr_n0 = C["date"].n

                @blk.tensor
                def _(t, c=c):
                    if c >= 1:
                        t.wait_ge(C["date"].sem, c)
                    for half in range(2):
                        inst = t.transpose(
                            trp[:, half * 128:(half + 1) * 128],
                            das[:, c * 256 + half * 128:c * 256 + (half + 1) * 128],
                            identf[:, :])
                    C["datr"].inc(inst)

                @blk.vector
                def _(v, c=c):
                    v.wait_ge(C["datr"].sem, C["datr"].n)
                    C["date"].inc(v.tensor_copy(datok[:, :, :].rearrange(
                        "p a f -> p (a f)"), trp[:, 0:256]))

                @blk.tensor
                def _(t, c=c):
                    t.wait_ge(C["date"].sem, C["date"].n)
                    if c >= 1:
                        t.wait_ge(C["ace"].sem, c)
                    for half in range(2):
                        inst = t.matmul(acp[:, :], datok[:, half, :],
                                        tri[:, half, :],
                                        start=(half == 0), stop=(half == 1))
                    C["acm"].inc(inst)

                @blk.vector
                def _(v, c=c):
                    v.wait_ge(C["acm"].sem, C["acm"].n)
                    C["ace"].inc(v.tensor_copy(acs_sb[:, c, :], acp[:, :]))

            @blk.vector
            def _(v):
                v.wait_ge(C["ace"].sem, C["ace"].n)
                for c in range(nch):
                    v.tensor_copy(tot_sb[:, c:c + 1], acs_sb[:, c, 255:256])
                v.drain(fusable=False)
                v.tensor_copy(tot_sb[:, 2 * nch - 1:2 * nch],
                              tot_sb[:, nch - 1:nch])
                v.drain(fusable=False)
                for c in range(nch - 2, -1, -1):
                    v.tensor_add(tot_sb[:, nch + c:nch + c + 1],
                                 tot_sb[:, c:c + 1],
                                 tot_sb[:, nch + c + 1:nch + c + 2])
                    v.drain(fusable=False)
                C["ttv"].inc(v.engine_nop())

            @blk.sync
            def _(sy):
                sy.wait_ge(C["ace"].sem, C["ace"].n)
                C["sta"].inc(sy.dma_start(
                    out=acso, in_=acs_sb[0:64, :, :].rearrange("p c f -> p (c f)")), 16)
                sy.wait_ge(C["ttv"].sem, C["ttv"].n)
                C["sta"].inc(sy.dma_start(out=ctoto, in_=tot_sb[0:64, 0:nch]), 16)
                C["sta"].inc(sy.dma_start(out=stoto,
                                          in_=tot_sb[0:64, nch:nch + 1]), 16)

            for c in range(nch):
                @blk.scalar
                def _(s, c=c):
                    s.wait_ge(C["ttv"].sem, C["ttv"].n)
                    s.activation(fseg[:, c, :], acs_sb[:, c, :], AF.Exp,
                                 scale=-1.0, bias=tot_sb[:, nch + c:nch + c + 1])
                    s.drain(fusable=False)
                    C["fex"].inc(s.activation(fchk[:, c, :], acs_sb[:, c, :],
                                              AF.Exp, scale=-1.0,
                                              bias=tot_sb[:, c:c + 1]))

                fex_n = C["fex"].n

                @blk.vector
                def _(v, c=c, fex_n=fex_n):
                    v.wait_ge(C["fex"].sem, fex_n)
                    v.tensor_mul(fseg[:, c, :], fseg[:, c, :], dts[:, c * 256:(c + 1) * 256])
                    v.drain(fusable=False)
                    C["fmu"].inc(v.tensor_mul(fchk[:, c, :], fchk[:, c, :],
                                              dts[:, c * 256:(c + 1) * 256]))

                fmu_n2 = C["fmu"].n

                @blk.tensor
                def _(t, c=c, fmu_n2=fmu_n2):
                    t.wait_ge(C["fmu"].sem, fmu_n2)
                    if c >= 1:
                        t.wait_ge(C["fte"].sem, 2 * c)
                    for half in range(2):
                        t.transpose(trp[:, half * 128:(half + 1) * 128],
                                    fseg[:, c, half * 128:(half + 1) * 128],
                                    identf[:, :])
                    C["ftr"].inc(t.transpose(trp[:, 256:384],
                                             fchk[:, c, 0:128], identf[:, :]))
                    C["ftr"].inc(t.transpose(trp[:, 384:512],
                                             fchk[:, c, 128:256], identf[:, :]))

                @blk.vector
                def _(v, c=c):
                    v.wait_ge(C["ftr"].sem, C["ftr"].n)
                    v.tensor_copy(fsegt[:, c, :, :].rearrange("p a f -> p (a f)"),
                                  trp[:, 0:256])
                    C["fte"].inc(v.drain(fusable=False))
                    v.tensor_copy(fchkt[:, c, :, :].rearrange("p a f -> p (a f)"),
                                  trp[:, 256:512])
                    C["fte"].inc(v.drain(fusable=False))

                @blk.tensor
                def _(t, c=c):
                    if c >= 1:
                        t.wait_ge(C["bte"].sem, c)
                    for half in range(2):
                        inst = t.transpose(
                            trb[:, half * 128:(half + 1) * 128],
                            xact[:, 32, c * 256 + half * 128:c * 256 + (half + 1) * 128],
                            ident[:, :])
                    C["btr"].inc(inst)

                @blk.vector
                def _(v, c=c):
                    v.wait_ge(C["btr"].sem, C["btr"].n)
                    C["bte"].inc(v.tensor_copy(
                        btok[:, c, :, :].rearrange("p a f -> p (a f)"),
                        trb[:, 0:256]))

        phase_barrier(blk, C, engines=("tensor", "vector", "scalar", "sync"))

        with ExitStack() as p3:
            ps3 = lambda n, s, d: p3.enter_context(nc.psum_tensor(n, s, d))
            spsum = [ps3("sps%d" % i, [128, 64], F32) for i in range(4)]
            trb2 = ps3("trb2", [128, 1024], BF16)

            for r in range(16):
                xte0 = C["xte"].n

                @blk.tensor
                def _(t, r=r, xte0=xte0):
                    for c in range(nch):
                        if r >= 1 or c >= 1:
                            t.wait_ge(C["xte"].sem, xte0 + c)
                        for half in range(2):
                            for q in range(2):
                                inst = t.transpose(
                                    trb2[:, (half * 2 + q) * 128:
                                         (half * 2 + q + 1) * 128],
                                    xact[:, r * 2 + q,
                                         c * 256 + half * 128:c * 256 + (half + 1) * 128],
                                    ident[:, :])
                        C["xtr"].inc(inst)

                base_xtr = C["xtr"].n - nch

                for c in range(nch):
                    @blk.vector
                    def _(v, c=c, base_xtr=base_xtr):
                        v.wait_ge(C["xtr"].sem, base_xtr + c + 1)
                        C["xte"].inc(v.tensor_copy(
                            xtok[:, c, :, :].rearrange("p a f -> p (a f)"),
                            trb2[:, 0:512]))

                for c in range(nch):
                    for half in range(2):
                        xte_n = C["xte"].n - nch + c + 1
                        xdv0 = C["xdv"].n

                        @blk.vector
                        def _(v, r=r, c=c, half=half, xte_n=xte_n, xdv0=xdv0):
                            v.wait_ge(C["xte"].sem, xte_n)
                            if xdv0 >= 1:
                                v.wait_ge(C["smm"].sem, C["smm"].n)
                            for hq in range(4):
                                v.tensor_scalar(
                                    xds[:, hq * 64:(hq + 1) * 64],
                                    xtok[:, c, half, hq * 64:(hq + 1) * 64],
                                    fsegt[:, c, half, r * 4 + hq:r * 4 + hq + 1],
                                    None, ALU.mult)
                            C["xdv"].inc(v.drain(fusable=False))

                        @blk.tensor
                        def _(t, r=r, c=c, half=half):
                            t.wait_ge(C["xdv"].sem, C["xdv"].n)
                            for hq in range(4):
                                inst = t.matmul(
                                    spsum[hq][:, :], btok[:, c, half, :],
                                    xds[:, hq * 64:(hq + 1) * 64],
                                    start=(c == 0 and half == 0),
                                    stop=(c == nch - 1 and half == 1))
                            C["smm"].inc(inst)

                @blk.vector
                def _(v, r=r):
                    v.wait_ge(C["smm"].sem, C["smm"].n)
                    v.wait_ge(C["stS"].sem, C["stS"].n)
                    v.wait_ge(C["stC"].sem, C["stC"].n)
                    for hq in range(4):
                        inst = v.tensor_copy(sstage[:, hq, :], spsum[hq][:, :])
                    C["sev"].inc(v.drain(fusable=False))

                @blk.sync
                def _(sy, r=r):
                    sy.wait_ge(C["sev"].sem, C["sev"].n)
                    C["stS"].inc(sy.dma_start(
                        out=ssto[:, r * 4:(r + 1) * 4, :],
                        in_=sstage[:, :, :]), 16)

                for c in range(nch):
                    for half in range(2):
                        xdv0 = C["xdv"].n

                        @blk.vector
                        def _(v, r=r, c=c, half=half):
                            v.wait_ge(C["smm"].sem, C["smm"].n)
                            if half == 0:
                                v.wait_ge(C["stS"].sem, C["stS"].n)
                                v.wait_ge(C["stC"].sem, C["stC"].n)
                            for hq in range(4):
                                v.tensor_scalar(
                                    xds[:, hq * 64:(hq + 1) * 64],
                                    xtok[:, c, half, hq * 64:(hq + 1) * 64],
                                    fchkt[:, c, half, r * 4 + hq:r * 4 + hq + 1],
                                    None, ALU.mult)
                            C["xdv"].inc(v.drain(fusable=False))

                        @blk.tensor
                        def _(t, r=r, c=c, half=half):
                            t.wait_ge(C["xdv"].sem, C["xdv"].n)
                            for hq in range(4):
                                inst = t.matmul(
                                    spsum[hq][:, :], btok[:, c, half, :],
                                    xds[:, hq * 64:(hq + 1) * 64],
                                    start=(half == 0), stop=(half == 1))
                            C["smm"].inc(inst)

                    @blk.vector
                    def _(v, r=r, c=c):
                        v.wait_ge(C["smm"].sem, C["smm"].n)
                        v.wait_ge(C["stC"].sem, C["stC"].n)
                        v.wait_ge(C["stS"].sem, C["stS"].n)
                        for hq in range(4):
                            inst = v.tensor_copy(sstage[:, hq, :], spsum[hq][:, :])
                        C["sev"].inc(v.drain(fusable=False))

                    @blk.sync
                    def _(sy, r=r, c=c):
                        sy.wait_ge(C["sev"].sem, C["sev"].n)
                        C["stC"].inc(sy.dma_start(
                            out=csto[c, :, r * 4:(r + 1) * 4, :],
                            in_=sstage[:, :, :]), 16)

        @blk.sync
        def _(sy):
            for chn in ("stS", "stC", "sta", "stx0", "stx1"):
                sy.wait_ge(C[chn].sem, C[chn].n)

    return nc


def build_l3(n_tiles=8, n_exp=32, n_prior=7):
    nc = bass.Bass("TRN2", target_bir_lowering=False)
    ntok = n_tiles * 128
    nch = ntok // 256
    NPT = CONV_DIM // 128
    NB = 12

    x2_in = nc.dram_tensor("x2", [ntok, H], F32, kind="ExternalInput").ap()
    xbca = nc.dram_tensor("xbca", [CONV_DIM, ntok], BF16, kind="ExternalInput").ap()
    zt_in = nc.dram_tensor("zt", [D_INNER, ntok], BF16, kind="ExternalInput").ap()
    dtt_in = nc.dram_tensor("dtt", [64, ntok], F32, kind="ExternalInput").ap()
    acs_in = nc.dram_tensor("acs", [64, ntok], F32, kind="ExternalInput").ap()
    cst_in = nc.dram_tensor("cstates", [nch, 128, 64, 64], F32, kind="ExternalInput").ap()
    pst_in = nc.dram_tensor("pstates", [n_prior, 128, 64, 64], F32, kind="ExternalInput").ap()
    ptot_in = nc.dram_tensor("ptots", [64, n_prior], F32, kind="ExternalInput").ap()
    dvec_in = nc.dram_tensor("dvec", [64, 1], F32, kind="ExternalInput").ap()
    nsc_in = nc.dram_tensor("nscale", [D_INNER, 1], F32, kind="ExternalInput").ap()
    wout_in = nc.dram_tensor("wout", [D_INNER, H], F32, kind="ExternalInput").ap()
    routerT = nc.dram_tensor("routerT", [H, n_exp], F32, kind="ExternalInput").ap()
    gwT = nc.dram_tensor("gwT", [n_exp, H, 768], F32, kind="ExternalInput").ap()
    uwT = nc.dram_tensor("uwT", [n_exp, H, 768], F32, kind="ExternalInput").ap()
    dwT = nc.dram_tensor("dwT", [n_exp, 768, H], F32, kind="ExternalInput").ap()
    out = nc.dram_tensor("out", [ntok, H], F32, kind="ExternalOutput").ap()
    bounce = nc.dram_tensor("bounce", [64, NB], F32)
    x3scr = nc.dram_tensor("x3scr", [ntok, H], F32)

    gw_ap = [gwT[e] for e in range(n_exp)]
    uw_ap = [uwT[e] for e in range(n_exp)]
    dw_ap = [dwT[e] for e in range(n_exp)]

    with ExitStack() as ctx:
        sb = lambda n, s, d: ctx.enter_context(nc.sbuf_tensor(n, s, d))
        ident = sb("identb", [128, 128], BF16)
        identf = sb("identf", [128, 128], F32)
        cst = sb("cst", [128, 2], F32)
        msktri = sb("msktri", [128, 128], F32)
        onesf = sb("onesf", [1, 128], F32)
        stats = sb("stats", [128, 32], F32)
        acs_sb = sb("acs_sb", [128, ntok], F32)
        dts = sb("dts", [128, ntok], F32)
        Wb = sb("Wb", [128, 64 * NB], F32)

        names = ["dga", "dgb", "dgc", "dse", "acT", "acE", "gtm", "gte",
                 "scv", "scm", "sce", "xtr", "xte", "mtv", "mts",
                 "ydm", "yev", "ysv", "zld0", "zld1", "gat", "glu", "rms",
                 "ynt", "yne", "ysc", "wom", "woe", "dwo0", "dwo1", "dx2",
                 "stx0", "stx1", "dxb",
                 "sq", "nv", "rs", "xn", "fat", "fae",
                 "rmm", "rmx", "rex", "rev", "drt",
                 "fgm", "fge", "fum", "fue", "fhh", "fht", "fhe", "fdm", "fde",
                 "fl0", "fl1", "fl2", "sto", "dx30", "dx31"]
        C = {n: Ch(ctx, nc, "c_" + n) for n in names}

        blk = ctx.enter_context(nc.Block())

        @blk.gpsimd
        def _(g):
            for t_, v_ in ((ident, 0.0), (identf, 0.0), (msktri, 0.0)):
                C["dse"].inc(g.memset(t_[:, :], v_))
            C["dse"].inc(g.memset(cst[:, 0:1], EPS))
            C["dse"].inc(g.memset(cst[:, 1:2], 1.0))
            C["dse"].inc(g.memset(onesf[:, :], 1.0))
            C["dse"].inc(g.memset(acs_sb[:, :], 0.0))
            C["dse"].inc(g.memset(dts[:, :], 0.0))
            g.wait_ge(C["dse"].sem, C["dse"].n)
            C["dse"].inc(g.affine_select(
                out=ident[:, :], in_=ident[:, :], compare_op=ALU.not_equal,
                fill=1.0, base=0, pattern=[[-1, 128]], channel_multiplier=1))
            C["dse"].inc(g.affine_select(
                out=identf[:, :], in_=identf[:, :], compare_op=ALU.not_equal,
                fill=1.0, base=0, pattern=[[-1, 128]], channel_multiplier=1))
            C["dse"].inc(g.affine_select(
                out=msktri[:, :], in_=msktri[:, :], compare_op=ALU.is_ge,
                fill=-1e30, base=0, pattern=[[1, 128]], channel_multiplier=-1))

        @blk.gpsimd
        def _(g):
            g.wait_ge(C["dse"].sem, 8)
            C["dga"].inc(g.dma_start(out=acs_sb[0:64, :], in_=acs_in), 16)
            C["dga"].inc(g.dma_start(out=dts[0:64, :], in_=dtt_in), 16)

        pre = dict(dse=C["dse"].n, dga=C["dga"].n)
        for nm in ("tensor", "vector", "scalar", "sync"):
            def emit(e, pre=dict(pre)):
                for k, v in pre.items():
                    e.wait_ge(C[k].sem, v)
            getattr(blk, nm)(emit)

        with ExitStack() as pA:
            sbA = lambda n, s, d: pA.enter_context(nc.sbuf_tensor(n, s, d))
            y_sb = sbA("y_sb", [128, 2 * nch, D_INNER], BF16)
            Sc_bf = sbA("Sc_bf", [128, nch, D_INNER], BF16)

            with ExitStack() as p1:
                sb1 = lambda n, s, d: p1.enter_context(nc.sbuf_tensor(n, s, d))
                ps1 = lambda n, s, d: p1.enter_context(nc.psum_tensor(n, s, d))
                bblk = sb1("bblk", [128, NB], F32)
                bflat = sb1("bflat", [1, 64 * NB], F32)
                ps_buf = sb1("ps_buf", [128, D_INNER], F32)
                S_run = sb1("S_run", [128, D_INNER], F32)
                wbp = ps1("wbp", [128, 1024], F32)

                @blk.gpsimd
                def _(g):
                    g.wait_ge(C["dse"].sem, C["dse"].n)
                    C["dse"].inc(g.memset(bblk[:, :], 0.0))
                    C["dse"].inc(g.memset(S_run[:, :], 0.0))
                    g.wait_ge(C["dse"].sem, C["dse"].n)
                    C["dgc"].inc(g.dma_start(out=bblk[0:64, 0:n_prior],
                                             in_=ptot_in), 16)
                    C["dgc"].inc(g.dma_start(out=bblk[0:64, NB - 1:NB],
                                             in_=dvec_in), 16)

                @blk.vector
                def _(v):
                    v.wait_ge(C["dgc"].sem, C["dgc"].n)
                    v.tensor_scalar(stats[:, 4:5], bblk[:, 0:1], 0.0, None,
                                    ALU.mult)
                    C["acE"].inc(v.drain(fusable=False))

                for i in range(n_prior):
                    @blk.vector
                    def _(v, i=i):
                        v.wait_ge(C["acE"].sem, C["acE"].n)
                        v.tensor_copy(stats[:, 5:6], bblk[:, i:i + 1])
                        v.drain(fusable=False)
                        v.tensor_copy(bblk[:, i:i + 1], stats[:, 4:5])
                        v.drain(fusable=False)
                        if i < n_prior - 1:
                            v.tensor_add(stats[:, 4:5], stats[:, 4:5],
                                         stats[:, 5:6])
                        C["acE"].inc(v.drain(fusable=False))

                @blk.vector
                def _(v):
                    v.wait_ge(C["acE"].sem, C["acE"].n)
                    for c in range(nch):
                        v.tensor_copy(bblk[:, n_prior + c:n_prior + c + 1],
                                      acs_sb[:, c * 256 + 255:c * 256 + 256])
                    C["acE"].inc(v.drain(fusable=False))

                @blk.scalar
                def _(s):
                    s.wait_ge(C["acE"].sem, C["acE"].n)
                    C["sce"].inc(s.activation(bblk[:, 0:n_prior + nch],
                                              bblk[:, 0:n_prior + nch], AF.Exp))

                @blk.gpsimd
                def _(g):
                    g.wait_ge(C["sce"].sem, C["sce"].n)
                    C["dgb"].inc(g.dma_start(out=bounce.ap(), in_=bblk[0:64, :]), 16)
                    g.wait_ge(C["dgb"].sem, C["dgb"].n)
                    C["dgb"].inc(g.dma_start(
                        out=bflat[:, :],
                        in_=bounce.ap().rearrange("r c -> (r c)")), 16)

                dgb_n = C["dgb"].n

                @blk.tensor
                def _(t, dgb_n=dgb_n):
                    t.wait_ge(C["dgb"].sem, dgb_n)
                    for fb in range((64 * NB + 511) // 512):
                        f0, f1 = fb * 512, min((fb + 1) * 512, 64 * NB)
                        C["gtm"].inc(t.matmul(wbp[:, f0:f1], onesf[:, :],
                                              bflat[:, f0:f1],
                                              start=True, stop=True))

                @blk.vector
                def _(v):
                    v.wait_ge(C["gtm"].sem, C["gtm"].n)
                    C["gte"].inc(v.tensor_copy(Wb[:, :], wbp[:, 0:64 * NB]))

                for i in range(n_prior):
                    scv0 = C["scv"].n

                    @blk.gpsimd
                    def _(g, i=i, scv0=scv0):
                        if i >= 1:
                            g.wait_ge(C["scv"].sem, scv0)
                        C["dwo0"].inc(g.dma_start(
                            out=ps_buf[:, :],
                            in_=pst_in[i].rearrange("p a b -> p (a b)")), 16)

                    dw_n = C["dwo0"].n
                    gte_n = C["gte"].n

                    @blk.vector
                    def _(v, i=i, dw_n=dw_n, gte_n=gte_n):
                        v.wait_ge(C["dwo0"].sem, dw_n)
                        v.wait_ge(C["gte"].sem, gte_n)
                        if i == 0:
                            v.wait_ge(C["dse"].sem, C["dse"].n)
                        for h in range(64):
                            v.scalar_tensor_tensor(
                                S_run[:, h * 64:(h + 1) * 64],
                                ps_buf[:, h * 64:(h + 1) * 64],
                                Wb[:, h * NB + i:h * NB + i + 1],
                                S_run[:, h * 64:(h + 1) * 64],
                                ALU.mult, ALU.add)
                        C["scv"].inc(v.drain(fusable=False))

                for c in range(nch):
                    scv0 = C["scv"].n

                    @blk.gpsimd
                    def _(g, c=c, scv0=scv0):
                        g.wait_ge(C["scv"].sem, scv0)
                        if c < nch - 1:
                            C["dwo1"].inc(g.dma_start(
                                out=ps_buf[:, :],
                                in_=cst_in[c].rearrange("p a b -> p (a b)")), 16)

                    dw_n = C["dwo1"].n
                    scv_n = C["scv"].n

                    @blk.vector
                    def _(v, c=c, dw_n=dw_n, scv_n=scv_n):
                        v.wait_ge(C["scv"].sem, scv_n)
                        v.tensor_copy(Sc_bf[:, c, :], S_run[:, :])
                        C["scm"].inc(v.drain(fusable=False))
                        if c < nch - 1:
                            v.wait_ge(C["dwo1"].sem, dw_n)
                            for h in range(64):
                                v.scalar_tensor_tensor(
                                    S_run[:, h * 64:(h + 1) * 64],
                                    S_run[:, h * 64:(h + 1) * 64],
                                    Wb[:, h * NB + n_prior + c:
                                       h * NB + n_prior + c + 1],
                                    ps_buf[:, h * 64:(h + 1) * 64],
                                    ALU.mult, ALU.add)
                            C["scv"].inc(v.drain(fusable=False))

            phase_barrier(blk, C)

            with ExitStack() as p2:
                sb2 = lambda n, s, d: p2.enter_context(nc.sbuf_tensor(n, s, d))
                ps2 = lambda n, s, d: p2.enter_context(nc.psum_tensor(n, s, d))
                xb_sb = sb2("xb_sb", [128, NPT, ntok], BF16)
                acstok = sb2("acstok", [128, nch, 2, 128], F32)
                negtok = sb2("negtok", [128, nch, 2, 128], F32)
                eactok = sb2("eactok", [128, nch, 2, 128], F32)
                dtttok = sb2("dtttok", [128, nch, 2, 128], F32)
                gt_sb = sb2("gt_sb", [128, nch, 384], BF16)
                zerof = sb2("zerof", [128, 128], F32)
                xtok = sb2("xtok", [128, 2, 256], BF16)
                xd2 = sb2("xd2", [128, 2, 256], BF16)
                mt = sb2("mt", [128, 3, 128], BF16)
                wm = sb2("wm", [128, 2, 128], F32)
                y1 = sb2("y1", [128, 128], F32)
                y2 = sb2("y2", [128, 128], F32)
                trf = ps2("trf", [128, 512], F32)
                trb = ps2("trbL3", [128, 1024], BF16)
                gtp = ps2("gtp", [128, 512], F32)
                wtp = ps2("wtp", [128, 256], F32)
                ydp = ps2("ydp", [128, 128], F32)
                yop = ps2("yop", [128, 128], F32)

                @blk.gpsimd
                def _(g):
                    C["dse"].inc(g.memset(zerof[:, :], 0.0))
                dse_z = C["dse"].n

                @blk.vector
                def _(v):
                    v.wait_ge(C["dse"].sem, dse_z)

                @blk.sync
                def _(sy):
                    C["dxb"].inc(sy.dma_start(
                        out=xb_sb[:, :, :],
                        in_=xbca.rearrange("(a p) t -> p a t", p=128)), 16)

                dxb_n = C["dxb"].n
                gte_base = C["gte"].n
                acE_base = C["acE"].n

                for c in range(nch):
                    acT0 = C["acT"].n

                    @blk.tensor
                    def _(t, c=c, acT0=acT0):
                        if c >= 1:
                            t.wait_ge(C["acE"].sem, acE_base + 2 * c)
                        for half in range(2):
                            t.transpose(trf[:, half * 128:(half + 1) * 128],
                                        acs_sb[:, c * 256 + half * 128:
                                               c * 256 + (half + 1) * 128],
                                        identf[:, :])
                        for half in range(2):
                            inst = t.transpose(
                                trf[:, 256 + half * 128:256 + (half + 1) * 128],
                                dts[:, c * 256 + half * 128:
                                    c * 256 + (half + 1) * 128],
                                identf[:, :])
                        C["acT"].inc(inst)

                    acT_n = C["acT"].n

                    @blk.vector
                    def _(v, c=c, acT_n=acT_n):
                        v.wait_ge(C["acT"].sem, acT_n)
                        v.tensor_copy(
                            acstok[:, c, :, :].rearrange("p a f -> p (a f)"),
                            trf[:, 0:256])
                        v.drain(fusable=False)
                        v.tensor_copy(
                            dtttok[:, c, :, :].rearrange("p a f -> p (a f)"),
                            trf[:, 256:512])
                        C["acE"].inc(v.drain(fusable=False))
                        v.tensor_scalar(
                            negtok[:, c, :, :].rearrange("p a f -> p (a f)"),
                            acstok[:, c, :, :].rearrange("p a f -> p (a f)"),
                            -1.0, None, ALU.mult)
                        C["acE"].inc(v.drain(fusable=False))

                    acE_n = C["acE"].n

                    @blk.scalar
                    def _(s, c=c, acE_n=acE_n):
                        s.wait_ge(C["acE"].sem, acE_n)
                        C["sce"].inc(s.activation(
                            eactok[:, c, :, :].rearrange("p a f -> p (a f)"),
                            acstok[:, c, :, :].rearrange("p a f -> p (a f)"),
                            AF.Exp))

                    gte_n = C["gte"].n

                    @blk.tensor
                    def _(t, c=c, gte_n=gte_n, dxb_n=dxb_n):
                        t.wait_ge(C["dxb"].sem, dxb_n)
                        if c >= 1:
                            t.wait_ge(C["gte"].sem, gte_n)
                        t.matmul(gtp[:, 0:256],
                                 xb_sb[:, 32, c * 256:c * 256 + 128],
                                 xb_sb[:, 33, c * 256:(c + 1) * 256],
                                 start=True, stop=True)
                        C["gtm"].inc(t.matmul(
                            gtp[:, 256:384],
                            xb_sb[:, 32, c * 256 + 128:(c + 1) * 256],
                            xb_sb[:, 33, c * 256 + 128:(c + 1) * 256],
                            start=True, stop=True))

                    @blk.vector
                    def _(v, c=c):
                        v.wait_ge(C["gtm"].sem, C["gtm"].n)
                        C["gte"].inc(v.tensor_copy(gt_sb[:, c, :], gtp[:, 0:384]))

                for r in range(16):
                    for c in range(nch):
                        xte0 = C["xte"].n
                        ydm0 = C["ydm"].n
                        ysv0 = C["ysv"].n

                        @blk.tensor
                        def _(t, r=r, c=c, xte0=xte0):
                            if r + c >= 1:
                                t.wait_ge(C["xte"].sem, xte0)
                            for half in range(2):
                                for q in range(2):
                                    inst = t.transpose(
                                        trb[:, (half * 2 + q) * 128:
                                            (half * 2 + q + 1) * 128],
                                        xb_sb[:, r * 2 + q,
                                              c * 256 + half * 128:
                                              c * 256 + (half + 1) * 128],
                                        ident[:, :])
                            C["xtr"].inc(inst)

                        @blk.vector
                        def _(v, r=r, c=c, ydm0=ydm0, ysv0=ysv0):
                            v.wait_ge(C["xtr"].sem, C["xtr"].n)
                            if ydm0 >= 1:
                                v.wait_ge(C["ydm"].sem, ydm0)
                            v.tensor_copy(
                                xtok[:, :, :].rearrange("p a f -> p (a f)"),
                                trb[:, 0:512])
                            C["xte"].inc(v.drain(fusable=False))
                            for half in range(2):
                                for hq in range(4):
                                    hh = r * 4 + hq
                                    v.tensor_scalar(
                                        xd2[:, half, hq * 64:(hq + 1) * 64],
                                        xtok[:, half, hq * 64:(hq + 1) * 64],
                                        dtttok[:, c, half, hh:hh + 1],
                                        None, ALU.mult)
                            C["xte"].inc(v.drain(fusable=False))

                        for hq in range(4):
                            hh = r * 4 + hq
                            mts_b = C["mts"].n
                            ydm_b = C["ydm"].n

                            @blk.vector
                            def _(v, c=c, hh=hh, mts_b=mts_b):
                                if mts_b >= 1:
                                    v.wait_ge(C["mts"].sem, mts_b)
                                for lh in range(2):
                                    v.tensor_scalar(
                                        wm[:, lh, :], zerof[:, :],
                                        acstok[:, c, lh, hh:hh + 1],
                                        None, ALU.add)
                                C["mtv"].inc(v.drain(fusable=False))

                            mtv_n = C["mtv"].n

                            @blk.tensor
                            def _(t, mtv_n=mtv_n, mts_b=mts_b):
                                t.wait_ge(C["mtv"].sem, mtv_n)
                                if mts_b >= 1:
                                    t.wait_ge(C["mts"].sem, mts_b)
                                for lh in range(2):
                                    inst = t.transpose(
                                        wtp[:, lh * 128:(lh + 1) * 128],
                                        wm[:, lh, :], identf[:, :])
                                C["xtr"].inc(inst)

                            xtr_n = C["xtr"].n

                            @blk.vector
                            def _(v, xtr_n=xtr_n):
                                v.wait_ge(C["xtr"].sem, xtr_n)
                                v.tensor_add(wm[:, 0, :], wtp[:, 0:128],
                                             msktri[:, :])
                                v.drain(fusable=False)
                                v.tensor_add(wm[:, 1, :], wtp[:, 128:256],
                                             msktri[:, :])
                                C["mtv"].inc(v.drain(fusable=False))

                            mtv_n2 = C["mtv"].n

                            @blk.scalar
                            def _(s, c=c, hh=hh, mtv_n2=mtv_n2, ydm_b=ydm_b):
                                s.wait_ge(C["mtv"].sem, mtv_n2)
                                if ydm_b >= 1:
                                    s.wait_ge(C["ydm"].sem, ydm_b)
                                s.activation(mt[:, 0, :], wm[:, 0, :], AF.Exp,
                                             bias=negtok[:, c, 0, hh:hh + 1])
                                s.drain(fusable=False)
                                s.activation(mt[:, 2, :], wm[:, 1, :], AF.Exp,
                                             bias=negtok[:, c, 1, hh:hh + 1])
                                s.drain(fusable=False)
                                s.activation(mt[:, 1, :], wtp[:, 128:256], AF.Exp,
                                             bias=negtok[:, c, 0, hh:hh + 1])
                                C["mts"].inc(s.drain(fusable=False))

                            mts_n = C["mts"].n

                            @blk.vector
                            def _(v, c=c, mts_n=mts_n):
                                v.wait_ge(C["mts"].sem, mts_n)
                                v.tensor_mul(mt[:, 0, :], mt[:, 0, :],
                                             gt_sb[:, c, 0:128])
                                v.drain(fusable=False)
                                v.tensor_mul(mt[:, 1, :], mt[:, 1, :],
                                             gt_sb[:, c, 128:256])
                                v.drain(fusable=False)
                                v.tensor_mul(mt[:, 2, :], mt[:, 2, :],
                                             gt_sb[:, c, 256:384])
                                C["mtv"].inc(v.drain(fusable=False))

                            mtv_n3 = C["mtv"].n
                            yev_b = C["yev"].n
                            xte_n = C["xte"].n

                            @blk.tensor
                            def _(t, c=c, hh=hh, hq=hq, mtv_n3=mtv_n3,
                                  yev_b=yev_b, xte_n=xte_n):
                                t.wait_ge(C["mtv"].sem, mtv_n3)
                                t.wait_ge(C["xte"].sem, xte_n)
                                if yev_b >= 1:
                                    t.wait_ge(C["yev"].sem, yev_b)
                                t.matmul(ydp[:, 0:64], mt[:, 0, :],
                                         xd2[:, 0, hq * 64:(hq + 1) * 64],
                                         start=True, stop=True)
                                t.matmul(ydp[:, 64:128], mt[:, 1, :],
                                         xd2[:, 0, hq * 64:(hq + 1) * 64],
                                         start=True, stop=False)
                                t.matmul(ydp[:, 64:128], mt[:, 2, :],
                                         xd2[:, 1, hq * 64:(hq + 1) * 64],
                                         start=False, stop=True)
                                t.matmul(yop[:, 0:64],
                                         xb_sb[:, 33, c * 256:c * 256 + 128],
                                         Sc_bf[:, c, hh * 64:(hh + 1) * 64],
                                         start=True, stop=True)
                                inst = t.matmul(
                                    yop[:, 64:128],
                                    xb_sb[:, 33, c * 256 + 128:(c + 1) * 256],
                                    Sc_bf[:, c, hh * 64:(hh + 1) * 64],
                                    start=True, stop=True)
                                C["ydm"].inc(inst)

                            ydm_n = C["ydm"].n
                            yev_b2 = C["yev"].n

                            @blk.scalar
                            def _(s, ydm_n=ydm_n, yev_b2=yev_b2):
                                s.wait_ge(C["ydm"].sem, ydm_n)
                                if yev_b2 >= 1:
                                    s.wait_ge(C["yev"].sem, yev_b2)
                                C["sce"].inc(s.copy(y1[:, :], ydp[:, 0:128]))

                            sce_n2 = C["sce"].n

                            @blk.vector
                            def _(v, r=r, c=c, hh=hh, hq=hq, sce_n2=sce_n2):
                                v.wait_ge(C["sce"].sem, sce_n2)
                                for lh in range(2):
                                    v.scalar_tensor_tensor(
                                        y2[:, lh * 64:(lh + 1) * 64],
                                        yop[:, lh * 64:(lh + 1) * 64],
                                        eactok[:, c, lh, hh:hh + 1],
                                        y1[:, lh * 64:(lh + 1) * 64],
                                        ALU.mult, ALU.add)
                                C["yev"].inc(v.drain(fusable=False))
                                for lh in range(2):
                                    v.scalar_tensor_tensor(
                                        y_sb[:, c * 2 + lh, hh * 64:(hh + 1) * 64],
                                        xtok[:, lh, hq * 64:(hq + 1) * 64],
                                        Wb[:, hh * NB + NB - 1:hh * NB + NB],
                                        y2[:, lh * 64:(lh + 1) * 64],
                                        ALU.mult, ALU.add)
                                C["ysv"].inc(v.drain(fusable=False))

            phase_barrier(blk, C)

            with ExitStack() as p3:
                sb3 = lambda n, s, d: p3.enter_context(nc.sbuf_tensor(n, s, d))
                ps3 = lambda n, s, d: p3.enter_context(nc.psum_tensor(n, s, d))
                sqscr = sb3("sqscr", [128, D_INNER], BF16)
                wo2 = sb3("wo2", [128, 32, 512], BF16)
                ztok = sb3("ztok", [128, 2, D_INNER], BF16)
                nsc = sb3("nsc", [128, 32], F32)
                ynT = sb3("ynT", [128, 32, 128], BF16)
                xstg = sb3("xstg", [128, 2, 512], F32)
                trb3 = ps3("trb3", [128, 2048], BF16)
                wop2 = ps3("wop2", [128, 512], F32)

                @blk.gpsimd
                def _(g):
                    C["dgb"].inc(g.dma_start(
                        out=nsc[:, :],
                        in_=nsc_in.rearrange("(a p) o -> p (a o)", p=128),
                        allow_slow_non_contiguous=True), 16)

                for st in range(n_tiles):
                    zld_ch = "zld%d" % (st % 2)
                    gat_b = C["glu"].n

                    @blk.sync
                    def _(sy, st=st, gat_b=gat_b):
                        if st >= 2:
                            sy.wait_ge(C["gat"].sem, st - 1)
                        C[zld_ch].inc(sy.dma_start(
                            out=ztok[:, st % 2, :],
                            in_=zt_in[:, st * 128:(st + 1) * 128],
                            transpose=True), 16)

                    zld_n = C[zld_ch].n

                    @blk.scalar
                    def _(s, st=st, zld_n=zld_n):
                        s.wait_ge(C[zld_ch].sem, zld_n)
                        C["glu"].inc(s.activation(ztok[:, st % 2, :],
                                                  ztok[:, st % 2, :], AF.Silu))

                    glu_n = C["glu"].n
                    ysv_all = C["ysv"].n

                    @blk.vector
                    def _(v, st=st, glu_n=glu_n, ysv_all=ysv_all):
                        v.wait_ge(C["glu"].sem, glu_n)
                        v.wait_ge(C["ysv"].sem, ysv_all)
                        C["gat"].inc(v.tensor_mul(y_sb[:, st, :], y_sb[:, st, :],
                                                  ztok[:, st % 2, :]))

                    gat_n2 = C["gat"].n

                    @blk.scalar
                    def _(s, st=st, gat_n2=gat_n2):
                        s.wait_ge(C["gat"].sem, gat_n2)
                        C["rms"].inc(s.activation(sqscr[:, :], y_sb[:, st, :],
                                                  AF.Square,
                                                  accum_out=stats[:, 1:2]))

                    rms_n = C["rms"].n

                    @blk.vector
                    def _(v, rms_n=rms_n):
                        v.wait_ge(C["rms"].sem, rms_n)
                        C["nv"].inc(v.tensor_scalar(stats[:, 5:6], stats[:, 1:2],
                                                    -1.0 / D_INNER, None,
                                                    ALU.mult))

                    nv_n = C["nv"].n

                    @blk.scalar
                    def _(s, nv_n=nv_n):
                        s.wait_ge(C["nv"].sem, nv_n)
                        s.activation(stats[:, 6:7], stats[:, 5:6], AF.Ln,
                                     bias=cst[:, 0:1], scale=-1.0)
                        s.drain(fusable=False)
                        C["rs"].inc(s.activation(stats[:, 7:8], stats[:, 6:7],
                                                 AF.Exp, scale=-0.5))

                    rs_n = C["rs"].n

                    @blk.vector
                    def _(v, st=st, rs_n=rs_n):
                        v.wait_ge(C["rs"].sem, rs_n)
                        C["ysc"].inc(v.tensor_scalar(y_sb[:, st, :],
                                                     y_sb[:, st, :],
                                                     stats[:, 7:8], None,
                                                     ALU.mult))

                for qt in range(4):
                    wom_b = C["wom"].n

                    @blk.gpsimd
                    def _(g, qt=qt, wom_b=wom_b):
                        if qt >= 1:
                            g.wait_ge(C["wom"].sem, wom_b)
                        C["dwo%d" % (qt % 2)].inc(g.dma_start(
                            out=wo2[:, :, :],
                            in_=wout_in[:, qt * 512:(qt + 1) * 512].rearrange(
                                "(c p) f -> p c f", p=128)), 16)

                    dwo_n = C["dwo%d" % (qt % 2)].n

                    for st in range(n_tiles):
                        ysc_all = C["ysc"].n
                        yne_b = C["yne"].n

                        yne_pre = C["yne"].n

                        @blk.tensor
                        def _(t, st=st, ysc_all=ysc_all, yne_pre=yne_pre):
                            t.wait_ge(C["ysc"].sem, ysc_all)
                            for grp in range(4):
                                gslot = (grp % 2) * 1024
                                need = yne_pre + grp - 1
                                if need >= 1:
                                    t.wait_ge(C["yne"].sem, need)
                                for j in range(8):
                                    chk = grp * 8 + j
                                    inst = t.transpose(
                                        trb3[:, gslot + j * 128:gslot + (j + 1) * 128],
                                        y_sb[:, st, chk * 128:(chk + 1) * 128],
                                        ident[:, :])
                                C["ynt"].inc(inst)

                        base_ynt = C["ynt"].n - 4

                        @blk.vector
                        def _(v, st=st, base_ynt=base_ynt):
                            v.wait_ge(C["dgb"].sem, C["dgb"].n)
                            for grp in range(4):
                                gslot = (grp % 2) * 1024
                                v.wait_ge(C["ynt"].sem, base_ynt + grp + 1)
                                for j in range(8):
                                    chk = grp * 8 + j
                                    v.tensor_scalar(
                                        ynT[:, chk, :],
                                        trb3[:, gslot + j * 128:gslot + (j + 1) * 128],
                                        nsc[:, chk:chk + 1], None, ALU.mult)
                                C["yne"].inc(v.drain(fusable=False))

                        yne_n = C["yne"].n
                        woe_b = C["woe"].n

                        @blk.tensor
                        def _(t, st=st, qt=qt, yne_n=yne_n, dwo_n=dwo_n,
                              woe_b=woe_b):
                            t.wait_ge(C["yne"].sem, yne_n)
                            t.wait_ge(C["dwo%d" % (qt % 2)].sem, dwo_n)
                            t.wait_ge(C["dgb"].sem, C["dgb"].n)
                            if woe_b >= 1:
                                t.wait_ge(C["woe"].sem, woe_b)
                            for chk in range(32):
                                inst = t.matmul(wop2[:, :], ynT[:, chk, :],
                                                wo2[:, chk, :],
                                                start=(chk == 0), stop=(chk == 31))
                            C["wom"].inc(inst)

                        wom_n = C["wom"].n
                        woe_pre = C["woe"].n
                        stx0_b = C["stx0"].n
                        stx1_b = C["stx1"].n

                        @blk.sync
                        def _(sy, st=st, qt=qt, woe_pre=woe_pre):
                            sy.wait_ge(C["woe"].sem, woe_pre)
                            C["dx2"].inc(sy.dma_start(
                                out=xstg[:, 0, :],
                                in_=x2_in[st * 128:(st + 1) * 128,
                                          qt * 512:(qt + 1) * 512]), 16)

                        dx2_n = C["dx2"].n

                        @blk.vector
                        def _(v, wom_n=wom_n, dx2_n=dx2_n, stx0_b=stx0_b,
                              stx1_b=stx1_b):
                            v.wait_ge(C["wom"].sem, wom_n)
                            v.wait_ge(C["dx2"].sem, dx2_n)
                            v.wait_ge(C["stx0"].sem, stx0_b)
                            v.wait_ge(C["stx1"].sem, stx1_b)
                            C["woe"].inc(v.tensor_add(xstg[:, 1, :],
                                                      xstg[:, 0, :], wop2[:, :]))

                        woe_n = C["woe"].n

                        @blk.sync
                        def _(sy, st=st, qt=qt, woe_n=woe_n):
                            sy.wait_ge(C["woe"].sem, woe_n)
                            C["stx%d" % (st % 2)].inc(sy.dma_start(
                                out=x3scr.ap()[st * 128:(st + 1) * 128,
                                               qt * 512:(qt + 1) * 512],
                                in_=xstg[:, 1, :]), 16)

        phase_barrier(blk, C)

        with ExitStack() as p4:
            sb4 = lambda n, s, d: p4.enter_context(nc.sbuf_tensor(n, s, d))
            ps4 = lambda n, s, d: p4.enter_context(nc.psum_tensor(n, s, d))
            x3_sb = sb4("x3_sb", [128, n_tiles, H], F32)
            xn = sb4("xnL3", [128, H], BF16)
            xnT = sb4("xnTL3", [128, n_tiles, H], BF16)
            wn = sb4("wnL3", [128, n_tiles, n_exp], F32)
            rt_sb = sb4("rtL3", [128, 16, n_exp], BF16)
            sqscr2 = sb4("sqscr2", [128, H], BF16)

            @blk.gpsimd
            def _(g):
                C["drt"].inc(g.dma_start(
                    out=rt_sb[:, :, :],
                    in_=routerT.rearrange("(c p) f -> p c f", p=128)), 16)

            with ExitStack() as pln:
                trp4 = pln.enter_context(nc.psum_tensor("trp4", [128, 2048], BF16))
                for tt in range(n_tiles):
                    @blk.sync
                    def _(sy, tt=tt):
                        if tt >= 2:
                            sy.wait_ge(C["xn"].sem, C["xn"].n - 1)
                        C["dx3%d" % (tt % 2)].inc(sy.dma_start(
                            out=x3_sb[:, tt, :],
                            in_=x3scr.ap()[tt * 128:(tt + 1) * 128, :]), 16)

                    if tt >= 1:
                        @blk.vector
                        def _(v, tt=tt):
                            v.wait_ge(C["fat"].sem, tt)

                    emit_ln_tile(blk, C, x3_sb[:, tt, :], xn[:, :], sqscr2,
                                 stats, cst[:, 0:1], "dx3%d" % (tt % 2))
                    emit_featmajor(blk, C, nc, xn, xnT[:, tt, :], trp4,
                                   ident[:, :], "fa")

            phase_barrier(blk, C)

            with ExitStack() as p5:
                rpsum = p5.enter_context(nc.psum_tensor("rpsumL3", [128, max(n_exp, 32)], F32))
                probs = p5.enter_context(nc.sbuf_tensor("probsL3", [128, n_exp], F32))
                top8 = p5.enter_context(nc.sbuf_tensor("top8L3", [128, 8], F32))
                emit_router(blk, C, nc, xnT, wn, rt_sb, rpsum, probs, top8,
                            stats, n_tiles)

            phase_barrier(blk, C)

            with ExitStack() as p6:
                sb6 = lambda n, s, d: p6.enter_context(nc.sbuf_tensor(n, s, d))
                ps6 = lambda n, s, d: p6.enter_context(nc.psum_tensor(n, s, d))
                slots = [sb6("slotL3%d" % i, [128, 12288], BF16) for i in range(3)]
                gact = sb6("gactL3", [128, n_tiles, 768], BF16)
                htmp = sb6("htmpL3", [128, 2, 768], BF16)
                utmp = sb6("utmpL3", [128, 768], BF16)
                gup = ps6("gupL3", [128, 1024], F32)
                htp = ps6("htpL3", [128, 1024], BF16)
                dpsum = ps6("dpsumL3", [128, 2048], F32)
                emit_ffn(blk, C, nc, x3_sb, xnT, wn, slots, gact, gact, htmp,
                         utmp, gup, htp, dpsum, ident[:, :], gw_ap, uw_ap,
                         dw_ap, n_tiles, n_exp, ["fl0", "fl1", "fl2"], "f")

            for tt in range(n_tiles):
                @blk.sync
                def _(sy, tt=tt):
                    sy.wait_ge(C["fde"].sem, C["fde"].n)
                    C["sto"].inc(sy.dma_start(
                        out=out[tt * 128:(tt + 1) * 128, :],
                        in_=x3_sb[:, tt, :]), 16)

        @blk.sync
        def _(sy):
            sy.wait_ge(C["sto"].sem, C["sto"].n)

    return nc


_NCS = {}


def _get(name, builder):
    if name not in _NCS:
        _NCS[name] = builder()
    return _NCS[name]


def kernel(L0_attn_a_hidden_states, L0_attn_a_w_qkv, L0_attn_a_w_o,
           L0_attn_b_W_in, L0_attn_b_conv_w, L0_attn_b_conv_b,
           L0_attn_b_dt_bias, L0_attn_b_A_log, L0_attn_b_D,
           L0_attn_b_norm_scale, L0_attn_b_W_out, L0_ffn_router_w,
           L0_ffn_gate_w, L0_ffn_up_w, L0_ffn_down_w):
    A = np.ascontiguousarray
    x = A(np.asarray(L0_attn_a_hidden_states, np.float32).reshape(S, H))
    w_qkv = np.asarray(L0_attn_a_w_qkv, np.float32)
    cores = list(range(N_CORES))
    NTOK = S // N_CORES

    nc1 = _get("l1", build_l1)
    wT = A(w_qkv.T)
    maskc = _causal_mask_const()
    in1 = []
    for c in cores:
        cols = [wT[:, p * H + c * 256: p * H + c * 256 + 256] for p in range(3)]
        in1.append({"xin": x, "wqkvT": A(np.concatenate(cols, axis=1)),
                    "maskc": maskc})
    r1 = run_bass_kernel_spmd(nc1, in1, core_ids=cores)
    ao = np.concatenate([r1.results[c]["attn_out"] for c in cores], axis=1)

    nc2a = _get("l2a", build_l2a)
    woT = A(np.asarray(L0_attn_a_w_o, np.float32).T)
    routerT = A(np.asarray(L0_ffn_router_w, np.float32).T)
    gwT = A(np.asarray(L0_ffn_gate_w, np.float32).transpose(0, 2, 1))
    uwT = A(np.asarray(L0_ffn_up_w, np.float32).transpose(0, 2, 1))
    dwT = A(np.asarray(L0_ffn_down_w, np.float32).transpose(0, 2, 1))
    win = np.zeros((H, 8576), np.float32)
    win[:, 0:D_IN_PROJ] = np.asarray(L0_attn_b_W_in, np.float32)
    dtb = A(np.asarray(L0_attn_b_dt_bias, np.float32).reshape(64, 1))
    alog = A(np.asarray(L0_attn_b_A_log, np.float32).reshape(64, 1))
    in2 = []
    for c in cores:
        sl = slice(c * NTOK, (c + 1) * NTOK)
        in2.append({"xin": A(x[sl]), "aoT": A(ao[sl].T), "woT": woT,
                    "routerT": routerT, "gwT": gwT, "uwT": uwT, "dwT": dwT,
                    "win": win, "dtb": dtb, "alog": alog})
    r2 = run_bass_kernel_spmd(nc2a, in2, core_ids=cores)

    nc2b = _get("l2b", build_l2b)
    convw = A(np.asarray(L0_attn_b_conv_w, np.float32).T)
    convb = A(np.asarray(L0_attn_b_conv_b, np.float32).reshape(CONV_DIM, 1))
    triu = np.triu(np.ones((256, 256), np.float32))
    in3 = []
    for c in cores:
        xb = r2.results[c]["xbct"]
        if c == 0:
            halo = np.zeros((CONV_DIM, 3), np.float32)
        else:
            halo = r2.results[c - 1]["xbct"][:, NTOK - 3:NTOK]
        in3.append({"xbce": A(np.concatenate([halo, xb], axis=1)),
                    "dtt": r2.results[c]["dtt"], "dat": r2.results[c]["dat"],
                    "convw": convw, "convb": convb, "triu": triu})
    r3 = run_bass_kernel_spmd(nc2b, in3, core_ids=cores)

    nc3 = _get("l3", build_l3)
    dvec = A(np.asarray(L0_attn_b_D, np.float32).reshape(64, 1))
    nscale = A(np.asarray(L0_attn_b_norm_scale, np.float32).reshape(D_INNER, 1))
    wout = A(np.asarray(L0_attn_b_W_out, np.float32))
    in4 = []
    for c in cores:
        pstates = np.zeros((7, 128, 64, 64), np.float32)
        ptots = np.zeros((64, 7), np.float32)
        for i in range(c):
            pstates[i] = r3.results[c - 1 - i]["sstate"]
            ptots[:, i] = r3.results[c - 1 - i]["stot"][:, 0]
        in4.append({"x2": r2.results[c]["x2"], "xbca": r3.results[c]["xbca"],
                    "zt": r2.results[c]["zt"], "dtt": r2.results[c]["dtt"],
                    "acs": r3.results[c]["acs"],
                    "cstates": r3.results[c]["cstates"],
                    "pstates": pstates, "ptots": ptots, "dvec": dvec,
                    "nscale": nscale, "wout": wout, "routerT": routerT,
                    "gwT": gwT, "uwT": uwT, "dwT": dwT})
    r4 = run_bass_kernel_spmd(nc3, in4, core_ids=cores)

    outp = np.concatenate([r4.results[c]["out"] for c in cores], axis=0)
    return outp.reshape(1, S, H).astype(np.float32)


# version 09 (final)
# speedup vs baseline: 1.2351x; 1.2080x over previous
from contextlib import ExitStack

import numpy as np
import concourse.bass as bass
import concourse.mybir as mybir
from concourse.bass_utils import run_bass_kernel_spmd
from concourse.bass_core_rust import IndirectOffsetOnAxis

F32 = mybir.dt.float32
BF16 = mybir.dt.bfloat16
I32 = mybir.dt.int32
BIGF = 1.0e8
AF = mybir.ActivationFunctionType
ALU = mybir.AluOpType
AX = mybir.AxisListType.X

H = 2048
S = 8192
N_CORES = 8
SCALE_A = float(1.0 / np.sqrt(128.0))
EPS = 1e-5
D_INNER = 4096
D_STATE = 128
HEADDIM = 64
NHEADS_M = 64
D_CONV = 4
CHUNK = 256
CONV_DIM = D_INNER + 2 * D_STATE
D_IN_PROJ = 2 * D_INNER + 2 * D_STATE + NHEADS_M


class Ch:

    def __init__(self, ctx, nc, name):
        self.sem = ctx.enter_context(nc.semaphore(name))
        self.n = 0

    def inc(self, inst, k=1):
        inst.then_inc(self.sem, k)
        self.n += k
        return inst


def emit_ln_tile(blk, C, xt_ap, xn_ap, sqscr, stats, eps_ap, dxch):
    dxc = C[dxch]
    dxn = dxc.n
    nseg = xt_ap.shape[-1] // 512

    @blk.vector
    def _(v):
        v.wait_ge(dxc.sem, dxn)
        for i in range(nseg):
            v.bn_stats(stats[:, 6 * i:6 * (i + 1)],
                       xt_ap[:, 512 * i:512 * (i + 1)])
        v.drain(fusable=False)
        C["nv"].inc(v.bn_aggr(stats[:, 24:26], stats[:, 0:6 * nseg]))

    nvn = C["nv"].n

    @blk.scalar
    def _(s):
        s.wait_ge(C["nv"].sem, nvn)
        s.activation(stats[:, 27:28], stats[:, 25:26], AF.Ln, bias=eps_ap)
        s.drain(fusable=False)
        C["rs"].inc(s.activation(stats[:, 26:27], stats[:, 27:28], AF.Exp,
                                 scale=-0.5))

    rsn = C["rs"].n

    @blk.vector
    def _(v):
        v.wait_ge(C["rs"].sem, rsn)
        C["xn"].inc(v.tensor_scalar(xn_ap, xt_ap, stats[:, 24:25],
                                    stats[:, 26:27], ALU.subtract, ALU.mult))


def build_l1(seq=S):
    nc = bass.Bass("TRN2", target_bir_lowering=False)
    NT = seq // 128

    xin = nc.dram_tensor("xin", [seq, H], F32, kind="ExternalInput").ap()
    wqkvT = nc.dram_tensor("wqkvT", [H, 768], F32, kind="ExternalInput").ap()
    maskc = nc.dram_tensor("maskc", [4, 128, 512], F32, kind="ExternalInput").ap()
    aout = nc.dram_tensor("attn_out", [seq, 256], F32, kind="ExternalOutput").ap()

    with ExitStack() as ctx:
        sb = lambda n, s, d: ctx.enter_context(nc.sbuf_tensor(n, s, d))

        cst = sb("cst", [128, 2], F32)
        ident = sb("ident", [128, 128], BF16)
        qT = sb("qT", [128, 2, seq], BF16)
        kT = sb("kT", [128, 2, seq], BF16)
        vsb = sb("vsb", [128, NT, 2, 128], BF16)
        mask = sb("mask", [128, 4, 512], F32)

        names = ["dx0", "dx1", "dwg", "dwm", "dwe", "sq", "nv", "rs", "xn", "tr", "xnT", "qkv", "mxg",
                 "qke", "ve", "vtr", "vfin", "qkm", "sc", "scv", "mx",
                 "exp", "rcp", "ptr", "pte", "pvg", "pv", "ao", "st"]
        C = {n: Ch(ctx, nc, "c_" + n) for n in names}

        blk = ctx.enter_context(nc.Block())

        @blk.gpsimd
        def _(g):
            C["dwe"].inc(g.memset(ident[:, :], 0.0))
            C["dwe"].inc(g.memset(cst[:, 0:1], EPS))
            g.wait_ge(C["dwe"].sem, C["dwe"].n)
            C["dwe"].inc(g.affine_select(
                out=ident[:, :], in_=ident[:, :],
                compare_op=ALU.not_equal, fill=1.0, base=0,
                pattern=[[-1, 128]], channel_multiplier=1))

        @blk.sync
        def _(sy):
            C["dwm"].inc(sy.dma_start(out=mask[:, :, :],
                                      in_=maskc.rearrange("m p f -> p m f")), 16)

        def wait_preamble(e):
            e.wait_ge(C["dwg"].sem, C["dwg"].n)
            e.wait_ge(C["dwm"].sem, C["dwm"].n)
            e.wait_ge(C["dwe"].sem, C["dwe"].n)

        @blk.scalar
        def _(s):
            wait_preamble(s)

        @blk.vector
        def _(v):
            wait_preamble(v)

        with ExitStack() as actx:
            sba = lambda n, s, d: actx.enter_context(nc.sbuf_tensor(n, s, d))
            psa = lambda n, s, d: actx.enter_context(nc.psum_tensor(n, s, d))
            wq = sba("wq", [128, 16, 768], BF16)
            xt = sba("xt", [128, 2, H], F32)

            @blk.gpsimd
            def _(g):
                C["dwg"].inc(g.dma_start(
                    out=wq[:, :, :],
                    in_=wqkvT.rearrange("(c p) f -> p c f", p=128)), 16)

            sqscr = sba("sqscr", [128, H], BF16)
            xn = sba("xn", [128, 2, H], BF16)
            xnT = sba("xnT", [128, 2, H], BF16)
            stats = sba("stats", [128, 32], F32)
            vt = sba("vt", [128, 256], BF16)
            trp = psa("trp", [128, 2048], BF16)
            qkp = psa("qkp", [128, 512], F32)
            vp = psa("vp", [128, 256], F32)
            vtp = psa("vtp", [128, 256], BF16)

            for tt in range(NT):
                @blk.sync
                def _(sy, tt=tt):
                    if tt >= 2:
                        sy.wait_ge(C["xn"].sem, tt - 1)
                    C["dx%d" % (tt % 2)].inc(sy.dma_start(
                        out=xt[:, tt % 2, :],
                        in_=xin[tt * 128:(tt + 1) * 128, :]), 16)

                if tt >= 2:
                    @blk.vector
                    def _(v, tt=tt):
                        v.wait_ge(C["tr"].sem, tt - 1)

                emit_ln_tile(blk, C, xt[:, tt % 2, :], xn[:, tt % 2, :],
                             sqscr, stats, cst[:, 0:1], "dx%d" % (tt % 2))

                @blk.tensor
                def _(t, tt=tt):
                    t.wait_ge(C["xn"].sem, C["xn"].n)
                    if tt == 0:
                        wait_preamble(t)
                    if tt >= 2:
                        t.wait_ge(C["xnT"].sem, tt - 1)
                    for chk in range(16):
                        inst = t.transpose(
                            trp[:, chk * 128:(chk + 1) * 128],
                            xn[:, tt % 2, chk * 128:(chk + 1) * 128],
                            ident[:, :])
                    C["tr"].inc(inst)

                @blk.vector
                def _(v, tt=tt):
                    v.wait_ge(C["tr"].sem, C["tr"].n)
                    v.tensor_copy(xnT[:, tt % 2, 0:1024], trp[:, 0:1024])
                    C["xnT"].inc(v.tensor_copy(xnT[:, tt % 2, 1024:2048],
                                               trp[:, 1024:2048]))

                @blk.tensor
                def _(t, tt=tt):
                    t.wait_ge(C["xnT"].sem, C["xnT"].n)
                    if tt >= 1:
                        t.wait_ge(C["qke"].sem, 2 * tt)
                        t.wait_ge(C["ve"].sem, tt)
                    for ft in range(6):
                        dst = (qkp[:, ft * 128:(ft + 1) * 128] if ft < 4
                               else vp[:, (ft - 4) * 128:(ft - 3) * 128])
                        for chk in range(16):
                            inst = t.matmul(
                                dst, wq[:, chk, ft * 128:(ft + 1) * 128],
                                xnT[:, tt % 2, chk * 128:(chk + 1) * 128],
                                start=(chk == 0), stop=(chk == 15))
                    C["qkv"].inc(inst)

                @blk.scalar
                def _(s, tt=tt):
                    s.wait_ge(C["qkv"].sem, C["qkv"].n)
                    for hh in range(2):
                        s.mul(qT[:, hh, tt * 128:(tt + 1) * 128],
                              qkp[:, hh * 128:(hh + 1) * 128], SCALE_A)
                        C["qke"].inc(s.copy(
                            kT[:, hh, tt * 128:(tt + 1) * 128],
                            qkp[:, (2 + hh) * 128:(3 + hh) * 128]))

                @blk.vector
                def _(v, tt=tt):
                    v.wait_ge(C["qkv"].sem, C["qkv"].n)
                    if tt >= 1:
                        v.wait_ge(C["vtr"].sem, tt)
                    C["ve"].inc(v.tensor_copy(vt[:, :], vp[:, 0:256]))

                @blk.tensor
                def _(t, tt=tt):
                    t.wait_ge(C["ve"].sem, C["ve"].n)
                    if tt >= 1:
                        t.wait_ge(C["vfin"].sem, 2 * tt)
                    t.transpose(vtp[:, 0:128], vt[:, 0:128], ident[:, :])
                    C["vtr"].inc(t.transpose(vtp[:, 128:256], vt[:, 128:256],
                                             ident[:, :]))

                @blk.vector
                def _(v, tt=tt):
                    v.wait_ge(C["vtr"].sem, C["vtr"].n)
                    v.tensor_copy(vsb[:, tt, 0, :], vtp[:, 0:128])
                    C["vfin"].inc(v.tensor_copy(vsb[:, tt, 1, :],
                                                vtp[:, 128:256]), 2)

        kdone = C["qke"].n
        vdone = C["vfin"].n

        with ExitStack() as bctx:
            sbb = lambda n, s, d: bctx.enter_context(nc.sbuf_tensor(n, s, d))
            psb = lambda n, s, d: bctx.enter_context(nc.psum_tensor(n, s, d))
            scores = sbb("scores", [128, 2, seq], F32)
            pbuf = sbb("pbuf", [128, 2, seq], BF16)
            negM = sbb("negM", [128, 2], F32)
            rsum = sbb("rsum", [128, 2], F32)
            recip = sbb("recip", [128, 2], F32)
            pt = sbb("pt", [128, 2, 2, 512], BF16)
            ao = sbb("ao", [128, 256], F32)
            scp = psb("scp", [128, 2, 2, 512], F32)
            ptp = psb("ptp", [128, 2, 1024], BF16)
            op = psb("op", [128, 2, 512], F32)

            NROWS = 2 * NT
            exp_after = [0] * NROWS
            ptr_after = [0] * NROWS
            rcp_after = [0] * NROWS
            sc_after = [0] * NROWS
            scv_after = [0] * NROWS
            pvg_after = [0] * NROWS
            rowinfo = {}

            def emit_qk(ri):
                qt, hh = ri // 2, ri % 2
                P = ri % 2
                nu = qt // 4 + 1
                base_sc = C["sc"].n
                base_mm = C["qkm"].n
                g_exp = exp_after[ri - 2] if ri >= 2 else 0
                g_sc = sc_after[ri - 2] if ri >= 2 else 0
                g_scv = scv_after[ri - 2] if ri >= 2 else 0
                g_ptr = ptr_after[ri - 2] if ri >= 2 else 0
                g_rcp = rcp_after[ri - 2] if ri >= 2 else 0

                @blk.tensor
                def _(t):
                    if ri == 0:
                        t.wait_ge(C["qke"].sem, kdone)
                    t.wait_ge(C["sc"].sem, g_sc)
                    t.wait_ge(C["scv"].sem, g_scv)
                    for u in range(nu):
                        if u >= 2:
                            t.wait_ge(C["sc"].sem, base_sc + u - 1)
                        C["qkm"].inc(t.matmul(
                            scp[:, P, u % 2, :],
                            qT[:, hh, qt * 128:(qt + 1) * 128],
                            kT[:, hh, u * 512:(u + 1) * 512],
                            start=True, stop=True))

                ev_eng = "scalar" if ri % 2 == 0 else "vector"

                def evac(e):
                    e.wait_ge(C["exp"].sem, g_exp)
                    for u in range(nu - 1):
                        e.wait_ge(C["qkm"].sem, base_mm + u + 1)
                        C["sc"].inc(e.tensor_copy(
                            scores[:, P, u * 512:(u + 1) * 512],
                            scp[:, P, u % 2, :]) if ri % 2 else e.copy(
                            scores[:, P, u * 512:(u + 1) * 512],
                            scp[:, P, u % 2, :]))
                getattr(blk, ev_eng)(evac)

                @blk.vector
                def _(v):
                    v.wait_ge(C["exp"].sem, g_exp)
                    v.wait_ge(C["qkm"].sem, base_mm + nu)
                    u = nu - 1
                    C["scv"].inc(v.tensor_tensor(
                        scores[:, P, u * 512:(u + 1) * 512],
                        scp[:, P, u % 2, :], mask[:, qt % 4, :], ALU.add))

                sc_need = C["sc"].n
                scv_need = C["scv"].n

                @blk.vector
                def _(v):
                    v.wait_ge(C["sc"].sem, sc_need)
                    v.wait_ge(C["scv"].sem, scv_need)
                    v.drain(fusable=False)
                    C["mx"].inc(v.reduce_max(negM[:, P:P + 1],
                                             scores[:, P, 0:nu * 512],
                                             axis=AX, negate=True))

                mx_n = C["mx"].n

                @blk.scalar
                def _(s):
                    s.wait_ge(C["mx"].sem, mx_n)
                    s.wait_ge(C["ptr"].sem, g_ptr)
                    s.wait_ge(C["rcp"].sem, g_rcp)
                    C["exp"].inc(s.activation(pbuf[:, P, 0:nu * 512],
                                              scores[:, P, 0:nu * 512],
                                              AF.Exp, bias=negM[:, P:P + 1],
                                              accum_out=rsum[:, P:P + 1]))
                exp_after[ri] = C["exp"].n

                @blk.vector
                def _(v):
                    v.wait_ge(C["exp"].sem, C["exp"].n)
                    v.reciprocal(recip[:, P:P + 1], rsum[:, P:P + 1])
                    C["rcp"].inc(v.drain(fusable=False))
                rcp_after[ri] = C["rcp"].n
                sc_after[ri] = C["sc"].n
                scv_after[ri] = C["scv"].n
                rowinfo[ri] = dict(exp_now=C["exp"].n)

            def emit_trpv(ri):
                qt, hh = ri // 2, ri % 2
                P = ri % 2
                ktn = qt + 1
                ngrp = (ktn + 3) // 4
                exp_now = rowinfo[ri]["exp_now"]
                base_pte = C["pte"].n
                base_pvg = C["pvg"].n
                base_ptr2 = C["ptr"].n
                ao_before = C["ao"].n
                g_pvg = pvg_after[ri - 2] if ri >= 2 else 0

                @blk.tensor
                def _(t):
                    t.wait_ge(C["exp"].sem, exp_now)
                    if ri == 0:
                        t.wait_ge(C["vfin"].sem, vdone)
                    t.wait_ge(C["ao"].sem, max(0, ao_before - 1))
                    t.wait_ge(C["pte"].sem, base_pte)
                    for g in range(ngrp):
                        k0, k1 = g * 4, min(g * 4 + 4, ktn)
                        if g >= 1:
                            t.wait_ge(C["pte"].sem, base_pte + g)
                        for kt in range(k0, k1):
                            inst = t.transpose(
                                ptp[:, P, (kt - k0) * 128:(kt - k0 + 1) * 128],
                                pbuf[:, P, kt * 128:(kt + 1) * 128],
                                ident[:, :])
                        C["ptr"].inc(inst)
                        t.wait_ge(C["pte"].sem, base_pte + g + 1)
                        for kt in range(k0, k1):
                            inst = t.matmul(
                                op[:, P, 0:128],
                                pt[:, P, g % 2,
                                   (kt - k0) * 128:(kt - k0 + 1) * 128],
                                vsb[:, kt, hh, :],
                                start=(kt == 0), stop=(kt == ktn - 1))
                        C["pvg"].inc(inst)
                ptr_after[ri] = C["ptr"].n

                @blk.vector
                def _(v):
                    v.wait_ge(C["pvg"].sem, g_pvg)
                    for g in range(ngrp):
                        k0, k1 = g * 4, min(g * 4 + 4, ktn)
                        v.wait_ge(C["ptr"].sem, base_ptr2 + g + 1)
                        if g >= 2:
                            v.wait_ge(C["pvg"].sem, base_pvg + g - 1)
                        C["pte"].inc(v.tensor_copy(
                            pt[:, P, g % 2, 0:(k1 - k0) * 128],
                            ptp[:, P, 0:(k1 - k0) * 128]))
                pvg_after[ri] = C["pvg"].n

                @blk.vector
                def _(v):
                    v.wait_ge(C["pvg"].sem, base_pvg + ngrp)
                    if qt >= 1 and hh == 0:
                        v.wait_ge(C["st"].sem, 16 * qt)
                    C["ao"].inc(v.tensor_scalar(
                        ao[:, hh * 128:(hh + 1) * 128], op[:, P, 0:128],
                        recip[:, P:P + 1], None, ALU.mult))

            for ri in range(NROWS + 1):
                if ri < NROWS:
                    emit_qk(ri)
                if ri >= 1:
                    emit_trpv(ri - 1)
                    if (ri - 1) % 2 == 1:
                        qt = (ri - 1) // 2

                        @blk.sync
                        def _(sy, qt=qt):
                            sy.wait_ge(C["ao"].sem, C["ao"].n)
                            C["st"].inc(sy.dma_start(
                                out=aout[qt * 128:(qt + 1) * 128, :],
                                in_=ao[:, :]), 16)

            @blk.sync
            def _(sy):
                sy.wait_ge(C["st"].sem, C["st"].n)

    return nc


def _causal_mask_const():
    m = np.arange(4)[:, None, None] * 128
    i = np.arange(128)[None, :, None]
    j = np.arange(512)[None, None, :]
    return np.where(j <= m + i, 0.0, -1e30).astype(np.float32)


def run_l1(x, w_qkv, seq=S):
    nc = build_l1(seq)
    wT = np.ascontiguousarray(w_qkv.T)
    maskc = _causal_mask_const()
    in_maps = []
    for c in range(N_CORES):
        cols = []
        for part in range(3):
            base = part * H + c * 256
            cols.append(wT[:, base:base + 256])
        wslice = np.ascontiguousarray(np.concatenate(cols, axis=1))
        in_maps.append({"xin": x, "wqkvT": wslice, "maskc": maskc})
    res = run_bass_kernel_spmd(nc, in_maps, core_ids=list(range(N_CORES)))
    return np.concatenate([res.results[c]["attn_out"] for c in range(N_CORES)],
                          axis=1)


N_TILES = 8


def phase_barrier(blk, C, engines=("tensor", "vector", "scalar", "gpsimd", "sync")):
    snap = [(ch.sem, ch.n) for ch in C.values() if ch.n > 0]

    def emit(e):
        for sem, n in snap:
            e.wait_ge(sem, n)

    for name in engines:
        getattr(blk, name)(emit)


def emit_featmajor(blk, C, nc, xn_ap, xnT_ap, trp, ident, key):
    xn_n = C["xn"].n
    pre = C[key + "e"].n

    @blk.tensor
    def _(t):
        t.wait_ge(C["xn"].sem, xn_n)
        if pre >= 1:
            t.wait_ge(C[key + "e"].sem, pre)
        for chk in range(16):
            inst = t.transpose(trp[:, chk * 128:(chk + 1) * 128],
                               xn_ap[:, chk * 128:(chk + 1) * 128], ident)
        C[key + "t"].inc(inst)

    trn = C[key + "t"].n

    @blk.vector
    def _(v):
        v.wait_ge(C[key + "t"].sem, trn)
        v.tensor_copy(xnT_ap[:, 0:1024], trp[:, 0:1024])
        C[key + "e"].inc(v.tensor_copy(xnT_ap[:, 1024:2048], trp[:, 1024:2048]))


def emit_router(blk, C, nc, xnT, wn, rt_sb, rpsum, probs, top8, stats, n_tiles,
                sparse=None):
    for tt in range(n_tiles):
        pre_ev = C["rev"].n

        @blk.tensor
        def _(t, tt=tt, pre_ev=pre_ev):
            if pre_ev >= 1:
                t.wait_ge(C["rev"].sem, pre_ev)
            ne = rt_sb.shape[2]
            for chk in range(16):
                inst = t.matmul(rpsum[:, 0:ne],
                                xnT[:, tt, chk * 128:(chk + 1) * 128],
                                rt_sb[:, chk, :],
                                start=(chk == 0), stop=(chk == 15))
            C["rmm"].inc(inst)

        rmm_n = C["rmm"].n

        @blk.vector
        def _(v, rmm_n=rmm_n):
            v.wait_ge(C["rmm"].sem, rmm_n)
            v.reduce_max(stats[:, 0:1], rpsum[:, 0:probs.shape[1]], axis=AX, negate=True)
            C["rmx"].inc(v.drain(fusable=False))

        rmx_n = C["rmx"].n

        @blk.scalar
        def _(s, rmx_n=rmx_n):
            s.wait_ge(C["rmx"].sem, rmx_n)
            C["rex"].inc(s.activation(probs[:, :], rpsum[:, 0:probs.shape[1]], AF.Exp,
                                      bias=stats[:, 0:1],
                                      accum_out=stats[:, 1:2]))

        rex_n = C["rex"].n

        @blk.vector
        def _(v, tt=tt, rex_n=rex_n):
            v.wait_ge(C["rex"].sem, rex_n)
            v.max(out=top8[:, :], in_=probs[:, :])
            v.drain(fusable=False)
            v.scalar_tensor_tensor(wn[:, tt, :], probs[:, :], top8[:, 3:4],
                                   probs[:, :], ALU.is_ge, ALU.mult)
            v.drain(fusable=False)
            v.reduce_sum(stats[:, 2:3], wn[:, tt, :], axis=AX)
            v.drain(fusable=False)
            v.reciprocal(stats[:, 3:4], stats[:, 2:3])
            v.drain(fusable=False)
            C["rev"].inc(v.tensor_scalar(wn[:, tt, :], wn[:, tt, :],
                                         stats[:, 3:4], None, ALU.mult))
            if sparse is not None:
                ind, w4 = sparse["ind"], sparse["w4"]
                eidx, scr = sparse["eidx"], sparse["scr"]
                ecolf = sparse["ecolf"]
                v.tensor_scalar(ind[:, tt, :], probs[:, :], top8[:, 3:4],
                                None, ALU.is_ge)
                v.tensor_scalar(w4[:, tt, :], top8[:, 0:4], stats[:, 3:4],
                                None, ALU.mult)
                for k in range(4):
                    v.tensor_scalar(scr[:, :], probs[:, :], top8[:, k:k + 1],
                                    None, ALU.is_equal)
                    v.drain(fusable=False)
                    v.tensor_mul(scr[:, :], scr[:, :], ecolf[:, :])
                    v.drain(fusable=False)
                    v.reduce_sum(eidx[:, tt, k:k + 1], scr[:, :], axis=AX)
                    inst = v.drain(fusable=False)
                C["rsx"].inc(inst)


def emit_moe_index(blk, C, nc, ind, eidx, w4, dsti, cumtok, triu_sb, identf,
                   ecolf, ecapf, scr_m, scr_b, scr_c, stats, cspsum, ctp, csT,
                   idxg_d, wgd_d, toki, cap, n_tiles, n_exp):
    ntok = n_tiles * 128
    dumpf = float(n_exp * cap)
    rsx_all = C["rsx"].n
    spm_all = C["spm"].n

    @blk.tensor
    def _(t):
        t.wait_ge(C["rsx"].sem, rsx_all)
        t.wait_ge(C["dtr"].sem, 16)
        for f0 in range(0, ntok, 512):
            f1 = min(f0 + 512, ntok)
            for tt in range(n_tiles):
                inst = t.matmul(cspsum[0:n_exp, f0:f1], ind[:, tt, :],
                                triu_sb[:, tt, f0:f1],
                                start=(tt == 0), stop=(tt == n_tiles - 1))
            C["csm"].inc(inst)

    csm_n = C["csm"].n

    @blk.vector
    def _(v):
        v.wait_ge(C["csm"].sem, csm_n)
        v.wait_ge(C["spm"].sem, spm_all)
        C["cse"].inc(v.tensor_copy(csT[0:n_exp, 0:ntok],
                                   cspsum[0:n_exp, 0:ntok]))

    cse_n = C["cse"].n
    base_ctt = C["ctt"].n
    base_cte = C["cte"].n
    for tt in range(n_tiles):
        @blk.tensor
        def _(t, tt=tt):
            if tt == 0:
                t.wait_ge(C["cse"].sem, cse_n)
                t.wait_ge(C["spm"].sem, spm_all)
            if tt >= 2:
                t.wait_ge(C["cte"].sem, base_cte + tt - 1)
            C["ctt"].inc(t.transpose(ctp[:, tt % 2, 0:128],
                                     csT[:, tt * 128:(tt + 1) * 128],
                                     identf))

        @blk.vector
        def _(v, tt=tt):
            v.wait_ge(C["ctt"].sem, base_ctt + tt + 1)
            C["cte"].inc(v.tensor_copy(cumtok[:, tt, :],
                                       ctp[:, tt % 2, 0:n_exp]))

    base_sdi = C["sdi"].n
    for tt in range(n_tiles):
        @blk.vector
        def _(v, tt=tt):
            v.wait_ge(C["cte"].sem, base_cte + tt + 1)
            ct = cumtok[:, tt, :]
            v.tensor_scalar(scr_m[:, :], ct, cap + 0.5, None, ALU.is_le)
            v.scalar_tensor_tensor(scr_b[:, :], ct, -1.0, ecapf[:, :],
                                   ALU.add, ALU.add)
            v.drain(fusable=False)
            v.tensor_mul(scr_b[:, :], scr_b[:, :], scr_m[:, :])
            v.tensor_scalar(scr_m[:, :], scr_m[:, :], -dumpf, None, ALU.mult)
            v.drain(fusable=False)
            v.tensor_scalar(scr_m[:, :], scr_m[:, :], dumpf, None, ALU.add)
            v.drain(fusable=False)
            v.tensor_add(scr_b[:, :], scr_b[:, :], scr_m[:, :])
            v.drain(fusable=False)
            for k in range(4):
                v.tensor_scalar(scr_c[:, :], ecolf[:, :],
                                eidx[:, tt, k:k + 1], None, ALU.is_equal)
                v.drain(fusable=False)
                v.tensor_mul(scr_c[:, :], scr_c[:, :], scr_b[:, :])
                v.drain(fusable=False)
                v.reduce_sum(stats[:, 8:9], scr_c[:, :], axis=AX)
                v.drain(fusable=False)
                C["sdi"].inc(v.tensor_copy(dsti[:, tt, k:k + 1],
                                           stats[:, 8:9]))

    spf_all = C["spf"].n
    ssc_after = {}
    for tt in range(n_tiles):
        sch = "ssc" if tt % 4 == 0 else "ssc%d" % (tt % 4)

        @blk.gpsimd
        def _(g, tt=tt, sch=sch):
            if tt == 0:
                g.wait_ge(C["spf"].sem, spf_all)
            if tt >= 4:
                g.wait_ge(C[sch].sem, ssc_after[tt - 4])
            for k in range(4):
                g.wait_ge(C["sdi"].sem, base_sdi + tt * 4 + k + 1)
                C[sch].inc(g.indirect_dma_start(
                    out=idxg_d,
                    out_offset=IndirectOffsetOnAxis(dsti[:, tt, k:k + 1], 0),
                    in_=toki[:, tt:tt + 1], in_offset=None), 16)
                C[sch].inc(g.indirect_dma_start(
                    out=wgd_d,
                    out_offset=IndirectOffsetOnAxis(dsti[:, tt, k:k + 1], 0),
                    in_=w4[:, tt, k:k + 1], in_offset=None), 16)
        ssc_after[tt] = C[sch].n


def emit_ffn_sparse(blk, C, nc, xscr_d, idxg_d, wgd_d, accd_d, slots,
                    xg, xgT, idx_sb, wg_sb, dsb, gact, htmp, utmp,
                    gup, htp, dpsum, gxtp, ident, gw_ap, uw_ap, dw_ap,
                    n_exp, ncap, ldch, tag):
    phase_done = {}
    nring = len(slots)
    rounds = 2 * ncap
    capn = ncap * 128

    base_gix = C["gix"].n
    base_ggx = C["ggx"].n
    base_gxt = C["gxt"].n
    base_gxe = C["gxe"].n
    base_sad = C["sad"].n
    sxs_all = C["sxs"].n
    ssc_all = [(ch, C[ch].n) for ch in ("ssc", "ssc1", "ssc2", "ssc3")]
    spf_all = C["spf"].n
    ue_done = {}
    um_done = {}
    de_done_cap = {}

    def emit_idx_load(e):
        @blk.sync
        def _(sy, e=e):
            if e == 0:
                for ch, cnt in ssc_all:
                    sy.wait_ge(C[ch].sem, cnt)
            else:
                sy.wait_ge(C["ggx"].sem, base_ggx + 16 * ncap * e)
            if e >= 4:
                sy.wait_ge(C["sad"].sem, base_sad + 16 * ncap * (e - 3))
                sy.wait_ge(C[tag + "ue"].sem, ue_done[e - 4])
            C["gix"].inc(sy.dma_start(
                out=idx_sb[:, e % 4, :],
                in_=idxg_d[e * capn:(e + 1) * capn, :].rearrange(
                    "(a p) o -> p (a o)", p=128),
                allow_slow_non_contiguous=True), 16)
            C["gix"].inc(sy.dma_start(
                out=wg_sb[:, e % 4, :],
                in_=wgd_d[e * capn:(e + 1) * capn, :].rearrange(
                    "(a p) o -> p (a o)", p=128),
                allow_slow_non_contiguous=True), 16)

    def emit_gather(e):
        @blk.gpsimd
        def _(g, e=e):
            g.wait_ge(C["gix"].sem, base_gix + 32 * (e + 1))
            if e == 0:
                g.wait_ge(C["sxs"].sem, sxs_all)
                g.wait_ge(C["spf"].sem, spf_all)
            else:
                g.wait_ge(C["gxt"].sem, base_gxt + rounds * e)
            for cpi in range(ncap):
                C["ggx"].inc(g.indirect_dma_start(
                    out=xg[:, cpi, :], out_offset=None,
                    in_=xscr_d,
                    in_offset=IndirectOffsetOnAxis(
                        idx_sb[:, e % 4, cpi:cpi + 1], 0)), 16)

    def emit_sads(e):
        @blk.gpsimd
        def _(g, e=e):
            for cpi in range(ncap):
                g.wait_ge(C[tag + "de"].sem, de_done_cap[(e, cpi)])
                g.wait_ge(C["sad"].sem, base_sad + 16 * (ncap * e + cpi))
                C["sad"].inc(g.indirect_dma_start(
                    out=accd_d,
                    out_offset=IndirectOffsetOnAxis(
                        idx_sb[:, e % 4, cpi:cpi + 1], 0),
                    in_=dsb[:, e % 2, :], in_offset=None,
                    compute_op=ALU.add), 16)

    emit_idx_load(0)
    emit_gather(0)

    for k in range(3 * n_exp):
        e, j = divmod(k, 3)
        src = [gw_ap, uw_ap, dw_ap][j]

        @blk.gpsimd
        def _(g, k=k, e=e, src=src):
            if k >= nring:
                ch, cnt = phase_done[k - nring]
                g.wait_ge(C[ch].sem, cnt)
            ap = src[e].rearrange("(c p) f -> p c f", p=128)
            nch = ap.shape[1]
            C[ldch[k % nring]].inc(g.dma_start(
                out=slots[k % nring][:, :].rearrange("p (c f) -> p c f", c=nch),
                in_=ap), 16)

        ld_needed = 16 * (k // nring + 1)

        if j == 0:
            if e >= 1:
                emit_sads(e - 1)
            if e + 1 < n_exp:
                emit_idx_load(e + 1)
                emit_gather(e + 1)

            base_ge = C[tag + "ge"].n
            hh_cur = C[tag + "hh"].n
            ue_cur = C[tag + "ue"].n
            um_prev = um_done.get(e - 1, 0)

            @blk.tensor
            def _(t, k=k, e=e, ld_needed=ld_needed, base_ge=base_ge,
                  ue_cur=ue_cur):
                t.wait_ge(C["ggx"].sem, base_ggx + 16 * ncap * (e + 1))
                for r in range(rounds):
                    gr = rounds * e + r
                    if gr >= 1:
                        t.wait_ge(C["gxe"].sem, base_gxe + gr)
                    cpi, half = r // 2, r % 2
                    for i in range(8):
                        inst = t.transpose(
                            gxtp[:, i, :],
                            xg[:, cpi, half * 1024 + i * 128:
                               half * 1024 + (i + 1) * 128],
                            ident)
                    C["gxt"].inc(inst)
                t.wait_ge(C["gxe"].sem, base_gxe + rounds * (e + 1))
                t.wait_ge(C[ldch[k % nring]].sem, ld_needed)
                t.wait_ge(C[tag + "ue"].sem, ue_cur)
                sl = slots[k % nring]
                for cpi in range(ncap):
                    if cpi >= 1:
                        t.wait_ge(C[tag + "ge"].sem, base_ge + cpi)
                    for (f0, f1) in ((0, 512), (512, 768)):
                        for chk in range(16):
                            inst = t.matmul(
                                gup[:, f0:f1],
                                xgT[:, cpi, chk * 128:(chk + 1) * 128],
                                sl[:, chk * 768 + f0:chk * 768 + f1],
                                start=(chk == 0), stop=(chk == 15))
                    C[tag + "gm"].inc(inst)

            for r in range(rounds):
                @blk.vector
                def _(v, e=e, r=r, um_prev=um_prev):
                    v.wait_ge(C["gxt"].sem, base_gxt + rounds * e + r + 1)
                    if r == 0 and e >= 1:
                        v.wait_ge(C[tag + "um"].sem, um_prev)
                    cpi, half = r // 2, r % 2
                    C["gxe"].inc(v.tensor_copy(
                        xgT[:, cpi, half * 1024:(half + 1) * 1024],
                        gxtp[:, :, :]))

            base_gm = C[tag + "gm"].n - ncap
            for cpi in range(ncap):
                @blk.scalar
                def _(s, cpi=cpi, base_gm=base_gm, hh_cur=hh_cur):
                    s.wait_ge(C[tag + "gm"].sem, base_gm + cpi + 1)
                    s.wait_ge(C[tag + "hh"].sem, hh_cur)
                    C[tag + "ge"].inc(s.activation(
                        gact[:, cpi, :], gup[:, 0:768], AF.Silu))
            phase_done[k] = (tag + "ge", C[tag + "ge"].n)

        elif j == 1:
            base_um = C[tag + "um"].n
            base_ue = C[tag + "ue"].n
            base_hh = C[tag + "hh"].n
            base_ht = C[tag + "ht"].n
            base_he = C[tag + "he"].n
            dm_cur = C[tag + "dm"].n
            ge_cur = C[tag + "ge"].n

            @blk.tensor
            def _(t, k=k, e=e, ld_needed=ld_needed, base_ue=base_ue,
                  base_hh=base_hh, base_he=base_he, ge_cur=ge_cur):
                t.wait_ge(C[ldch[k % nring]].sem, ld_needed)
                t.wait_ge(C[tag + "ge"].sem, ge_cur)
                sl = slots[k % nring]
                for cpi in range(ncap):
                    if cpi >= 1:
                        t.wait_ge(C[tag + "ue"].sem, base_ue + cpi)
                    for (f0, f1) in ((0, 512), (512, 768)):
                        for chk in range(16):
                            inst = t.matmul(
                                gup[:, f0:f1],
                                xgT[:, cpi, chk * 128:(chk + 1) * 128],
                                sl[:, chk * 768 + f0:chk * 768 + f1],
                                start=(chk == 0), stop=(chk == 15))
                    C[tag + "um"].inc(inst)
                    t.wait_ge(C[tag + "hh"].sem, base_hh + cpi + 1)
                    if cpi >= 2:
                        t.wait_ge(C[tag + "he"].sem, base_he + cpi - 1)
                    for ic in range(6):
                        inst = t.transpose(
                            htp[:, ic * 128:(ic + 1) * 128],
                            htmp[:, cpi % 2, ic * 128:(ic + 1) * 128],
                            ident)
                    C[tag + "ht"].inc(inst)

            for cpi in range(ncap):
                @blk.vector
                def _(v, e=e, cpi=cpi, base_um=base_um, base_ht=base_ht):
                    v.wait_ge(C[tag + "um"].sem, base_um + cpi + 1)
                    if cpi == 0:
                        v.wait_ge(C["gix"].sem, base_gix + 32 * (e + 1))
                    v.tensor_scalar(utmp[:, :], gup[:, 0:768],
                                    wg_sb[:, e % 4, cpi:cpi + 1], None,
                                    ALU.mult)
                    C[tag + "ue"].inc(v.drain(fusable=False))
                    if cpi >= 2:
                        v.wait_ge(C[tag + "ht"].sem, base_ht + cpi - 1)
                    C[tag + "hh"].inc(v.tensor_mul(
                        htmp[:, cpi % 2, :], gact[:, cpi, :], utmp[:, :]))

                @blk.vector
                def _(v, cpi=cpi, base_ht=base_ht, dm_cur=dm_cur):
                    v.wait_ge(C[tag + "ht"].sem, base_ht + cpi + 1)
                    v.wait_ge(C[tag + "dm"].sem, dm_cur)
                    C[tag + "he"].inc(v.tensor_copy(gact[:, ncap + cpi, :],
                                                    htp[:, 0:768]))
            ue_done[e] = C[tag + "ue"].n
            um_done[e] = C[tag + "um"].n
            phase_done[k] = (tag + "he", C[tag + "he"].n)

        else:
            base_dm = C[tag + "dm"].n
            base_de = C[tag + "de"].n
            he_cur = C[tag + "he"].n

            @blk.tensor
            def _(t, k=k, ld_needed=ld_needed, base_de=base_de,
                  he_cur=he_cur):
                t.wait_ge(C[ldch[k % nring]].sem, ld_needed)
                t.wait_ge(C[tag + "he"].sem, he_cur)
                sl = slots[k % nring]
                for cpi in range(ncap):
                    if cpi >= 1:
                        t.wait_ge(C[tag + "de"].sem, base_de + cpi)
                    for ic in range(6):
                        for nb in range(4):
                            inst = t.matmul(
                                dpsum[:, nb * 512:(nb + 1) * 512],
                                gact[:, ncap + cpi, ic * 128:(ic + 1) * 128],
                                sl[:, ic * 2048 + nb * 512:
                                   ic * 2048 + (nb + 1) * 512],
                                start=(ic == 0), stop=(ic == 5))
                    C[tag + "dm"].inc(inst)

            base_dm2 = C[tag + "dm"].n - ncap
            for cpi in range(ncap):
                @blk.vector
                def _(v, e=e, cpi=cpi, base_dm2=base_dm2):
                    v.wait_ge(C[tag + "dm"].sem, base_dm2 + cpi + 1)
                    if cpi >= 1:
                        v.wait_ge(C["sad"].sem,
                                  base_sad + 16 * (ncap * e + cpi))
                    elif e >= 2:
                        v.wait_ge(C["sad"].sem,
                                  base_sad + 16 * ncap * (e - 1))
                    C[tag + "de"].inc(v.tensor_copy(dsb[:, e % 2, :],
                                                    dpsum[:, :]))
                de_done_cap[(e, cpi)] = C[tag + "de"].n
            phase_done[k] = (tag + "de", C[tag + "de"].n)

    emit_sads(n_exp - 1)


def emit_final_adds(blk, C, nc, x_sb, accd_d, axf, n_tiles):
    sad_all = C["sad"].n
    base_fax = C["fax"].n
    base_fad = C["fad"].n
    for tt in range(n_tiles):
        @blk.sync
        def _(sy, tt=tt):
            if tt == 0:
                sy.wait_ge(C["sad"].sem, sad_all)
            if tt >= 1:
                sy.wait_ge(C["fad"].sem, base_fad + tt)
            C["fax"].inc(sy.dma_start(
                out=axf[:, tt % 2, :],
                in_=accd_d[tt * 128:(tt + 1) * 128, :]), 16)

        @blk.vector
        def _(v, tt=tt):
            v.wait_ge(C["fax"].sem, base_fax + 16 * (tt + 1))
            C["fad"].inc(v.tensor_add(x_sb[:, tt, :], x_sb[:, tt, :],
                                      axf[:, tt % 2, :]))


def emit_ffn(blk, C, nc, x_sb, xnT, wn, slots, gact, hT, htmp, utmp,
             gup, htp, dpsum, ident, gw_ap, uw_ap, dw_ap, n_tiles, n_exp,
             ldch, tag):
    phase_done = {}
    nring = len(slots)

    for k in range(3 * n_exp):
        e, j = divmod(k, 3)
        src = [gw_ap, uw_ap, dw_ap][j]

        @blk.gpsimd
        def _(g, k=k, e=e, src=src):
            if k >= nring:
                ch, cnt = phase_done[k - nring]
                g.wait_ge(C[ch].sem, cnt)
            ap = src[e].rearrange("(c p) f -> p c f", p=128)
            nch = ap.shape[1]
            C[ldch[k % nring]].inc(g.dma_start(
                out=slots[k % nring][:, :].rearrange("p (c f) -> p c f", c=nch),
                in_=ap), 16)

        ld_needed = 16 * (k // nring + 1)

        if j == 0:
            base_ge = C[tag + "ge"].n
            hh_cur = C[tag + "hh"].n

            ue_cur = C[tag + "ue"].n

            @blk.tensor
            def _(t, k=k, ld_needed=ld_needed, base_ge=base_ge, ue_cur=ue_cur):
                t.wait_ge(C[ldch[k % nring]].sem, ld_needed)
                t.wait_ge(C[tag + "ue"].sem, ue_cur)
                sl = slots[k % nring]
                for tt in range(n_tiles):
                    if tt >= 1:
                        t.wait_ge(C[tag + "ge"].sem, base_ge + tt)
                    for (f0, f1) in ((0, 512), (512, 768)):
                        for chk in range(16):
                            inst = t.matmul(
                                gup[:, f0:f1],
                                xnT[:, tt, chk * 128:(chk + 1) * 128],
                                sl[:, chk * 768 + f0:chk * 768 + f1],
                                start=(chk == 0), stop=(chk == 15))
                    C[tag + "gm"].inc(inst)

            base_gm = C[tag + "gm"].n - n_tiles
            for tt in range(n_tiles):
                @blk.scalar
                def _(s, tt=tt, base_gm=base_gm, hh_cur=hh_cur):
                    s.wait_ge(C[tag + "gm"].sem, base_gm + tt + 1)
                    s.wait_ge(C[tag + "hh"].sem, hh_cur)
                    C[tag + "ge"].inc(s.activation(
                        gact[:, tt, :], gup[:, 0:768], AF.Silu))
            phase_done[k] = (tag + "ge", C[tag + "ge"].n)

        elif j == 1:
            base_um = C[tag + "um"].n
            base_ue = C[tag + "ue"].n
            base_hh = C[tag + "hh"].n
            base_ht = C[tag + "ht"].n
            base_he = C[tag + "he"].n
            dm_cur = C[tag + "dm"].n

            ge_cur = C[tag + "ge"].n

            @blk.tensor
            def _(t, k=k, ld_needed=ld_needed, base_ue=base_ue,
                  base_hh=base_hh, base_he=base_he, ge_cur=ge_cur):
                t.wait_ge(C[ldch[k % nring]].sem, ld_needed)
                t.wait_ge(C[tag + "ge"].sem, ge_cur)
                sl = slots[k % nring]
                for tt in range(n_tiles):
                    if tt >= 1:
                        t.wait_ge(C[tag + "ue"].sem, base_ue + tt)
                    for (f0, f1) in ((0, 512), (512, 768)):
                        for chk in range(16):
                            inst = t.matmul(
                                gup[:, f0:f1],
                                xnT[:, tt, chk * 128:(chk + 1) * 128],
                                sl[:, chk * 768 + f0:chk * 768 + f1],
                                start=(chk == 0), stop=(chk == 15))
                    C[tag + "um"].inc(inst)
                    t.wait_ge(C[tag + "hh"].sem, base_hh + tt + 1)
                    if tt >= 2:
                        t.wait_ge(C[tag + "he"].sem, base_he + tt - 1)
                    for ic in range(6):
                        inst = t.transpose(
                            htp[:, ic * 128:(ic + 1) * 128],
                            htmp[:, tt % 2, ic * 128:(ic + 1) * 128],
                            ident)
                    C[tag + "ht"].inc(inst)

            for tt in range(n_tiles):
                @blk.vector
                def _(v, e=e, tt=tt, base_um=base_um, base_ht=base_ht):
                    v.wait_ge(C[tag + "um"].sem, base_um + tt + 1)
                    v.tensor_scalar(utmp[:, :], gup[:, 0:768],
                                    wn[:, tt, e:e + 1], None, ALU.mult)
                    C[tag + "ue"].inc(v.drain(fusable=False))
                    if tt >= 2:
                        v.wait_ge(C[tag + "ht"].sem, base_ht + tt - 1)
                    C[tag + "hh"].inc(v.tensor_mul(
                        htmp[:, tt % 2, :], gact[:, tt, :], utmp[:, :]))

                @blk.vector
                def _(v, tt=tt, base_ht=base_ht, dm_cur=dm_cur):
                    v.wait_ge(C[tag + "ht"].sem, base_ht + tt + 1)
                    v.wait_ge(C[tag + "dm"].sem, dm_cur)
                    C[tag + "he"].inc(v.tensor_copy(hT[:, tt, :],
                                                    htp[:, 0:768]))
            phase_done[k] = (tag + "he", C[tag + "he"].n)

        else:
            base_dm = C[tag + "dm"].n
            base_de = C[tag + "de"].n
            he_cur = C[tag + "he"].n

            @blk.tensor
            def _(t, k=k, ld_needed=ld_needed, base_de=base_de, he_cur=he_cur):
                t.wait_ge(C[ldch[k % nring]].sem, ld_needed)
                t.wait_ge(C[tag + "he"].sem, he_cur)
                sl = slots[k % nring]
                for tt in range(n_tiles):
                    if tt >= 1:
                        t.wait_ge(C[tag + "de"].sem, base_de + tt)
                    for ic in range(6):
                        for nb in range(4):
                            inst = t.matmul(
                                dpsum[:, nb * 512:(nb + 1) * 512],
                                hT[:, tt, ic * 128:(ic + 1) * 128],
                                sl[:, ic * 2048 + nb * 512:
                                   ic * 2048 + (nb + 1) * 512],
                                start=(ic == 0), stop=(ic == 5))
                    C[tag + "dm"].inc(inst)

            base_dm2 = C[tag + "dm"].n - n_tiles
            for tt in range(n_tiles):
                @blk.vector
                def _(v, tt=tt, base_dm2=base_dm2):
                    v.wait_ge(C[tag + "dm"].sem, base_dm2 + tt + 1)
                    C[tag + "de"].inc(v.tensor_add(
                        x_sb[:, tt, :], x_sb[:, tt, :], dpsum[:, :]))
            phase_done[k] = (tag + "de", C[tag + "de"].n)


def build_l2a(n_tiles=8, n_exp=32, cap=256):
    nc = bass.Bass("TRN2", target_bir_lowering=False)
    ntok = n_tiles * 128
    ncap = cap // 128
    nslot = n_exp * cap

    xin = nc.dram_tensor("xin", [ntok, H], F32, kind="ExternalInput").ap()
    aoT = nc.dram_tensor("aoT", [H, ntok], F32, kind="ExternalInput").ap()
    woT = nc.dram_tensor("woT", [H, H], F32, kind="ExternalInput").ap()
    routerT = nc.dram_tensor("routerT", [H, n_exp], F32, kind="ExternalInput").ap()
    gwT = nc.dram_tensor("gwT", [n_exp, H, 768], F32, kind="ExternalInput").ap()
    uwT = nc.dram_tensor("uwT", [n_exp, H, 768], F32, kind="ExternalInput").ap()
    dwT = nc.dram_tensor("dwT", [n_exp, 768, H], F32, kind="ExternalInput").ap()
    win = nc.dram_tensor("win", [H, 8576], F32, kind="ExternalInput").ap()
    dtb_in = nc.dram_tensor("dtb", [64, 1], F32, kind="ExternalInput").ap()
    alog_in = nc.dram_tensor("alog", [64, 1], F32, kind="ExternalInput").ap()
    triu_in = nc.dram_tensor("triu1k", [ntok, ntok], F32,
                             kind="ExternalInput").ap()

    xscr = nc.dram_tensor("xscr", [ntok + 8, H], BF16)
    idxg = nc.dram_tensor("idxg", [nslot + 1, 1], I32)
    wgd = nc.dram_tensor("wgd", [nslot + 1, 1], F32)
    accd = nc.dram_tensor("accd", [ntok + 8, H], F32)

    x2_out = nc.dram_tensor("x2", [ntok, H], F32, kind="ExternalOutput").ap()
    xbct_out = nc.dram_tensor("xbct", [CONV_DIM, ntok], F32, kind="ExternalOutput").ap()
    zt_out = nc.dram_tensor("zt", [D_INNER, ntok], BF16, kind="ExternalOutput").ap()
    dtt_out = nc.dram_tensor("dtt", [64, ntok], F32, kind="ExternalOutput").ap()
    dat_out = nc.dram_tensor("dat", [64, ntok], F32, kind="ExternalOutput").ap()

    gw_ap = [gwT[e] for e in range(n_exp)]
    uw_ap = [uwT[e] for e in range(n_exp)]
    dw_ap = [dwT[e] for e in range(n_exp)]

    with ExitStack() as ctx:
        sb = lambda n, s, d: ctx.enter_context(nc.sbuf_tensor(n, s, d))
        x_sb = sb("x_sb", [128, n_tiles, H], F32)
        xn = sb("xn", [128, H], BF16)
        wn = sb("wn", [128, n_tiles, n_exp], F32)
        cst = sb("cst", [128, 2], F32)
        ident = sb("ident", [128, 128], BF16)
        identf = sb("identf", [128, 128], F32)
        stats = sb("stats", [128, 32], F32)
        sqscr = sb("sqscr", [128, H], BF16)
        rt_sb = sb("rt_sb", [128, 16, n_exp], BF16)
        dtb = sb("dtbs", [128, 1], F32)
        nega = sb("nega", [128, 1], F32)
        ecolf = sb("ecolf", [128, n_exp], F32)
        ecapf = sb("ecapf", [128, n_exp], F32)
        toki = sb("toki", [128, n_tiles], I32)
        zf = sb("zf", [128, nslot // 128], F32)
        zi = sb("zi", [128, nslot // 128], I32)
        azf = sb("azf", [128, H], F32)

        names = ["dxa0", "dxa1", "dao0", "dao1", "dwo", "drt", "dsm", "dse",
                 "sq", "nv", "rs", "xn", "fat", "fae", "wom", "woe",
                 "rmm", "rmx", "rex", "rev",
                 "fgm", "fge", "fum", "fue", "fhh", "fht", "fhe", "fdm", "fde",
                 "fl0", "fl1", "fl2", "dwi0", "dwi1", "wim", "wie",
                 "dtc", "dtv", "st", "stz0", "stz1", "stb0", "stb1", "stx",
                 "rsx", "csm", "cse", "ctt", "cte", "sdi", "ssc", "spf",
                 "ssc1", "ssc2", "ssc3",
                 "spm", "sxs", "gix", "ggx", "gxt", "gxe", "sad", "fax",
                 "fad", "dtr"]
        C = {n: Ch(ctx, nc, "c_" + n) for n in names}

        blk = ctx.enter_context(nc.Block())

        @blk.gpsimd
        def _(g):
            C["dse"].inc(g.memset(ident[:, :], 0.0))
            C["dse"].inc(g.memset(cst[:, 0:1], EPS))
            C["dse"].inc(g.memset(cst[:, 1:2], 1.0))
            C["dse"].inc(g.memset(dtb[:, :], 0.0))
            C["dse"].inc(g.memset(nega[:, :], 0.0))
            C["spm"].inc(g.memset(identf[:, :], 0.0))
            C["spm"].inc(g.memset(zf[:, :], float(ntok)))
            C["spm"].inc(g.memset(azf[:, :], 0.0))
            C["spm"].inc(g.iota(ecolf[:, :], pattern=[[1, n_exp]],
                                channel_multiplier=0,
                                allow_small_or_imprecise_dtypes=True))
            C["spm"].inc(g.iota(ecapf[:, :], pattern=[[cap, n_exp]],
                                channel_multiplier=0,
                                allow_small_or_imprecise_dtypes=True))
            C["spm"].inc(g.iota(toki[:, :], pattern=[[128, n_tiles]],
                                channel_multiplier=1))
            g.wait_ge(C["dse"].sem, C["dse"].n)
            g.wait_ge(C["spm"].sem, C["spm"].n)
            C["dse"].inc(g.affine_select(
                out=ident[:, :], in_=ident[:, :],
                compare_op=ALU.not_equal, fill=1.0, base=0,
                pattern=[[-1, 128]], channel_multiplier=1))
            C["spm"].inc(g.affine_select(
                out=identf[:, :], in_=identf[:, :],
                compare_op=ALU.not_equal, fill=1.0, base=0,
                pattern=[[-1, 128]], channel_multiplier=1))
            C["spm"].inc(g.tensor_copy(zi[:, :], zf[:, :]))
            g.wait_ge(C["spm"].sem, C["spm"].n)
            C["spf"].inc(g.dma_start(
                out=idxg.ap()[0:nslot, :].rearrange("(p a) o -> p (a o)",
                                                    p=128),
                in_=zi[:, :]), 16)
            C["spf"].inc(g.dma_start(
                out=wgd.ap()[0:nslot, :].rearrange("(p a) o -> p (a o)",
                                                   p=128),
                in_=azf[:, 0:nslot // 128]), 16)
            for tt in range(n_tiles):
                C["spf"].inc(g.dma_start(
                    out=accd.ap()[tt * 128:(tt + 1) * 128, :],
                    in_=azf[:, :]), 16)
            C["spf"].inc(g.dma_start(
                out=accd.ap()[ntok:ntok + 8, :], in_=azf[0:8, :]), 16)
            C["spf"].inc(g.dma_start(
                out=xscr.ap()[ntok:ntok + 8, :], in_=azf[0:8, :]), 16)
            C["drt"].inc(g.dma_start(
                out=rt_sb[:, :, :],
                in_=routerT.rearrange("(c p) f -> p c f", p=128)), 16)

        @blk.sync
        def _(sy):
            sy.wait_ge(C["dse"].sem, 5)
            C["dsm"].inc(sy.dma_start(out=dtb[0:64, :], in_=dtb_in), 16)
            C["dsm"].inc(sy.dma_start(out=nega[0:64, :], in_=alog_in), 16)

        @blk.scalar
        def _(s):
            s.wait_ge(C["dse"].sem, C["dse"].n)
            s.wait_ge(C["dsm"].sem, C["dsm"].n)
            C["dse"].inc(s.activation(nega[:, :], nega[:, :], AF.Exp))

        pre_done = dict(dse=C["dse"].n, dsm=C["dsm"].n, drt=C["drt"].n,
                        spm=C["spm"].n)

        def wait_pre(e):
            for k, v in pre_done.items():
                e.wait_ge(C[k].sem, v)

        @blk.vector
        def _(v):
            wait_pre(v)

        @blk.tensor
        def _(t):
            wait_pre(t)

        @blk.scalar
        def _(s):
            wait_pre(s)

        pA = ExitStack()
        xnT = pA.enter_context(nc.sbuf_tensor("xnT", [128, n_tiles, H], BF16))
        with ExitStack() as p1:
            sb1 = lambda n, s, d: p1.enter_context(nc.sbuf_tensor(n, s, d))
            ps1 = lambda n, s, d: p1.enter_context(nc.psum_tensor(n, s, d))
            wo_sb = sb1("wo_sb", [128, 16, H], BF16)
            ao_sb = sb1("ao_sb", [128, 2, H], BF16)
            xt = sb1("xt", [128, 2, H], F32)
            wop = ps1("wop", [128, H], F32)
            trp = ps1("trp", [128, 2048], BF16)

            @blk.gpsimd
            def _(g):
                C["dwo"].inc(g.dma_start(
                    out=wo_sb[:, :, :],
                    in_=woT.rearrange("(c p) f -> p c f", p=128)), 16)

            for tt in range(n_tiles):
                @blk.sync
                def _(sy, tt=tt):
                    if tt >= 2:
                        sy.wait_ge(C["woe"].sem, tt - 1)
                    C["dxa%d" % (tt % 2)].inc(sy.dma_start(
                        out=xt[:, tt % 2, :],
                        in_=xin[tt * 128:(tt + 1) * 128, :]), 16)

                @blk.gpsimd
                def _(g, tt=tt):
                    if tt >= 2:
                        sl = C["wom"].n
                        g.wait_ge(C["wom"].sem, sl - 1)
                    C["dao%d" % (tt % 2)].inc(g.dma_start(
                        out=ao_sb[:, tt % 2, :].rearrange("p (c t) -> p c t", c=16),
                        in_=aoT[:, tt * 128:(tt + 1) * 128].rearrange(
                            "(c p) t -> p c t", p=128)), 16)

                dao_n = C["dao%d" % (tt % 2)].n
                woe_n = C["woe"].n

                @blk.tensor
                def _(t, tt=tt, dao_n=dao_n, woe_n=woe_n):
                    t.wait_ge(C["dao%d" % (tt % 2)].sem, dao_n)
                    if tt == 0:
                        t.wait_ge(C["dwo"].sem, 16)
                    if tt >= 1:
                        t.wait_ge(C["woe"].sem, woe_n)
                    for chk in range(16):
                        for nb in range(4):
                            inst = t.matmul(
                                wop[:, nb * 512:(nb + 1) * 512],
                                ao_sb[:, tt % 2, chk * 128:(chk + 1) * 128],
                                wo_sb[:, chk, nb * 512:(nb + 1) * 512],
                                start=(chk == 0), stop=(chk == 15))
                    C["wom"].inc(inst)

                dx_n = C["dxa%d" % (tt % 2)].n
                wom_n = C["wom"].n

                @blk.vector
                def _(v, tt=tt, dx_n=dx_n, wom_n=wom_n):
                    v.wait_ge(C["dxa%d" % (tt % 2)].sem, dx_n)
                    v.wait_ge(C["wom"].sem, wom_n)
                    C["woe"].inc(v.tensor_add(x_sb[:, tt, :], xt[:, tt % 2, :],
                                              wop[:, :]))

                if tt >= 1:
                    @blk.vector
                    def _(v, tt=tt):
                        v.wait_ge(C["fat"].sem, tt)
                        v.wait_ge(C["sxs"].sem, 16 * tt)

                emit_ln_tile(blk, C, x_sb[:, tt, :], xn[:, :], sqscr, stats,
                             cst[:, 0:1], "woe")
                emit_featmajor(blk, C, nc, xn, xnT[:, tt, :], trp, ident[:, :],
                               "fa")

                xn_n = C["xn"].n

                @blk.sync
                def _(sy, tt=tt, xn_n=xn_n):
                    sy.wait_ge(C["xn"].sem, xn_n)
                    C["sxs"].inc(sy.dma_start(
                        out=xscr.ap()[tt * 128:(tt + 1) * 128, :],
                        in_=xn[:, :]), 16)

        phase_barrier(blk, C)

        with ExitStack() as p2:
            sb2 = lambda n, s, d: p2.enter_context(nc.sbuf_tensor(n, s, d))
            ps2 = lambda n, s, d: p2.enter_context(nc.psum_tensor(n, s, d))
            rpsum = ps2("rpsum", [128, max(n_exp, 32)], F32)
            probs = sb2("probs", [128, n_exp], F32)
            top8 = sb2("top8", [128, 8], F32)
            ind = sb2("ind", [128, n_tiles, n_exp], F32)
            w4 = sb2("w4", [128, n_tiles, 4], F32)
            eidx = sb2("eidx", [128, n_tiles, 4], F32)
            dsti = sb2("dsti", [128, n_tiles, 4], I32)
            cumtok = sb2("cumtok", [128, n_tiles, n_exp], F32)
            scr_m = sb2("scr_m", [128, n_exp], F32)
            scr_b = sb2("scr_b", [128, n_exp], F32)
            scr_c = sb2("scr_c", [128, n_exp], F32)
            triu_sb = sb2("triu_sb", [128, n_tiles, ntok], F32)
            csT = sb2("csT", [128, ntok], F32)
            cspsum = ps2("cspsum", [128, ntok], F32)
            ctp = ps2("ctp", [128, 2, 512], F32)

            @blk.sync
            def _(sy):
                C["dtr"].inc(sy.dma_start(
                    out=triu_sb[:, :, :],
                    in_=triu_in.rearrange("(a p) f -> p a f", p=128)), 16)

            @blk.gpsimd
            def _(g):
                C["spm"].inc(g.memset(csT[:, :], 0.0))

            emit_router(blk, C, nc, xnT, wn, rt_sb, rpsum, probs, top8, stats,
                        n_tiles, sparse=dict(ind=ind, w4=w4, eidx=eidx,
                                             scr=scr_c, ecolf=ecolf))
            emit_moe_index(blk, C, nc, ind, eidx, w4, dsti, cumtok, triu_sb,
                           identf[:, :], ecolf, ecapf, scr_m, scr_b, scr_c,
                           stats, cspsum, ctp, csT, idxg.ap(), wgd.ap(), toki,
                           cap, n_tiles, n_exp)

        phase_barrier(blk, C)
        pA.close()

        with ExitStack() as p3:
            sb3 = lambda n, s, d: p3.enter_context(nc.sbuf_tensor(n, s, d))
            ps3 = lambda n, s, d: p3.enter_context(nc.psum_tensor(n, s, d))
            slots = [sb3("slot%d" % i, [128, 12288], BF16) for i in range(2)]
            gact = sb3("gact", [128, 2 * ncap, 768], BF16)
            htmp = sb3("htmp", [128, 2, 768], BF16)
            utmp = sb3("utmp", [128, 768], BF16)
            xg = sb3("xg", [128, ncap, H], BF16)
            xgT = sb3("xgT", [128, ncap, H], BF16)
            idx_sb = sb3("idx_sb", [128, 4, ncap], I32)
            wg_sb = sb3("wg_sb", [128, 4, ncap], F32)
            dsb = sb3("dsb", [128, 2, H], F32)
            gup = ps3("gup", [128, 1024], F32)
            htp = ps3("htp", [128, 1024], BF16)
            dpsum = ps3("dpsum", [128, 2048], F32)
            gxtp = ps3("gxtp", [128, 8, 128], BF16)
            emit_ffn_sparse(blk, C, nc, xscr.ap(), idxg.ap(), wgd.ap(),
                            accd.ap(), slots, xg, xgT, idx_sb, wg_sb, dsb,
                            gact, htmp, utmp, gup, htp, dpsum, gxtp,
                            ident[:, :], gw_ap, uw_ap, dw_ap,
                            n_exp, ncap, ["fl0", "fl1"], "f")

        phase_barrier(blk, C)

        with ExitStack() as pfin:
            axf = pfin.enter_context(nc.sbuf_tensor("axf", [128, 2, H], F32))
            emit_final_adds(blk, C, nc, x_sb, accd.ap(), axf, n_tiles)

        phase_barrier(blk, C)

        pB = ExitStack()
        xnT = pB.enter_context(nc.sbuf_tensor("xnT2", [128, n_tiles, H],
                                              BF16))
        with ExitStack() as p4:
            trp = p4.enter_context(nc.psum_tensor("trp2", [128, 2048], BF16))
            for tt in range(n_tiles):
                if tt >= 1:
                    @blk.vector
                    def _(v, tt=tt):
                        v.wait_ge(C["fat"].sem, n_tiles + tt)

                emit_ln_tile(blk, C, x_sb[:, tt, :], xn[:, :], sqscr, stats,
                             cst[:, 0:1], "fad")
                emit_featmajor(blk, C, nc, xn, xnT[:, tt, :], trp, ident[:, :],
                               "fa")

                @blk.sync
                def _(sy, tt=tt):
                    sy.wait_ge(C["fad"].sem, C["fad"].n)
                    C["stx"].inc(sy.dma_start(
                        out=x2_out[tt * 128:(tt + 1) * 128, :],
                        in_=x_sb[:, tt, :]), 16)

        phase_barrier(blk, C)

        with ExitStack() as p5:
            sb5 = lambda n, s, d: p5.enter_context(nc.sbuf_tensor(n, s, d))
            ps5 = lambda n, s, d: p5.enter_context(nc.psum_tensor(n, s, d))
            wi_sb = sb5("wi_sb", [128, 2, 16, 128], BF16)
            zst = sb5("zst", [128, 2, 1024], BF16)
            xst = sb5("xst", [128, 2, 1024], F32)
            dtt_sb = sb5("dtt_sb", [128, 1024], F32)
            dat_sb = sb5("dat_sb", [128, 1024], F32)
            dsc = sb5("dsc", [128, 2, 1024], F32)
            wip = [ps5("wip%d" % i, [128, 1024], F32) for i in range(2)]

            n_dt = 67
            tile_groups = [(i, min(i + 4, n_tiles)) for i in range(0, n_tiles, 4)]
            fae_all = C["fae"].n
            for d in range(n_dt):
                @blk.gpsimd
                def _(g, d=d):
                    if d >= 2:
                        g.wait_ge(C["wim"].sem, d - 1)
                    C["dwi%d" % (d % 2)].inc(g.dma_start(
                        out=wi_sb[:, d % 2, :, :],
                        in_=win[:, d * 128:(d + 1) * 128].rearrange(
                            "(c p) f -> p c f", p=128)), 16)

                dwi_n = C["dwi%d" % (d % 2)].n
                wie_n = C["wie"].n

                @blk.tensor
                def _(t, d=d, dwi_n=dwi_n, wie_n=wie_n):
                    t.wait_ge(C["dwi%d" % (d % 2)].sem, dwi_n)
                    if d == 0:
                        t.wait_ge(C["fae"].sem, fae_all)
                    if d >= 2:
                        t.wait_ge(C["wie"].sem, wie_n - 1)
                    for gi, (t0, t1) in enumerate(tile_groups):
                        for chk in range(16):
                            inst = t.matmul(
                                wip[d % 2][:, t0 * 128:t1 * 128],
                                wi_sb[:, d % 2, chk, :],
                                xnT[:, t0:t1, chk * 128:(chk + 1) * 128],
                                start=(chk == 0), stop=(chk == 15))
                    C["wim"].inc(inst)

                wim_n = C["wim"].n

                if d < 32:
                    @blk.vector
                    def _(v, d=d, wim_n=wim_n):
                        v.wait_ge(C["wim"].sem, wim_n)
                        if d >= 2:
                            v.wait_ge(C["stz%d" % (d % 2)].sem, 16 * (d // 2))
                        C["wie"].inc(v.tensor_copy(zst[:, d % 2, 0:ntok],
                                                   wip[d % 2][:, 0:ntok]))

                    @blk.sync
                    def _(sy, d=d):
                        sy.wait_ge(C["wie"].sem, C["wie"].n)
                        C["stz%d" % (d % 2)].inc(sy.dma_start(
                            out=zt_out[d * 128:(d + 1) * 128, :],
                            in_=zst[:, d % 2, 0:ntok]), 16)
                elif d < 66:
                    @blk.vector
                    def _(v, d=d, wim_n=wim_n):
                        v.wait_ge(C["wim"].sem, wim_n)
                        if d >= 34:
                            v.wait_ge(C["stb%d" % (d % 2)].sem,
                                      16 * ((d - 32) // 2))
                        C["wie"].inc(v.tensor_copy(xst[:, d % 2, 0:ntok],
                                                   wip[d % 2][:, 0:ntok]))

                    @blk.sync
                    def _(sy, d=d):
                        sy.wait_ge(C["wie"].sem, C["wie"].n)
                        C["stb%d" % (d % 2)].inc(sy.dma_start(
                            out=xbct_out[(d - 32) * 128:(d - 31) * 128, :],
                            in_=xst[:, d % 2, 0:ntok]), 16)
                else:
                    @blk.vector
                    def _(v, d=d, wim_n=wim_n):
                        v.wait_ge(C["wim"].sem, wim_n)
                        C["dtv"].inc(v.tensor_scalar(dsc[:, 0, 0:ntok],
                                                     wip[d % 2][:, 0:ntok],
                                                     dtb[:, :], None, ALU.add))

                    @blk.scalar
                    def _(s):
                        s.wait_ge(C["dtv"].sem, C["dtv"].n)
                        s.activation(dsc[:, 1, 0:ntok], dsc[:, 0, 0:ntok],
                                     AF.Abs)
                        s.drain(fusable=False)
                        s.activation(dsc[:, 1, 0:ntok], dsc[:, 1, 0:ntok],
                                     AF.Exp, scale=-1.0)
                        s.drain(fusable=False)
                        C["dtc"].inc(s.activation(dsc[:, 1, 0:ntok],
                                                  dsc[:, 1, 0:ntok],
                                                  AF.Ln, bias=cst[:, 1:2]))

                    @blk.vector
                    def _(v):
                        v.wait_ge(C["dtc"].sem, C["dtc"].n)
                        v.tensor_scalar(dsc[:, 0, 0:ntok], dsc[:, 0, 0:ntok],
                                        0.0, None, ALU.max)
                        v.drain(fusable=False)
                        v.tensor_add(dtt_sb[:, 0:ntok], dsc[:, 0, 0:ntok],
                                     dsc[:, 1, 0:ntok])
                        v.drain(fusable=False)
                        v.tensor_scalar(dat_sb[:, 0:ntok], dtt_sb[:, 0:ntok],
                                        nega[:, :], -1.0, ALU.mult, ALU.mult)
                        C["dtv"].inc(v.drain(fusable=False))

                    @blk.sync
                    def _(sy):
                        sy.wait_ge(C["dtv"].sem, C["dtv"].n)
                        C["st"].inc(sy.dma_start(out=dtt_out,
                                                 in_=dtt_sb[0:64, 0:ntok]), 16)
                        C["st"].inc(sy.dma_start(out=dat_out,
                                                 in_=dat_sb[0:64, 0:ntok]), 16)

        @blk.sync
        def _(sy):
            for chn in ("st", "stz0", "stz1", "stb0", "stb1", "stx"):
                sy.wait_ge(C[chn].sem, C[chn].n)

        pB.close()

    return nc


def build_l2b(n_tiles=8):
    nc = bass.Bass("TRN2", target_bir_lowering=False)
    ntok = n_tiles * 128
    nch = ntok // 256

    xbce = nc.dram_tensor("xbce", [CONV_DIM, ntok + 3], F32, kind="ExternalInput").ap()
    dtt_in = nc.dram_tensor("dtt", [64, ntok], F32, kind="ExternalInput").ap()
    dat_in = nc.dram_tensor("dat", [64, ntok], F32, kind="ExternalInput").ap()
    convw = nc.dram_tensor("convw", [CONV_DIM, 4], F32, kind="ExternalInput").ap()
    convb = nc.dram_tensor("convb", [CONV_DIM, 1], F32, kind="ExternalInput").ap()
    triu = nc.dram_tensor("triu", [256, 256], F32, kind="ExternalInput").ap()

    xbao = nc.dram_tensor("xbca", [CONV_DIM, ntok], BF16, kind="ExternalOutput").ap()
    acso = nc.dram_tensor("acs", [64, ntok], F32, kind="ExternalOutput").ap()
    csto = nc.dram_tensor("cstates", [nch, 128, 64, 64], F32, kind="ExternalOutput").ap()
    ssto = nc.dram_tensor("sstate", [128, 64, 64], F32, kind="ExternalOutput").ap()
    ctoto = nc.dram_tensor("ctot", [64, nch], F32, kind="ExternalOutput").ap()
    stoto = nc.dram_tensor("stot", [64, 1], F32, kind="ExternalOutput").ap()

    NPT = CONV_DIM // 128

    with ExitStack() as ctx:
        sb = lambda n, s, d: ctx.enter_context(nc.sbuf_tensor(n, s, d))
        xact = sb("xact", [128, NPT, ntok], BF16)
        cw = sb("cw", [128, NPT, 4], F32)
        cb = sb("cb", [128, NPT, 1], F32)
        tri = sb("tri", [128, 2, 256], F32)
        dts = sb("dts", [128, ntok], F32)
        das = sb("das", [128, ntok], F32)
        ident = sb("identb", [128, 128], BF16)
        identf = sb("identf", [128, 128], F32)
        acs_sb = sb("acs_sb", [128, nch, 256], F32)
        tot_sb = sb("tot_sb", [128, 2 * nch], F32)
        fseg = sb("fseg", [128, nch, 256], F32)
        fchk = sb("fchk", [128, nch, 256], F32)
        fsegt = sb("fsegt", [128, nch, 2, 128], F32)
        fchkt = sb("fchkt", [128, nch, 2, 128], F32)
        datok = sb("datok", [128, 2, 128], F32)
        btok = sb("btok", [128, nch, 2, 128], BF16)
        xtok = sb("xtok", [128, nch, 2, 256], BF16)
        xds = sb("xds", [128, 256], BF16)
        sstage = sb("sstage", [128, 4, 64], F32)

        names = ["dxc0", "dxc1", "dcst", "dcsg", "dse", "cnv", "slu", "datr", "date",
                 "acm", "ace", "ttv", "fex", "fmu", "ftr", "fte", "btr", "bte",
                 "xtr", "xte", "xdv", "smm", "sev", "stS", "stC", "sta",
                 "stx0", "stx1"]
        C = {n: Ch(ctx, nc, "c_" + n) for n in names}

        blk = ctx.enter_context(nc.Block())

        @blk.gpsimd
        def _(g):
            C["dse"].inc(g.memset(ident[:, :], 0.0))
            C["dse"].inc(g.memset(identf[:, :], 0.0))
            C["dse"].inc(g.memset(dts[:, :], 0.0))
            C["dse"].inc(g.memset(das[:, :], 0.0))
            g.wait_ge(C["dse"].sem, C["dse"].n)
            C["dse"].inc(g.affine_select(
                out=ident[:, :], in_=ident[:, :],
                compare_op=ALU.not_equal, fill=1.0, base=0,
                pattern=[[-1, 128]], channel_multiplier=1))
            C["dse"].inc(g.affine_select(
                out=identf[:, :], in_=identf[:, :],
                compare_op=ALU.not_equal, fill=1.0, base=0,
                pattern=[[-1, 128]], channel_multiplier=1))

        @blk.sync
        def _(sy):
            C["dcst"].inc(sy.dma_start(
                out=cw[:, :, :], in_=convw.rearrange("(a p) f -> p a f", p=128)), 16)
            C["dcst"].inc(sy.dma_start(
                out=cb[:, :, :], in_=convb.rearrange("(a p) f -> p a f", p=128),
                allow_slow_non_contiguous=True), 16)
            C["dcst"].inc(sy.dma_start(
                out=tri[:, :, :], in_=triu.rearrange("(a p) l -> p a l", p=128)), 16)

        @blk.gpsimd
        def _(g):
            g.wait_ge(C["dse"].sem, 4)
            C["dcsg"].inc(g.dma_start(out=dts[0:64, :], in_=dtt_in), 16)
            C["dcsg"].inc(g.dma_start(out=das[0:64, :], in_=dat_in), 16)

        pre = dict(dse=C["dse"].n, dcst=C["dcst"].n, dcsg=C["dcsg"].n)

        for nm in ("tensor", "vector", "scalar"):
            def emit(e, pre=pre):
                for k, v in pre.items():
                    e.wait_ge(C[k].sem, v)
            getattr(blk, nm)(emit)

        with ExitStack() as p1:
            sb1 = lambda n, s, d: p1.enter_context(nc.sbuf_tensor(n, s, d))
            xr = sb1("xr", [128, 2, ntok + 3], F32)
            cacc = sb1("cacc", [128, 2, ntok], F32)

            for pt in range(NPT):
                @blk.sync
                def _(sy, pt=pt):
                    if pt >= 2:
                        sy.wait_ge(C["cnv"].sem, 3 * (pt - 1))
                    C["dxc%d" % (pt % 2)].inc(sy.dma_start(
                        out=xr[:, pt % 2, :],
                        in_=xbce[pt * 128:(pt + 1) * 128, :]), 16)

                dx_n = C["dxc%d" % (pt % 2)].n
                slu_n = C["slu"].n

                @blk.vector
                def _(v, pt=pt, dx_n=dx_n, slu_n=slu_n):
                    v.wait_ge(C["dxc%d" % (pt % 2)].sem, dx_n)
                    if pt >= 2:
                        v.wait_ge(C["slu"].sem, slu_n - 1)
                    b = pt % 2
                    v.tensor_scalar(cacc[:, b, :], xr[:, b, 0:ntok],
                                    cw[:, pt, 0:1], cb[:, pt, 0:1],
                                    ALU.mult, ALU.add)
                    C["cnv"].inc(v.drain(fusable=False))
                    v.scalar_tensor_tensor(cacc[:, b, :], xr[:, b, 1:1 + ntok],
                                           cw[:, pt, 1:2], cacc[:, b, :],
                                           ALU.mult, ALU.add)
                    C["cnv"].inc(v.drain(fusable=False))
                    v.scalar_tensor_tensor(cacc[:, b, :], xr[:, b, 2:2 + ntok],
                                           cw[:, pt, 2:3], cacc[:, b, :],
                                           ALU.mult, ALU.add)
                    v.drain(fusable=False)
                    v.scalar_tensor_tensor(cacc[:, b, :], xr[:, b, 3:3 + ntok],
                                           cw[:, pt, 3:4], cacc[:, b, :],
                                           ALU.mult, ALU.add)
                    C["cnv"].inc(v.drain(fusable=False))

                cnv_n = C["cnv"].n

                @blk.scalar
                def _(s, pt=pt, cnv_n=cnv_n):
                    s.wait_ge(C["cnv"].sem, cnv_n)
                    if pt >= 2:
                        s.wait_ge(C["stx%d" % (pt % 2)].sem, 16 * (pt // 2))
                    C["slu"].inc(s.activation(xact[:, pt, :], cacc[:, pt % 2, :],
                                              AF.Silu))

                @blk.sync
                def _(sy, pt=pt):
                    sy.wait_ge(C["slu"].sem, C["slu"].n)
                    C["stx%d" % (pt % 2)].inc(sy.dma_start(
                        out=xbao[pt * 128:(pt + 1) * 128, :],
                        in_=xact[:, pt, :]), 16)

        phase_barrier(blk, C, engines=("tensor", "vector", "scalar", "sync"))

        with ExitStack() as p2:
            ps2 = lambda n, s, d: p2.enter_context(nc.psum_tensor(n, s, d))
            trp = ps2("trpb", [128, 512], F32)
            acp = ps2("acp", [128, 256], F32)
            trb = ps2("trbb", [128, 1024], BF16)

            for c in range(nch):
                datr_n0 = C["date"].n

                @blk.tensor
                def _(t, c=c):
                    if c >= 1:
                        t.wait_ge(C["date"].sem, c)
                    for half in range(2):
                        inst = t.transpose(
                            trp[:, half * 128:(half + 1) * 128],
                            das[:, c * 256 + half * 128:c * 256 + (half + 1) * 128],
                            identf[:, :])
                    C["datr"].inc(inst)

                @blk.vector
                def _(v, c=c):
                    v.wait_ge(C["datr"].sem, C["datr"].n)
                    C["date"].inc(v.tensor_copy(datok[:, :, :].rearrange(
                        "p a f -> p (a f)"), trp[:, 0:256]))

                @blk.tensor
                def _(t, c=c):
                    t.wait_ge(C["date"].sem, C["date"].n)
                    if c >= 1:
                        t.wait_ge(C["ace"].sem, c)
                    for half in range(2):
                        inst = t.matmul(acp[:, :], datok[:, half, :],
                                        tri[:, half, :],
                                        start=(half == 0), stop=(half == 1))
                    C["acm"].inc(inst)

                @blk.vector
                def _(v, c=c):
                    v.wait_ge(C["acm"].sem, C["acm"].n)
                    C["ace"].inc(v.tensor_copy(acs_sb[:, c, :], acp[:, :]))

            @blk.vector
            def _(v):
                v.wait_ge(C["ace"].sem, C["ace"].n)
                for c in range(nch):
                    v.tensor_copy(tot_sb[:, c:c + 1], acs_sb[:, c, 255:256])
                v.drain(fusable=False)
                v.tensor_copy(tot_sb[:, 2 * nch - 1:2 * nch],
                              tot_sb[:, nch - 1:nch])
                v.drain(fusable=False)
                for c in range(nch - 2, -1, -1):
                    v.tensor_add(tot_sb[:, nch + c:nch + c + 1],
                                 tot_sb[:, c:c + 1],
                                 tot_sb[:, nch + c + 1:nch + c + 2])
                    v.drain(fusable=False)
                C["ttv"].inc(v.engine_nop())

            @blk.sync
            def _(sy):
                sy.wait_ge(C["ace"].sem, C["ace"].n)
                C["sta"].inc(sy.dma_start(
                    out=acso, in_=acs_sb[0:64, :, :].rearrange("p c f -> p (c f)")), 16)
                sy.wait_ge(C["ttv"].sem, C["ttv"].n)
                C["sta"].inc(sy.dma_start(out=ctoto, in_=tot_sb[0:64, 0:nch]), 16)
                C["sta"].inc(sy.dma_start(out=stoto,
                                          in_=tot_sb[0:64, nch:nch + 1]), 16)

            for c in range(nch):
                @blk.scalar
                def _(s, c=c):
                    s.wait_ge(C["ttv"].sem, C["ttv"].n)
                    s.activation(fseg[:, c, :], acs_sb[:, c, :], AF.Exp,
                                 scale=-1.0, bias=tot_sb[:, nch + c:nch + c + 1])
                    s.drain(fusable=False)
                    C["fex"].inc(s.activation(fchk[:, c, :], acs_sb[:, c, :],
                                              AF.Exp, scale=-1.0,
                                              bias=tot_sb[:, c:c + 1]))

                fex_n = C["fex"].n

                @blk.vector
                def _(v, c=c, fex_n=fex_n):
                    v.wait_ge(C["fex"].sem, fex_n)
                    v.tensor_mul(fseg[:, c, :], fseg[:, c, :], dts[:, c * 256:(c + 1) * 256])
                    v.drain(fusable=False)
                    C["fmu"].inc(v.tensor_mul(fchk[:, c, :], fchk[:, c, :],
                                              dts[:, c * 256:(c + 1) * 256]))

                fmu_n2 = C["fmu"].n

                @blk.tensor
                def _(t, c=c, fmu_n2=fmu_n2):
                    t.wait_ge(C["fmu"].sem, fmu_n2)
                    if c >= 1:
                        t.wait_ge(C["fte"].sem, 2 * c)
                    for half in range(2):
                        t.transpose(trp[:, half * 128:(half + 1) * 128],
                                    fseg[:, c, half * 128:(half + 1) * 128],
                                    identf[:, :])
                    C["ftr"].inc(t.transpose(trp[:, 256:384],
                                             fchk[:, c, 0:128], identf[:, :]))
                    C["ftr"].inc(t.transpose(trp[:, 384:512],
                                             fchk[:, c, 128:256], identf[:, :]))

                @blk.vector
                def _(v, c=c):
                    v.wait_ge(C["ftr"].sem, C["ftr"].n)
                    v.tensor_copy(fsegt[:, c, :, :].rearrange("p a f -> p (a f)"),
                                  trp[:, 0:256])
                    C["fte"].inc(v.drain(fusable=False))
                    v.tensor_copy(fchkt[:, c, :, :].rearrange("p a f -> p (a f)"),
                                  trp[:, 256:512])
                    C["fte"].inc(v.drain(fusable=False))

                @blk.tensor
                def _(t, c=c):
                    if c >= 1:
                        t.wait_ge(C["bte"].sem, c)
                    for half in range(2):
                        inst = t.transpose(
                            trb[:, half * 128:(half + 1) * 128],
                            xact[:, 32, c * 256 + half * 128:c * 256 + (half + 1) * 128],
                            ident[:, :])
                    C["btr"].inc(inst)

                @blk.vector
                def _(v, c=c):
                    v.wait_ge(C["btr"].sem, C["btr"].n)
                    C["bte"].inc(v.tensor_copy(
                        btok[:, c, :, :].rearrange("p a f -> p (a f)"),
                        trb[:, 0:256]))

        phase_barrier(blk, C, engines=("tensor", "vector", "scalar", "sync"))

        with ExitStack() as p3:
            ps3 = lambda n, s, d: p3.enter_context(nc.psum_tensor(n, s, d))
            spsum = [ps3("sps%d" % i, [128, 64], F32) for i in range(4)]
            trb2 = ps3("trb2", [128, 1024], BF16)

            for r in range(16):
                xte0 = C["xte"].n

                @blk.tensor
                def _(t, r=r, xte0=xte0):
                    for c in range(nch):
                        if r >= 1 or c >= 1:
                            t.wait_ge(C["xte"].sem, xte0 + c)
                        for half in range(2):
                            for q in range(2):
                                inst = t.transpose(
                                    trb2[:, (half * 2 + q) * 128:
                                         (half * 2 + q + 1) * 128],
                                    xact[:, r * 2 + q,
                                         c * 256 + half * 128:c * 256 + (half + 1) * 128],
                                    ident[:, :])
                        C["xtr"].inc(inst)

                base_xtr = C["xtr"].n - nch

                for c in range(nch):
                    @blk.vector
                    def _(v, c=c, base_xtr=base_xtr):
                        v.wait_ge(C["xtr"].sem, base_xtr + c + 1)
                        C["xte"].inc(v.tensor_copy(
                            xtok[:, c, :, :].rearrange("p a f -> p (a f)"),
                            trb2[:, 0:512]))

                for c in range(nch):
                    for half in range(2):
                        xte_n = C["xte"].n - nch + c + 1
                        xdv0 = C["xdv"].n

                        @blk.vector
                        def _(v, r=r, c=c, half=half, xte_n=xte_n, xdv0=xdv0):
                            v.wait_ge(C["xte"].sem, xte_n)
                            if xdv0 >= 1:
                                v.wait_ge(C["smm"].sem, C["smm"].n)
                            for hq in range(4):
                                v.tensor_scalar(
                                    xds[:, hq * 64:(hq + 1) * 64],
                                    xtok[:, c, half, hq * 64:(hq + 1) * 64],
                                    fsegt[:, c, half, r * 4 + hq:r * 4 + hq + 1],
                                    None, ALU.mult)
                            C["xdv"].inc(v.drain(fusable=False))

                        @blk.tensor
                        def _(t, r=r, c=c, half=half):
                            t.wait_ge(C["xdv"].sem, C["xdv"].n)
                            for hq in range(4):
                                inst = t.matmul(
                                    spsum[hq][:, :], btok[:, c, half, :],
                                    xds[:, hq * 64:(hq + 1) * 64],
                                    start=(c == 0 and half == 0),
                                    stop=(c == nch - 1 and half == 1))
                            C["smm"].inc(inst)

                @blk.vector
                def _(v, r=r):
                    v.wait_ge(C["smm"].sem, C["smm"].n)
                    v.wait_ge(C["stS"].sem, C["stS"].n)
                    v.wait_ge(C["stC"].sem, C["stC"].n)
                    for hq in range(4):
                        inst = v.tensor_copy(sstage[:, hq, :], spsum[hq][:, :])
                    C["sev"].inc(v.drain(fusable=False))

                @blk.sync
                def _(sy, r=r):
                    sy.wait_ge(C["sev"].sem, C["sev"].n)
                    C["stS"].inc(sy.dma_start(
                        out=ssto[:, r * 4:(r + 1) * 4, :],
                        in_=sstage[:, :, :]), 16)

                for c in range(nch):
                    for half in range(2):
                        xdv0 = C["xdv"].n

                        @blk.vector
                        def _(v, r=r, c=c, half=half):
                            v.wait_ge(C["smm"].sem, C["smm"].n)
                            if half == 0:
                                v.wait_ge(C["stS"].sem, C["stS"].n)
                                v.wait_ge(C["stC"].sem, C["stC"].n)
                            for hq in range(4):
                                v.tensor_scalar(
                                    xds[:, hq * 64:(hq + 1) * 64],
                                    xtok[:, c, half, hq * 64:(hq + 1) * 64],
                                    fchkt[:, c, half, r * 4 + hq:r * 4 + hq + 1],
                                    None, ALU.mult)
                            C["xdv"].inc(v.drain(fusable=False))

                        @blk.tensor
                        def _(t, r=r, c=c, half=half):
                            t.wait_ge(C["xdv"].sem, C["xdv"].n)
                            for hq in range(4):
                                inst = t.matmul(
                                    spsum[hq][:, :], btok[:, c, half, :],
                                    xds[:, hq * 64:(hq + 1) * 64],
                                    start=(half == 0), stop=(half == 1))
                            C["smm"].inc(inst)

                    @blk.vector
                    def _(v, r=r, c=c):
                        v.wait_ge(C["smm"].sem, C["smm"].n)
                        v.wait_ge(C["stC"].sem, C["stC"].n)
                        v.wait_ge(C["stS"].sem, C["stS"].n)
                        for hq in range(4):
                            inst = v.tensor_copy(sstage[:, hq, :], spsum[hq][:, :])
                        C["sev"].inc(v.drain(fusable=False))

                    @blk.sync
                    def _(sy, r=r, c=c):
                        sy.wait_ge(C["sev"].sem, C["sev"].n)
                        C["stC"].inc(sy.dma_start(
                            out=csto[c, :, r * 4:(r + 1) * 4, :],
                            in_=sstage[:, :, :]), 16)

        @blk.sync
        def _(sy):
            for chn in ("stS", "stC", "sta", "stx0", "stx1"):
                sy.wait_ge(C[chn].sem, C[chn].n)

    return nc


def build_l3(n_tiles=8, n_exp=32, n_prior=7, cap=512):
    nc = bass.Bass("TRN2", target_bir_lowering=False)
    ntok = n_tiles * 128
    nch = ntok // 256
    NPT = CONV_DIM // 128
    NB = 12
    ncap = cap // 128
    nslot = n_exp * cap

    x2_in = nc.dram_tensor("x2", [ntok, H], F32, kind="ExternalInput").ap()
    xbca = nc.dram_tensor("xbca", [CONV_DIM, ntok], BF16, kind="ExternalInput").ap()
    zt_in = nc.dram_tensor("zt", [D_INNER, ntok], BF16, kind="ExternalInput").ap()
    dtt_in = nc.dram_tensor("dtt", [64, ntok], F32, kind="ExternalInput").ap()
    acs_in = nc.dram_tensor("acs", [64, ntok], F32, kind="ExternalInput").ap()
    cst_in = nc.dram_tensor("cstates", [nch, 128, 64, 64], F32, kind="ExternalInput").ap()
    pst_in = nc.dram_tensor("pstates", [n_prior, 128, 64, 64], F32, kind="ExternalInput").ap()
    ptot_in = nc.dram_tensor("ptots", [64, n_prior], F32, kind="ExternalInput").ap()
    dvec_in = nc.dram_tensor("dvec", [64, 1], F32, kind="ExternalInput").ap()
    nsc_in = nc.dram_tensor("nscale", [D_INNER, 1], F32, kind="ExternalInput").ap()
    wout_in = nc.dram_tensor("wout", [D_INNER, H], F32, kind="ExternalInput").ap()
    routerT = nc.dram_tensor("routerT", [H, n_exp], F32, kind="ExternalInput").ap()
    gwT = nc.dram_tensor("gwT", [n_exp, H, 768], F32, kind="ExternalInput").ap()
    uwT = nc.dram_tensor("uwT", [n_exp, H, 768], F32, kind="ExternalInput").ap()
    dwT = nc.dram_tensor("dwT", [n_exp, 768, H], F32, kind="ExternalInput").ap()
    out = nc.dram_tensor("out", [ntok, H], F32, kind="ExternalOutput").ap()
    triu_in = nc.dram_tensor("triu1k", [ntok, ntok], F32,
                             kind="ExternalInput").ap()
    bounce = nc.dram_tensor("bounce", [64, NB], F32)
    x3scr = nc.dram_tensor("x3scr", [ntok, H], F32)
    xscr = nc.dram_tensor("xscr", [ntok + 8, H], BF16)
    idxg = nc.dram_tensor("idxg", [nslot + 1, 1], I32)
    wgd = nc.dram_tensor("wgd", [nslot + 1, 1], F32)
    accd = nc.dram_tensor("accd", [ntok + 8, H], F32)

    gw_ap = [gwT[e] for e in range(n_exp)]
    uw_ap = [uwT[e] for e in range(n_exp)]
    dw_ap = [dwT[e] for e in range(n_exp)]

    with ExitStack() as ctx:
        sb = lambda n, s, d: ctx.enter_context(nc.sbuf_tensor(n, s, d))
        ident = sb("identb", [128, 128], BF16)
        identf = sb("identf", [128, 128], F32)
        cst = sb("cst", [128, 2], F32)
        msktri = sb("msktri", [128, 128], F32)
        onesf = sb("onesf", [1, 128], F32)
        stats = sb("stats", [128, 32], F32)
        acs_sb = sb("acs_sb", [128, ntok], F32)
        dts = sb("dts", [128, ntok], F32)
        Wb = sb("Wb", [128, 64 * NB], F32)
        ecolf = sb("ecolf", [128, n_exp], F32)
        ecapf = sb("ecapf", [128, n_exp], F32)
        toki = sb("toki", [128, n_tiles], I32)
        zf = sb("zf", [128, nslot // 128], F32)
        zi = sb("zi", [128, nslot // 128], I32)
        azf = sb("azf", [128, H], BF16)

        names = ["dga", "dgb", "dgc", "dse", "acT", "acE", "gtm", "gte",
                 "scv", "scm", "sce", "xtr", "xte", "mtv", "mts",
                 "ydm", "yev", "ysv", "zld0", "zld1", "gat", "glu", "rms",
                 "ynt", "yne", "ysc", "wom", "woe", "dwo0", "dwo1", "dx2",
                 "stx0", "stx1", "dxb",
                 "sq", "nv", "rs", "xn", "fat", "fae",
                 "rmm", "rmx", "rex", "rev", "drt",
                 "fgm", "fge", "fum", "fue", "fhh", "fht", "fhe", "fdm", "fde",
                 "fl0", "fl1", "fl2", "sto", "dx30", "dx31",
                 "rsx", "csm", "cse", "ctt", "cte", "sdi", "ssc", "spf",
                 "ssc1", "ssc2", "ssc3",
                 "spm", "sxs", "gix", "ggx", "gxt", "gxe", "sad", "fax",
                 "fad", "dtr"]
        C = {n: Ch(ctx, nc, "c_" + n) for n in names}

        blk = ctx.enter_context(nc.Block())

        @blk.gpsimd
        def _(g):
            for t_, v_ in ((ident, 0.0), (identf, 0.0), (msktri, 0.0)):
                C["dse"].inc(g.memset(t_[:, :], v_))
            C["dse"].inc(g.memset(cst[:, 0:1], EPS))
            C["dse"].inc(g.memset(cst[:, 1:2], 1.0))
            C["dse"].inc(g.memset(onesf[:, :], 1.0))
            C["dse"].inc(g.memset(acs_sb[:, :], 0.0))
            C["dse"].inc(g.memset(dts[:, :], 0.0))
            C["spm"].inc(g.memset(zf[:, :], float(ntok)))
            C["spm"].inc(g.memset(azf[:, :], 0.0))
            C["spm"].inc(g.iota(ecolf[:, :], pattern=[[1, n_exp]],
                                channel_multiplier=0,
                                allow_small_or_imprecise_dtypes=True))
            C["spm"].inc(g.iota(ecapf[:, :], pattern=[[cap, n_exp]],
                                channel_multiplier=0,
                                allow_small_or_imprecise_dtypes=True))
            C["spm"].inc(g.iota(toki[:, :], pattern=[[128, n_tiles]],
                                channel_multiplier=1))
            g.wait_ge(C["dse"].sem, C["dse"].n)
            g.wait_ge(C["spm"].sem, C["spm"].n)
            C["dse"].inc(g.affine_select(
                out=ident[:, :], in_=ident[:, :], compare_op=ALU.not_equal,
                fill=1.0, base=0, pattern=[[-1, 128]], channel_multiplier=1))
            C["dse"].inc(g.affine_select(
                out=identf[:, :], in_=identf[:, :], compare_op=ALU.not_equal,
                fill=1.0, base=0, pattern=[[-1, 128]], channel_multiplier=1))
            C["dse"].inc(g.affine_select(
                out=msktri[:, :], in_=msktri[:, :], compare_op=ALU.is_ge,
                fill=-1e30, base=0, pattern=[[1, 128]], channel_multiplier=-1))
            C["spm"].inc(g.tensor_copy(zi[:, :], zf[:, :]))
            g.wait_ge(C["spm"].sem, C["spm"].n)
            C["spf"].inc(g.dma_start(
                out=idxg.ap()[0:nslot, :].rearrange("(p a) o -> p (a o)",
                                                    p=128),
                in_=zi[:, :]), 16)
            C["spf"].inc(g.dma_start(
                out=wgd.ap()[0:nslot, :].rearrange("(p a) o -> p (a o)",
                                                   p=128),
                in_=azf[:, 0:nslot // 128]), 16)
            for tt in range(n_tiles):
                C["spf"].inc(g.dma_start(
                    out=accd.ap()[tt * 128:(tt + 1) * 128, :],
                    in_=azf[:, :]), 16)
            C["spf"].inc(g.dma_start(
                out=accd.ap()[ntok:ntok + 8, :], in_=azf[0:8, :]), 16)
            C["spf"].inc(g.dma_start(
                out=xscr.ap()[ntok:ntok + 8, :], in_=azf[0:8, :]), 16)

        @blk.gpsimd
        def _(g):
            g.wait_ge(C["dse"].sem, 8)
            C["dga"].inc(g.dma_start(out=acs_sb[0:64, :], in_=acs_in), 16)
            C["dga"].inc(g.dma_start(out=dts[0:64, :], in_=dtt_in), 16)

        pre = dict(dse=C["dse"].n, dga=C["dga"].n, spm=C["spm"].n)
        for nm in ("tensor", "vector", "scalar", "sync"):
            def emit(e, pre=dict(pre)):
                for k, v in pre.items():
                    e.wait_ge(C[k].sem, v)
            getattr(blk, nm)(emit)

        with ExitStack() as pA:
            sbA = lambda n, s, d: pA.enter_context(nc.sbuf_tensor(n, s, d))
            y_sb = sbA("y_sb", [128, 2 * nch, D_INNER], BF16)
            Sc_bf = sbA("Sc_bf", [128, nch, D_INNER], BF16)

            with ExitStack() as p1:
                sb1 = lambda n, s, d: p1.enter_context(nc.sbuf_tensor(n, s, d))
                ps1 = lambda n, s, d: p1.enter_context(nc.psum_tensor(n, s, d))
                bblk = sb1("bblk", [128, NB], F32)
                bflat = sb1("bflat", [1, 64 * NB], F32)
                ps_buf = sb1("ps_buf", [128, D_INNER], F32)
                S_run = sb1("S_run", [128, D_INNER], F32)
                wbp = ps1("wbp", [128, 1024], F32)

                @blk.gpsimd
                def _(g):
                    g.wait_ge(C["dse"].sem, C["dse"].n)
                    C["dse"].inc(g.memset(bblk[:, :], 0.0))
                    C["dse"].inc(g.memset(S_run[:, :], 0.0))
                    g.wait_ge(C["dse"].sem, C["dse"].n)
                    C["dgc"].inc(g.dma_start(out=bblk[0:64, 0:n_prior],
                                             in_=ptot_in), 16)
                    C["dgc"].inc(g.dma_start(out=bblk[0:64, NB - 1:NB],
                                             in_=dvec_in), 16)

                @blk.vector
                def _(v):
                    v.wait_ge(C["dgc"].sem, C["dgc"].n)
                    v.tensor_scalar(stats[:, 4:5], bblk[:, 0:1], 0.0, None,
                                    ALU.mult)
                    C["acE"].inc(v.drain(fusable=False))

                for i in range(n_prior):
                    @blk.vector
                    def _(v, i=i):
                        v.wait_ge(C["acE"].sem, C["acE"].n)
                        v.tensor_copy(stats[:, 5:6], bblk[:, i:i + 1])
                        v.drain(fusable=False)
                        v.tensor_copy(bblk[:, i:i + 1], stats[:, 4:5])
                        v.drain(fusable=False)
                        if i < n_prior - 1:
                            v.tensor_add(stats[:, 4:5], stats[:, 4:5],
                                         stats[:, 5:6])
                        C["acE"].inc(v.drain(fusable=False))

                @blk.vector
                def _(v):
                    v.wait_ge(C["acE"].sem, C["acE"].n)
                    for c in range(nch):
                        v.tensor_copy(bblk[:, n_prior + c:n_prior + c + 1],
                                      acs_sb[:, c * 256 + 255:c * 256 + 256])
                    C["acE"].inc(v.drain(fusable=False))

                @blk.scalar
                def _(s):
                    s.wait_ge(C["acE"].sem, C["acE"].n)
                    C["sce"].inc(s.activation(bblk[:, 0:n_prior + nch],
                                              bblk[:, 0:n_prior + nch], AF.Exp))

                @blk.gpsimd
                def _(g):
                    g.wait_ge(C["sce"].sem, C["sce"].n)
                    C["dgb"].inc(g.dma_start(out=bounce.ap(), in_=bblk[0:64, :]), 16)
                    g.wait_ge(C["dgb"].sem, C["dgb"].n)
                    C["dgb"].inc(g.dma_start(
                        out=bflat[:, :],
                        in_=bounce.ap().rearrange("r c -> (r c)")), 16)

                dgb_n = C["dgb"].n

                @blk.tensor
                def _(t, dgb_n=dgb_n):
                    t.wait_ge(C["dgb"].sem, dgb_n)
                    for fb in range((64 * NB + 511) // 512):
                        f0, f1 = fb * 512, min((fb + 1) * 512, 64 * NB)
                        C["gtm"].inc(t.matmul(wbp[:, f0:f1], onesf[:, :],
                                              bflat[:, f0:f1],
                                              start=True, stop=True))

                @blk.vector
                def _(v):
                    v.wait_ge(C["gtm"].sem, C["gtm"].n)
                    C["gte"].inc(v.tensor_copy(Wb[:, :], wbp[:, 0:64 * NB]))

                for i in range(n_prior):
                    scv0 = C["scv"].n

                    @blk.gpsimd
                    def _(g, i=i, scv0=scv0):
                        if i >= 1:
                            g.wait_ge(C["scv"].sem, scv0)
                        C["dwo0"].inc(g.dma_start(
                            out=ps_buf[:, :],
                            in_=pst_in[i].rearrange("p a b -> p (a b)")), 16)

                    dw_n = C["dwo0"].n
                    gte_n = C["gte"].n

                    @blk.vector
                    def _(v, i=i, dw_n=dw_n, gte_n=gte_n):
                        v.wait_ge(C["dwo0"].sem, dw_n)
                        v.wait_ge(C["gte"].sem, gte_n)
                        if i == 0:
                            v.wait_ge(C["dse"].sem, C["dse"].n)
                        for h in range(64):
                            v.scalar_tensor_tensor(
                                S_run[:, h * 64:(h + 1) * 64],
                                ps_buf[:, h * 64:(h + 1) * 64],
                                Wb[:, h * NB + i:h * NB + i + 1],
                                S_run[:, h * 64:(h + 1) * 64],
                                ALU.mult, ALU.add)
                        C["scv"].inc(v.drain(fusable=False))

                for c in range(nch):
                    scv0 = C["scv"].n

                    @blk.gpsimd
                    def _(g, c=c, scv0=scv0):
                        g.wait_ge(C["scv"].sem, scv0)
                        if c < nch - 1:
                            C["dwo1"].inc(g.dma_start(
                                out=ps_buf[:, :],
                                in_=cst_in[c].rearrange("p a b -> p (a b)")), 16)

                    dw_n = C["dwo1"].n
                    scv_n = C["scv"].n

                    @blk.vector
                    def _(v, c=c, dw_n=dw_n, scv_n=scv_n):
                        v.wait_ge(C["scv"].sem, scv_n)
                        v.tensor_copy(Sc_bf[:, c, :], S_run[:, :])
                        C["scm"].inc(v.drain(fusable=False))
                        if c < nch - 1:
                            v.wait_ge(C["dwo1"].sem, dw_n)
                            for h in range(64):
                                v.scalar_tensor_tensor(
                                    S_run[:, h * 64:(h + 1) * 64],
                                    S_run[:, h * 64:(h + 1) * 64],
                                    Wb[:, h * NB + n_prior + c:
                                       h * NB + n_prior + c + 1],
                                    ps_buf[:, h * 64:(h + 1) * 64],
                                    ALU.mult, ALU.add)
                            C["scv"].inc(v.drain(fusable=False))

            phase_barrier(blk, C)

            with ExitStack() as p2:
                sb2 = lambda n, s, d: p2.enter_context(nc.sbuf_tensor(n, s, d))
                ps2 = lambda n, s, d: p2.enter_context(nc.psum_tensor(n, s, d))
                xb_sb = sb2("xb_sb", [128, NPT, ntok], BF16)
                acstok = sb2("acstok", [128, nch, 2, 128], F32)
                negtok = sb2("negtok", [128, nch, 2, 128], F32)
                eactok = sb2("eactok", [128, nch, 2, 128], F32)
                dtttok = sb2("dtttok", [128, nch, 2, 128], F32)
                gt_sb = sb2("gt_sb", [128, nch, 384], BF16)
                zerof = sb2("zerof", [128, 128], F32)
                xtok = sb2("xtok", [128, 2, 256], BF16)
                xd2 = sb2("xd2", [128, 2, 256], BF16)
                mt = sb2("mt", [128, 3, 128], BF16)
                wm = sb2("wm", [128, 2, 128], F32)
                y1 = sb2("y1", [128, 128], F32)
                y2 = sb2("y2", [128, 128], F32)
                trf = ps2("trf", [128, 512], F32)
                trb = ps2("trbL3", [128, 1024], BF16)
                gtp = ps2("gtp", [128, 512], F32)
                wtp = ps2("wtp", [128, 256], F32)
                ydp = ps2("ydp", [128, 128], F32)
                yop = ps2("yop", [128, 128], F32)

                @blk.gpsimd
                def _(g):
                    C["dse"].inc(g.memset(zerof[:, :], 0.0))
                dse_z = C["dse"].n

                @blk.vector
                def _(v):
                    v.wait_ge(C["dse"].sem, dse_z)

                @blk.sync
                def _(sy):
                    C["dxb"].inc(sy.dma_start(
                        out=xb_sb[:, :, :],
                        in_=xbca.rearrange("(a p) t -> p a t", p=128)), 16)

                dxb_n = C["dxb"].n
                gte_base = C["gte"].n
                acE_base = C["acE"].n

                for c in range(nch):
                    acT0 = C["acT"].n

                    @blk.tensor
                    def _(t, c=c, acT0=acT0):
                        if c >= 1:
                            t.wait_ge(C["acE"].sem, acE_base + 2 * c)
                        for half in range(2):
                            t.transpose(trf[:, half * 128:(half + 1) * 128],
                                        acs_sb[:, c * 256 + half * 128:
                                               c * 256 + (half + 1) * 128],
                                        identf[:, :])
                        for half in range(2):
                            inst = t.transpose(
                                trf[:, 256 + half * 128:256 + (half + 1) * 128],
                                dts[:, c * 256 + half * 128:
                                    c * 256 + (half + 1) * 128],
                                identf[:, :])
                        C["acT"].inc(inst)

                    acT_n = C["acT"].n

                    @blk.vector
                    def _(v, c=c, acT_n=acT_n):
                        v.wait_ge(C["acT"].sem, acT_n)
                        v.tensor_copy(
                            acstok[:, c, :, :].rearrange("p a f -> p (a f)"),
                            trf[:, 0:256])
                        v.drain(fusable=False)
                        v.tensor_copy(
                            dtttok[:, c, :, :].rearrange("p a f -> p (a f)"),
                            trf[:, 256:512])
                        C["acE"].inc(v.drain(fusable=False))
                        v.tensor_scalar(
                            negtok[:, c, :, :].rearrange("p a f -> p (a f)"),
                            acstok[:, c, :, :].rearrange("p a f -> p (a f)"),
                            -1.0, None, ALU.mult)
                        C["acE"].inc(v.drain(fusable=False))

                    acE_n = C["acE"].n

                    @blk.scalar
                    def _(s, c=c, acE_n=acE_n):
                        s.wait_ge(C["acE"].sem, acE_n)
                        C["sce"].inc(s.activation(
                            eactok[:, c, :, :].rearrange("p a f -> p (a f)"),
                            acstok[:, c, :, :].rearrange("p a f -> p (a f)"),
                            AF.Exp))

                    gte_n = C["gte"].n

                    @blk.tensor
                    def _(t, c=c, gte_n=gte_n, dxb_n=dxb_n):
                        t.wait_ge(C["dxb"].sem, dxb_n)
                        if c >= 1:
                            t.wait_ge(C["gte"].sem, gte_n)
                        t.matmul(gtp[:, 0:256],
                                 xb_sb[:, 32, c * 256:c * 256 + 128],
                                 xb_sb[:, 33, c * 256:(c + 1) * 256],
                                 start=True, stop=True)
                        C["gtm"].inc(t.matmul(
                            gtp[:, 256:384],
                            xb_sb[:, 32, c * 256 + 128:(c + 1) * 256],
                            xb_sb[:, 33, c * 256 + 128:(c + 1) * 256],
                            start=True, stop=True))

                    @blk.vector
                    def _(v, c=c):
                        v.wait_ge(C["gtm"].sem, C["gtm"].n)
                        C["gte"].inc(v.tensor_copy(gt_sb[:, c, :], gtp[:, 0:384]))

                for r in range(16):
                    for c in range(nch):
                        xte0 = C["xte"].n
                        ydm0 = C["ydm"].n
                        ysv0 = C["ysv"].n

                        @blk.tensor
                        def _(t, r=r, c=c, xte0=xte0):
                            if r + c >= 1:
                                t.wait_ge(C["xte"].sem, xte0)
                            for half in range(2):
                                for q in range(2):
                                    inst = t.transpose(
                                        trb[:, (half * 2 + q) * 128:
                                            (half * 2 + q + 1) * 128],
                                        xb_sb[:, r * 2 + q,
                                              c * 256 + half * 128:
                                              c * 256 + (half + 1) * 128],
                                        ident[:, :])
                            C["xtr"].inc(inst)

                        @blk.vector
                        def _(v, r=r, c=c, ydm0=ydm0, ysv0=ysv0):
                            v.wait_ge(C["xtr"].sem, C["xtr"].n)
                            if ydm0 >= 1:
                                v.wait_ge(C["ydm"].sem, ydm0)
                            v.tensor_copy(
                                xtok[:, :, :].rearrange("p a f -> p (a f)"),
                                trb[:, 0:512])
                            C["xte"].inc(v.drain(fusable=False))
                            for half in range(2):
                                for hq in range(4):
                                    hh = r * 4 + hq
                                    v.tensor_scalar(
                                        xd2[:, half, hq * 64:(hq + 1) * 64],
                                        xtok[:, half, hq * 64:(hq + 1) * 64],
                                        dtttok[:, c, half, hh:hh + 1],
                                        None, ALU.mult)
                            C["xte"].inc(v.drain(fusable=False))

                        for hq in range(4):
                            hh = r * 4 + hq
                            mts_b = C["mts"].n
                            ydm_b = C["ydm"].n

                            @blk.vector
                            def _(v, c=c, hh=hh, mts_b=mts_b):
                                if mts_b >= 1:
                                    v.wait_ge(C["mts"].sem, mts_b)
                                for lh in range(2):
                                    v.tensor_scalar(
                                        wm[:, lh, :], zerof[:, :],
                                        acstok[:, c, lh, hh:hh + 1],
                                        None, ALU.add)
                                C["mtv"].inc(v.drain(fusable=False))

                            mtv_n = C["mtv"].n

                            @blk.tensor
                            def _(t, mtv_n=mtv_n, mts_b=mts_b):
                                t.wait_ge(C["mtv"].sem, mtv_n)
                                if mts_b >= 1:
                                    t.wait_ge(C["mts"].sem, mts_b)
                                for lh in range(2):
                                    inst = t.transpose(
                                        wtp[:, lh * 128:(lh + 1) * 128],
                                        wm[:, lh, :], identf[:, :])
                                C["xtr"].inc(inst)

                            xtr_n = C["xtr"].n

                            @blk.vector
                            def _(v, xtr_n=xtr_n):
                                v.wait_ge(C["xtr"].sem, xtr_n)
                                v.tensor_add(wm[:, 0, :], wtp[:, 0:128],
                                             msktri[:, :])
                                v.drain(fusable=False)
                                v.tensor_add(wm[:, 1, :], wtp[:, 128:256],
                                             msktri[:, :])
                                C["mtv"].inc(v.drain(fusable=False))

                            mtv_n2 = C["mtv"].n

                            @blk.scalar
                            def _(s, c=c, hh=hh, mtv_n2=mtv_n2, ydm_b=ydm_b):
                                s.wait_ge(C["mtv"].sem, mtv_n2)
                                if ydm_b >= 1:
                                    s.wait_ge(C["ydm"].sem, ydm_b)
                                s.activation(mt[:, 0, :], wm[:, 0, :], AF.Exp,
                                             bias=negtok[:, c, 0, hh:hh + 1])
                                s.drain(fusable=False)
                                s.activation(mt[:, 2, :], wm[:, 1, :], AF.Exp,
                                             bias=negtok[:, c, 1, hh:hh + 1])
                                s.drain(fusable=False)
                                s.activation(mt[:, 1, :], wtp[:, 128:256], AF.Exp,
                                             bias=negtok[:, c, 0, hh:hh + 1])
                                C["mts"].inc(s.drain(fusable=False))

                            mts_n = C["mts"].n

                            @blk.vector
                            def _(v, c=c, mts_n=mts_n):
                                v.wait_ge(C["mts"].sem, mts_n)
                                v.tensor_mul(mt[:, 0, :], mt[:, 0, :],
                                             gt_sb[:, c, 0:128])
                                v.drain(fusable=False)
                                v.tensor_mul(mt[:, 1, :], mt[:, 1, :],
                                             gt_sb[:, c, 128:256])
                                v.drain(fusable=False)
                                v.tensor_mul(mt[:, 2, :], mt[:, 2, :],
                                             gt_sb[:, c, 256:384])
                                C["mtv"].inc(v.drain(fusable=False))

                            mtv_n3 = C["mtv"].n
                            yev_b = C["yev"].n
                            xte_n = C["xte"].n

                            @blk.tensor
                            def _(t, c=c, hh=hh, hq=hq, mtv_n3=mtv_n3,
                                  yev_b=yev_b, xte_n=xte_n):
                                t.wait_ge(C["mtv"].sem, mtv_n3)
                                t.wait_ge(C["xte"].sem, xte_n)
                                if yev_b >= 1:
                                    t.wait_ge(C["yev"].sem, yev_b)
                                t.matmul(ydp[:, 0:64], mt[:, 0, :],
                                         xd2[:, 0, hq * 64:(hq + 1) * 64],
                                         start=True, stop=True)
                                t.matmul(ydp[:, 64:128], mt[:, 1, :],
                                         xd2[:, 0, hq * 64:(hq + 1) * 64],
                                         start=True, stop=False)
                                t.matmul(ydp[:, 64:128], mt[:, 2, :],
                                         xd2[:, 1, hq * 64:(hq + 1) * 64],
                                         start=False, stop=True)
                                t.matmul(yop[:, 0:64],
                                         xb_sb[:, 33, c * 256:c * 256 + 128],
                                         Sc_bf[:, c, hh * 64:(hh + 1) * 64],
                                         start=True, stop=True)
                                inst = t.matmul(
                                    yop[:, 64:128],
                                    xb_sb[:, 33, c * 256 + 128:(c + 1) * 256],
                                    Sc_bf[:, c, hh * 64:(hh + 1) * 64],
                                    start=True, stop=True)
                                C["ydm"].inc(inst)

                            ydm_n = C["ydm"].n
                            yev_b2 = C["yev"].n

                            @blk.scalar
                            def _(s, ydm_n=ydm_n, yev_b2=yev_b2):
                                s.wait_ge(C["ydm"].sem, ydm_n)
                                if yev_b2 >= 1:
                                    s.wait_ge(C["yev"].sem, yev_b2)
                                C["sce"].inc(s.copy(y1[:, :], ydp[:, 0:128]))

                            sce_n2 = C["sce"].n

                            @blk.vector
                            def _(v, r=r, c=c, hh=hh, hq=hq, sce_n2=sce_n2):
                                v.wait_ge(C["sce"].sem, sce_n2)
                                for lh in range(2):
                                    v.scalar_tensor_tensor(
                                        y2[:, lh * 64:(lh + 1) * 64],
                                        yop[:, lh * 64:(lh + 1) * 64],
                                        eactok[:, c, lh, hh:hh + 1],
                                        y1[:, lh * 64:(lh + 1) * 64],
                                        ALU.mult, ALU.add)
                                C["yev"].inc(v.drain(fusable=False))
                                for lh in range(2):
                                    v.scalar_tensor_tensor(
                                        y_sb[:, c * 2 + lh, hh * 64:(hh + 1) * 64],
                                        xtok[:, lh, hq * 64:(hq + 1) * 64],
                                        Wb[:, hh * NB + NB - 1:hh * NB + NB],
                                        y2[:, lh * 64:(lh + 1) * 64],
                                        ALU.mult, ALU.add)
                                C["ysv"].inc(v.drain(fusable=False))

            phase_barrier(blk, C)

            with ExitStack() as p3:
                sb3 = lambda n, s, d: p3.enter_context(nc.sbuf_tensor(n, s, d))
                ps3 = lambda n, s, d: p3.enter_context(nc.psum_tensor(n, s, d))
                sqscr = sb3("sqscr", [128, D_INNER], BF16)
                wo2 = sb3("wo2", [128, 32, 512], BF16)
                ztok = sb3("ztok", [128, 2, D_INNER], BF16)
                nsc = sb3("nsc", [128, 32], F32)
                ynT = sb3("ynT", [128, 32, 128], BF16)
                xstg = sb3("xstg", [128, 2, 512], F32)
                trb3 = ps3("trb3", [128, 2048], BF16)
                wop2 = ps3("wop2", [128, 512], F32)

                @blk.gpsimd
                def _(g):
                    C["dgb"].inc(g.dma_start(
                        out=nsc[:, :],
                        in_=nsc_in.rearrange("(a p) o -> p (a o)", p=128),
                        allow_slow_non_contiguous=True), 16)

                for st in range(n_tiles):
                    zld_ch = "zld%d" % (st % 2)
                    gat_b = C["glu"].n

                    @blk.sync
                    def _(sy, st=st, gat_b=gat_b):
                        if st >= 2:
                            sy.wait_ge(C["gat"].sem, st - 1)
                        C[zld_ch].inc(sy.dma_start(
                            out=ztok[:, st % 2, :],
                            in_=zt_in[:, st * 128:(st + 1) * 128],
                            transpose=True), 16)

                    zld_n = C[zld_ch].n

                    @blk.scalar
                    def _(s, st=st, zld_n=zld_n):
                        s.wait_ge(C[zld_ch].sem, zld_n)
                        C["glu"].inc(s.activation(ztok[:, st % 2, :],
                                                  ztok[:, st % 2, :], AF.Silu))

                    glu_n = C["glu"].n
                    ysv_all = C["ysv"].n

                    @blk.vector
                    def _(v, st=st, glu_n=glu_n, ysv_all=ysv_all):
                        v.wait_ge(C["glu"].sem, glu_n)
                        v.wait_ge(C["ysv"].sem, ysv_all)
                        C["gat"].inc(v.tensor_mul(y_sb[:, st, :], y_sb[:, st, :],
                                                  ztok[:, st % 2, :]))

                    gat_n2 = C["gat"].n

                    @blk.scalar
                    def _(s, st=st, gat_n2=gat_n2):
                        s.wait_ge(C["gat"].sem, gat_n2)
                        C["rms"].inc(s.activation(sqscr[:, :], y_sb[:, st, :],
                                                  AF.Square,
                                                  accum_out=stats[:, 1:2]))

                    rms_n = C["rms"].n

                    @blk.vector
                    def _(v, rms_n=rms_n):
                        v.wait_ge(C["rms"].sem, rms_n)
                        C["nv"].inc(v.tensor_scalar(stats[:, 5:6], stats[:, 1:2],
                                                    -1.0 / D_INNER, None,
                                                    ALU.mult))

                    nv_n = C["nv"].n

                    @blk.scalar
                    def _(s, nv_n=nv_n):
                        s.wait_ge(C["nv"].sem, nv_n)
                        s.activation(stats[:, 6:7], stats[:, 5:6], AF.Ln,
                                     bias=cst[:, 0:1], scale=-1.0)
                        s.drain(fusable=False)
                        C["rs"].inc(s.activation(stats[:, 7:8], stats[:, 6:7],
                                                 AF.Exp, scale=-0.5))

                    rs_n = C["rs"].n

                    @blk.vector
                    def _(v, st=st, rs_n=rs_n):
                        v.wait_ge(C["rs"].sem, rs_n)
                        C["ysc"].inc(v.tensor_scalar(y_sb[:, st, :],
                                                     y_sb[:, st, :],
                                                     stats[:, 7:8], None,
                                                     ALU.mult))

                for qt in range(4):
                    wom_b = C["wom"].n

                    @blk.gpsimd
                    def _(g, qt=qt, wom_b=wom_b):
                        if qt >= 1:
                            g.wait_ge(C["wom"].sem, wom_b)
                        C["dwo%d" % (qt % 2)].inc(g.dma_start(
                            out=wo2[:, :, :],
                            in_=wout_in[:, qt * 512:(qt + 1) * 512].rearrange(
                                "(c p) f -> p c f", p=128)), 16)

                    dwo_n = C["dwo%d" % (qt % 2)].n

                    for st in range(n_tiles):
                        ysc_all = C["ysc"].n
                        yne_b = C["yne"].n

                        yne_pre = C["yne"].n

                        @blk.tensor
                        def _(t, st=st, ysc_all=ysc_all, yne_pre=yne_pre):
                            t.wait_ge(C["ysc"].sem, ysc_all)
                            for grp in range(4):
                                gslot = (grp % 2) * 1024
                                need = yne_pre + grp - 1
                                if need >= 1:
                                    t.wait_ge(C["yne"].sem, need)
                                for j in range(8):
                                    chk = grp * 8 + j
                                    inst = t.transpose(
                                        trb3[:, gslot + j * 128:gslot + (j + 1) * 128],
                                        y_sb[:, st, chk * 128:(chk + 1) * 128],
                                        ident[:, :])
                                C["ynt"].inc(inst)

                        base_ynt = C["ynt"].n - 4

                        @blk.vector
                        def _(v, st=st, base_ynt=base_ynt):
                            v.wait_ge(C["dgb"].sem, C["dgb"].n)
                            for grp in range(4):
                                gslot = (grp % 2) * 1024
                                v.wait_ge(C["ynt"].sem, base_ynt + grp + 1)
                                for j in range(8):
                                    chk = grp * 8 + j
                                    v.tensor_scalar(
                                        ynT[:, chk, :],
                                        trb3[:, gslot + j * 128:gslot + (j + 1) * 128],
                                        nsc[:, chk:chk + 1], None, ALU.mult)
                                C["yne"].inc(v.drain(fusable=False))

                        yne_n = C["yne"].n
                        woe_b = C["woe"].n

                        @blk.tensor
                        def _(t, st=st, qt=qt, yne_n=yne_n, dwo_n=dwo_n,
                              woe_b=woe_b):
                            t.wait_ge(C["yne"].sem, yne_n)
                            t.wait_ge(C["dwo%d" % (qt % 2)].sem, dwo_n)
                            t.wait_ge(C["dgb"].sem, C["dgb"].n)
                            if woe_b >= 1:
                                t.wait_ge(C["woe"].sem, woe_b)
                            for chk in range(32):
                                inst = t.matmul(wop2[:, :], ynT[:, chk, :],
                                                wo2[:, chk, :],
                                                start=(chk == 0), stop=(chk == 31))
                            C["wom"].inc(inst)

                        wom_n = C["wom"].n
                        woe_pre = C["woe"].n
                        stx0_b = C["stx0"].n
                        stx1_b = C["stx1"].n

                        @blk.sync
                        def _(sy, st=st, qt=qt, woe_pre=woe_pre):
                            sy.wait_ge(C["woe"].sem, woe_pre)
                            C["dx2"].inc(sy.dma_start(
                                out=xstg[:, 0, :],
                                in_=x2_in[st * 128:(st + 1) * 128,
                                          qt * 512:(qt + 1) * 512]), 16)

                        dx2_n = C["dx2"].n

                        @blk.vector
                        def _(v, wom_n=wom_n, dx2_n=dx2_n, stx0_b=stx0_b,
                              stx1_b=stx1_b):
                            v.wait_ge(C["wom"].sem, wom_n)
                            v.wait_ge(C["dx2"].sem, dx2_n)
                            v.wait_ge(C["stx0"].sem, stx0_b)
                            v.wait_ge(C["stx1"].sem, stx1_b)
                            C["woe"].inc(v.tensor_add(xstg[:, 1, :],
                                                      xstg[:, 0, :], wop2[:, :]))

                        woe_n = C["woe"].n

                        @blk.sync
                        def _(sy, st=st, qt=qt, woe_n=woe_n):
                            sy.wait_ge(C["woe"].sem, woe_n)
                            C["stx%d" % (st % 2)].inc(sy.dma_start(
                                out=x3scr.ap()[st * 128:(st + 1) * 128,
                                               qt * 512:(qt + 1) * 512],
                                in_=xstg[:, 1, :]), 16)

        phase_barrier(blk, C)

        with ExitStack() as p4:
            sb4 = lambda n, s, d: p4.enter_context(nc.sbuf_tensor(n, s, d))
            ps4 = lambda n, s, d: p4.enter_context(nc.psum_tensor(n, s, d))
            x3_sb = sb4("x3_sb", [128, n_tiles, H], F32)
            xn = sb4("xnL3", [128, H], BF16)
            wn = sb4("wnL3", [128, n_tiles, n_exp], F32)
            rt_sb = sb4("rtL3", [128, 16, n_exp], BF16)
            sqscr2 = sb4("sqscr2", [128, H], BF16)

            @blk.gpsimd
            def _(g):
                C["drt"].inc(g.dma_start(
                    out=rt_sb[:, :, :],
                    in_=routerT.rearrange("(c p) f -> p c f", p=128)), 16)

            pA2 = ExitStack()
            xnT = pA2.enter_context(nc.sbuf_tensor("xnTL3",
                                                   [128, n_tiles, H], BF16))

            with ExitStack() as pln:
                trp4 = pln.enter_context(nc.psum_tensor("trp4", [128, 2048], BF16))
                for tt in range(n_tiles):
                    @blk.sync
                    def _(sy, tt=tt):
                        if tt >= 2:
                            sy.wait_ge(C["xn"].sem, C["xn"].n - 1)
                        C["dx3%d" % (tt % 2)].inc(sy.dma_start(
                            out=x3_sb[:, tt, :],
                            in_=x3scr.ap()[tt * 128:(tt + 1) * 128, :]), 16)

                    if tt >= 1:
                        @blk.vector
                        def _(v, tt=tt):
                            v.wait_ge(C["fat"].sem, tt)
                            v.wait_ge(C["sxs"].sem, 16 * tt)

                    emit_ln_tile(blk, C, x3_sb[:, tt, :], xn[:, :], sqscr2,
                                 stats, cst[:, 0:1], "dx3%d" % (tt % 2))
                    emit_featmajor(blk, C, nc, xn, xnT[:, tt, :], trp4,
                                   ident[:, :], "fa")

                    xn_n = C["xn"].n

                    @blk.sync
                    def _(sy, tt=tt, xn_n=xn_n):
                        sy.wait_ge(C["xn"].sem, xn_n)
                        C["sxs"].inc(sy.dma_start(
                            out=xscr.ap()[tt * 128:(tt + 1) * 128, :],
                            in_=xn[:, :]), 16)

            phase_barrier(blk, C)

            with ExitStack() as p5:
                sb5 = lambda n, s, d: p5.enter_context(nc.sbuf_tensor(n, s, d))
                ps5b = lambda n, s, d: p5.enter_context(nc.psum_tensor(n, s, d))
                rpsum = ps5b("rpsumL3", [128, max(n_exp, 32)], F32)
                probs = sb5("probsL3", [128, n_exp], F32)
                top8 = sb5("top8L3", [128, 8], F32)
                ind = sb5("indL3", [128, n_tiles, n_exp], F32)
                w4 = sb5("w4L3", [128, n_tiles, 4], F32)
                eidx = sb5("eidxL3", [128, n_tiles, 4], F32)
                dsti = sb5("dstiL3", [128, n_tiles, 4], I32)
                cumtok = sb5("cumtokL3", [128, n_tiles, n_exp], F32)
                scr_m = sb5("scr_mL3", [128, n_exp], F32)
                scr_b = sb5("scr_bL3", [128, n_exp], F32)
                scr_c = sb5("scr_cL3", [128, n_exp], F32)
                triu_sb = sb5("triu_sbL3", [128, n_tiles, ntok], F32)
                csT = sb5("csTL3", [128, ntok], F32)
                cspsum = ps5b("cspsumL3", [128, ntok], F32)
                ctp = ps5b("ctpL3", [128, 2, 512], F32)

                @blk.sync
                def _(sy):
                    C["dtr"].inc(sy.dma_start(
                        out=triu_sb[:, :, :],
                        in_=triu_in.rearrange("(a p) f -> p a f", p=128)), 16)

                @blk.gpsimd
                def _(g):
                    C["spm"].inc(g.memset(csT[:, :], 0.0))

                emit_router(blk, C, nc, xnT, wn, rt_sb, rpsum, probs, top8,
                            stats, n_tiles,
                            sparse=dict(ind=ind, w4=w4, eidx=eidx,
                                        scr=scr_c, ecolf=ecolf))
                emit_moe_index(blk, C, nc, ind, eidx, w4, dsti, cumtok,
                               triu_sb, identf[:, :], ecolf, ecapf, scr_m,
                               scr_b, scr_c, stats, cspsum, ctp, csT,
                               idxg.ap(), wgd.ap(), toki, cap, n_tiles,
                               n_exp)

            phase_barrier(blk, C)
            pA2.close()

            with ExitStack() as p6:
                sb6 = lambda n, s, d: p6.enter_context(nc.sbuf_tensor(n, s, d))
                ps6 = lambda n, s, d: p6.enter_context(nc.psum_tensor(n, s, d))
                slots = [sb6("slotL3%d" % i, [128, 12288], BF16) for i in range(2)]
                gact = sb6("gactL3", [128, 2 * ncap, 768], BF16)
                htmp = sb6("htmpL3", [128, 2, 768], BF16)
                utmp = sb6("utmpL3", [128, 768], BF16)
                xg = sb6("xgL3", [128, ncap, H], BF16)
                xgT = sb6("xgTL3", [128, ncap, H], BF16)
                idx_sb = sb6("idx_sbL3", [128, 4, ncap], I32)
                wg_sb = sb6("wg_sbL3", [128, 4, ncap], F32)
                dsb = sb6("dsbL3", [128, 2, H], F32)
                gup = ps6("gupL3", [128, 1024], F32)
                htp = ps6("htpL3", [128, 1024], BF16)
                dpsum = ps6("dpsumL3", [128, 2048], F32)
                gxtp = ps6("gxtpL3", [128, 8, 128], BF16)
                emit_ffn_sparse(blk, C, nc, xscr.ap(), idxg.ap(), wgd.ap(),
                                accd.ap(), slots, xg, xgT, idx_sb, wg_sb,
                                dsb, gact, htmp, utmp, gup, htp, dpsum,
                                gxtp, ident[:, :], gw_ap, uw_ap, dw_ap,
                                n_exp, ncap, ["fl0", "fl1"], "f")

            phase_barrier(blk, C)

            with ExitStack() as pfin:
                axf = pfin.enter_context(nc.sbuf_tensor("axfL3",
                                                        [128, 2, H], F32))
                emit_final_adds(blk, C, nc, x3_sb, accd.ap(), axf, n_tiles)

            base_fad = C["fad"].n - n_tiles
            for tt in range(n_tiles):
                @blk.sync
                def _(sy, tt=tt, base_fad=base_fad):
                    sy.wait_ge(C["fad"].sem, base_fad + tt + 1)
                    C["sto"].inc(sy.dma_start(
                        out=out[tt * 128:(tt + 1) * 128, :],
                        in_=x3_sb[:, tt, :]), 16)

        @blk.sync
        def _(sy):
            sy.wait_ge(C["sto"].sem, C["sto"].n)

    return nc


_NCS = {}


def _get(name, builder):
    if name not in _NCS:
        _NCS[name] = builder()
    return _NCS[name]


def kernel(L0_attn_a_hidden_states, L0_attn_a_w_qkv, L0_attn_a_w_o,
           L0_attn_b_W_in, L0_attn_b_conv_w, L0_attn_b_conv_b,
           L0_attn_b_dt_bias, L0_attn_b_A_log, L0_attn_b_D,
           L0_attn_b_norm_scale, L0_attn_b_W_out, L0_ffn_router_w,
           L0_ffn_gate_w, L0_ffn_up_w, L0_ffn_down_w):
    A = np.ascontiguousarray
    x = A(np.asarray(L0_attn_a_hidden_states, np.float32).reshape(S, H))
    w_qkv = np.asarray(L0_attn_a_w_qkv, np.float32)
    cores = list(range(N_CORES))
    NTOK = S // N_CORES

    nc1 = _get("l1", build_l1)
    wT = A(w_qkv.T)
    maskc = _causal_mask_const()
    in1 = []
    for c in cores:
        cols = [wT[:, p * H + c * 256: p * H + c * 256 + 256] for p in range(3)]
        in1.append({"xin": x, "wqkvT": A(np.concatenate(cols, axis=1)),
                    "maskc": maskc})
    r1 = run_bass_kernel_spmd(nc1, in1, core_ids=cores)
    ao = np.concatenate([r1.results[c]["attn_out"] for c in cores], axis=1)

    nc2a = _get("l2a", build_l2a)
    woT = A(np.asarray(L0_attn_a_w_o, np.float32).T)
    routerT = A(np.asarray(L0_ffn_router_w, np.float32).T)
    gwT = A(np.asarray(L0_ffn_gate_w, np.float32).transpose(0, 2, 1))
    uwT = A(np.asarray(L0_ffn_up_w, np.float32).transpose(0, 2, 1))
    dwT = A(np.asarray(L0_ffn_down_w, np.float32).transpose(0, 2, 1))
    win = np.zeros((H, 8576), np.float32)
    win[:, 0:D_IN_PROJ] = np.asarray(L0_attn_b_W_in, np.float32)
    dtb = A(np.asarray(L0_attn_b_dt_bias, np.float32).reshape(64, 1))
    alog = A(np.asarray(L0_attn_b_A_log, np.float32).reshape(64, 1))
    triu1k = np.triu(np.ones((NTOK, NTOK), np.float32))
    in2 = []
    for c in cores:
        sl = slice(c * NTOK, (c + 1) * NTOK)
        in2.append({"xin": A(x[sl]), "aoT": A(ao[sl].T), "woT": woT,
                    "routerT": routerT, "gwT": gwT, "uwT": uwT, "dwT": dwT,
                    "win": win, "dtb": dtb, "alog": alog, "triu1k": triu1k})
    r2 = run_bass_kernel_spmd(nc2a, in2, core_ids=cores)

    nc2b = _get("l2b", build_l2b)
    convw = A(np.asarray(L0_attn_b_conv_w, np.float32).T)
    convb = A(np.asarray(L0_attn_b_conv_b, np.float32).reshape(CONV_DIM, 1))
    triu = np.triu(np.ones((256, 256), np.float32))
    in3 = []
    for c in cores:
        xb = r2.results[c]["xbct"]
        if c == 0:
            halo = np.zeros((CONV_DIM, 3), np.float32)
        else:
            halo = r2.results[c - 1]["xbct"][:, NTOK - 3:NTOK]
        in3.append({"xbce": A(np.concatenate([halo, xb], axis=1)),
                    "dtt": r2.results[c]["dtt"], "dat": r2.results[c]["dat"],
                    "convw": convw, "convb": convb, "triu": triu})
    r3 = run_bass_kernel_spmd(nc2b, in3, core_ids=cores)

    nc3 = _get("l3", build_l3)
    dvec = A(np.asarray(L0_attn_b_D, np.float32).reshape(64, 1))
    nscale = A(np.asarray(L0_attn_b_norm_scale, np.float32).reshape(D_INNER, 1))
    wout = A(np.asarray(L0_attn_b_W_out, np.float32))
    in4 = []
    for c in cores:
        pstates = np.zeros((7, 128, 64, 64), np.float32)
        ptots = np.zeros((64, 7), np.float32)
        for i in range(c):
            pstates[i] = r3.results[c - 1 - i]["sstate"]
            ptots[:, i] = r3.results[c - 1 - i]["stot"][:, 0]
        in4.append({"x2": r2.results[c]["x2"], "xbca": r3.results[c]["xbca"],
                    "zt": r2.results[c]["zt"], "dtt": r2.results[c]["dtt"],
                    "acs": r3.results[c]["acs"],
                    "cstates": r3.results[c]["cstates"],
                    "pstates": pstates, "ptots": ptots, "dvec": dvec,
                    "nscale": nscale, "wout": wout, "routerT": routerT,
                    "gwT": gwT, "uwT": uwT, "dwT": dwT, "triu1k": triu1k})
    r4 = run_bass_kernel_spmd(nc3, in4, core_ids=cores)

    outp = np.concatenate([r4.results[c]["out"] for c in cores], axis=0)
    return outp.reshape(1, S, H).astype(np.float32)
